# version 05
# baseline (speedup 1.0000x reference)
import math
import numpy as np
import ml_dtypes
import dataclasses
import concourse.bass as bass
from concourse.expressions import RuntimeValue
import concourse.mybir as mybir
from concourse.bass_utils import run_bass_kernel_spmd

F32 = mybir.dt.float32
BF16 = mybir.dt.bfloat16
AF = mybir.ActivationFunctionType
ALU = mybir.AluOpType
AX = mybir.AxisListType
bfl = ml_dtypes.bfloat16

B, S, D = 4, 2048, 2048
H, HKV, DH = 16, 4, 128
TLOC = 1024
NDT = 16
NCH = 8
CH = 128
HD = 8192
HDS = HD // 8
EPS1 = 1e-5
EPS2 = 1e-6

XA_K = 0
XA_KT = 2048
XA_V = 4096
XA_G0 = 6144
XA_G1 = 7168
XA_ST = 8192
XA_E = 10240
XA_F = 12288
XA_SP = 14336

lam_np = np.exp2(-8.0 * np.arange(1, H + 1, dtype=np.float64) / H)
blk_f = [float(np.exp(-lam_np[h] * CH)) for h in range(H)]


def build_nc():
    nc = bass.Bass("TRN2", target_bir_lowering=False, debug=False,
                   detect_race_conditions=False, enable_asserts=False)
    xsT = nc.dram_tensor("xsT", [D, TLOC], F32, kind="ExternalInput").ap()
    xlast = nc.dram_tensor("xlast", [4, D], F32, kind="ExternalInput").ap()
    wqT = nc.dram_tensor("wqT", [D, D], F32, kind="ExternalInput").ap()
    wkT = nc.dram_tensor("wkT", [D, 512], F32, kind="ExternalInput").ap()
    wvT = nc.dram_tensor("wvT", [D, 512], F32, kind="ExternalInput").ap()
    woT = nc.dram_tensor("woT", [D, D], F32, kind="ExternalInput").ap()
    gw = nc.dram_tensor("gw", [D, D], F32, kind="ExternalInput").ap()
    aggw = nc.dram_tensor("aggw", [D, 1024], F32, kind="ExternalInput").ap()
    bcw = nc.dram_tensor("bcw", [D, 1024], F32, kind="ExternalInput").ap()
    d1h = nc.dram_tensor("d1h", [D, 4096], F32, kind="ExternalInput").ap()
    d2h = nc.dram_tensor("d2h", [4096, D], F32, kind="ExternalInput").ap()
    rvec = nc.dram_tensor("rvec", [4, 7 * D + 4096], F32, kind="ExternalInput").ap()
    xinfo = nc.dram_tensor("xinfo", [1, 24], mybir.dt.int32, kind="ExternalInput").ap()
    nonces = nc.dram_tensor("nonces", [1, 8], mybir.dt.int32, kind="ExternalInput").ap()
    pvecs = nc.dram_tensor("pvecs", [128, 48], F32, kind="ExternalInput").ap()
    cosF = nc.dram_tensor("cosF", [128, 1024], F32, kind="ExternalInput").ap()
    sinS = nc.dram_tensor("sinS", [128, 1024], F32, kind="ExternalInput").ap()
    diagT = nc.dram_tensor("diagT", [128, 2048], F32, kind="ExternalInput").ap()
    qdecT = nc.dram_tensor("qdecT", [128, 2048], F32, kind="ExternalInput").ap()
    kdecC = nc.dram_tensor("kdecC", [128, 16], F32, kind="ExternalInput").ap()
    identb = nc.dram_tensor("identb", [128, 128], BF16, kind="ExternalInput").ap()
    identf = nc.dram_tensor("identf", [128, 4], F32, kind="ExternalInput").ap()
    onesb = nc.dram_tensor("onesb", [128, 1], BF16, kind="ExternalInput").ap()
    onesf = nc.dram_tensor("onesf", [128, 128], F32, kind="ExternalInput").ap()
    bselP = nc.dram_tensor("bselP", [128, 4], F32, kind="ExternalInput").ap()
    selS = nc.dram_tensor("selS", [128, 1], F32, kind="ExternalInput").ap()
    consts = nc.dram_tensor("consts", [128, 2], F32, kind="ExternalInput").ap()
    outT = nc.dram_tensor("outT", [D, TLOC], F32, kind="ExternalOutput").ap()
    SHB = nc.dram_tensor("SHB", [272, 2048], F32, addr_space="Shared")
    SHF = nc.dram_tensor("SHF", [2, 2], mybir.dt.int32, addr_space="Shared")
    dc1 = nc.dram_tensor("dc1", [D, 4096], BF16)
    dc2 = nc.dram_tensor("dc2", [4096, D], BF16)

    from contextlib import ExitStack
    with ExitStack() as _es:
        XA = _es.enter_context(nc.sbuf_tensor("XA", [128, 16384], F32))
        B1 = _es.enter_context(nc.sbuf_tensor("B1", [128, 16384], BF16))
        SO = _es.enter_context(nc.sbuf_tensor("SO", [128, 16384], BF16))
        XN = _es.enter_context(nc.sbuf_tensor("XN", [128, 16384], BF16))
        WS = _es.enter_context(nc.sbuf_tensor("WS", [128, 4096], BF16))
        ROWS = _es.enter_context(nc.sbuf_tensor("ROWS", [128, 4608], F32))
        TH = _es.enter_context(nc.sbuf_tensor("TH", [128, 4104], F32))
        identb_s = _es.enter_context(nc.sbuf_tensor("identb_s", [128, 128], BF16))
        identf_s = _es.enter_context(nc.sbuf_tensor("identf_s", [128, 4], F32))
        onesb_s = _es.enter_context(nc.sbuf_tensor("onesb_s", [128, 1], BF16))
        onesf_s = _es.enter_context(nc.sbuf_tensor("onesf_s", [128, 128], F32))
        pvecs_s = _es.enter_context(nc.sbuf_tensor("pvecs_s", [128, 48], F32))
        bselP_s = _es.enter_context(nc.sbuf_tensor("bselP_s", [128, 4], F32))
        selS_s = _es.enter_context(nc.sbuf_tensor("selS_s", [128, 1], F32))
        kdecC_s = _es.enter_context(nc.sbuf_tensor("kdecC_s", [128, 16], F32))
        xnlT_s = _es.enter_context(nc.sbuf_tensor("xnlT_s", [128, 64], BF16))
        tT_s = _es.enter_context(nc.sbuf_tensor("tT_s", [128, 64], BF16))
        uT_s = _es.enter_context(nc.sbuf_tensor("uT_s", [128, 128], BF16))
        bcT_s = _es.enter_context(nc.sbuf_tensor("bcT_s", [128, 64], F32))
        bcP_s = _es.enter_context(nc.sbuf_tensor("bcP_s", [128, 16], F32))
        consts_s = _es.enter_context(nc.sbuf_tensor("consts_s", [128, 2], F32))
        xinfo_s = _es.enter_context(nc.sbuf_tensor("xinfo_s", [1, 24], mybir.dt.int32))
        nonce_s = _es.enter_context(nc.sbuf_tensor("nonce_s", [1, 8], mybir.dt.int32))
        flag_s = _es.enter_context(nc.sbuf_tensor("flag_s", [1, 1], mybir.dt.int32))
        PA = _es.enter_context(nc.psum_tensor("PA", [128, 1536], F32))
        PB = _es.enter_context(nc.psum_tensor("PB", [128, 2048], F32))
        PT = _es.enter_context(nc.psum_tensor("PT", [128, 1024], BF16))
        semX = _es.enter_context(nc.semaphore("sX"))
        semW = _es.enter_context(nc.semaphore("sW"))
        semR = _es.enter_context(nc.semaphore("sR"))
        semT = _es.enter_context(nc.semaphore("sT"))
        semV = _es.enter_context(nc.semaphore("sV"))
        semS = _es.enter_context(nc.semaphore("sS"))
        semG = _es.enter_context(nc.semaphore("sG"))
        semC = _es.enter_context(nc.semaphore("sC"))
        semBe = _es.enter_context(nc.semaphore("sBe"))
        semBo = _es.enter_context(nc.semaphore("sBo"))
        semWe = _es.enter_context(nc.semaphore("sWe"))
        semWo = _es.enter_context(nc.semaphore("sWo"))
        semROW = _es.enter_context(nc.semaphore("sROW"))
        semCD = _es.enter_context(nc.semaphore("sCD"))
        semSTG = _es.enter_context(nc.semaphore("sSTG"))
        semG1 = _es.enter_context(nc.semaphore("sG1"))
        semG2 = _es.enter_context(nc.semaphore("sG2"))
        semG3 = _es.enter_context(nc.semaphore("sG3"))
        semG4 = _es.enter_context(nc.semaphore("sG4"))
        semG5 = _es.enter_context(nc.semaphore("sG5"))
        semOe = _es.enter_context(nc.semaphore("sOe"))
        semOo = _es.enter_context(nc.semaphore("sOo"))
        semCS = _es.enter_context(nc.semaphore("sCS"))
        semPe = _es.enter_context(nc.semaphore("sPe"))
        semPo = _es.enter_context(nc.semaphore("sPo"))
        semRe = _es.enter_context(nc.semaphore("sRe"))
        semRo = _es.enter_context(nc.semaphore("sRo"))
        semWX = _es.enter_context(nc.semaphore("sWX"))
        semPF = _es.enter_context(nc.semaphore("sPF"))
        semSH = _es.enter_context(nc.semaphore("sSH"))
        semTe = _es.enter_context(nc.semaphore("sTe"))
        semTo = _es.enter_context(nc.semaphore("sTo"))
        semCe = _es.enter_context(nc.semaphore("sCe"))
        semCo = _es.enter_context(nc.semaphore("sCo"))
        blk = _es.enter_context(nc.Block())
        cnt = dict(X=0, W=0, R=0, T=0, V=0, S=0, G=0, C=0,
                   Be=0, Bo=0, We=0, Wo=0, ROW=0, CD=0, STG=0,
                   G1=0, G2=0, G3=0, G4=0, G5=0, Oe=0, Oo=0, CS=0,
                   Pe=0, Po=0, Re=0, Ro=0,
                   WX=0, PF=0, SH=0, Te=0, To=0, Ce=0, Co=0)
        sems = dict(X=semX, W=semW, R=semR, T=semT, V=semV, S=semS, G=semG, C=semC,
                    Be=semBe, Bo=semBo, We=semWe, Wo=semWo, ROW=semROW, CD=semCD,
                    STG=semSTG, G1=semG1, G2=semG2, G3=semG3, G4=semG4, G5=semG5,
                    Oe=semOe, Oo=semOo, CS=semCS,
                    Pe=semPe, Po=semPo, Re=semRe, Ro=semRo,
                    WX=semWX, PF=semPF, SH=semSH, Te=semTe, To=semTo,
                    Ce=semCe, Co=semCo)

        def inc(key, inst, k=1):
            inst.then_inc(sems[key], k)
            cnt[key] += k
            return cnt[key]

        def dinc(key, inst):
            return inc(key, inst, 16)

        def wt(eng, key, val=None):
            eng.wait_ge(sems[key], cnt[key] if val is None else val)

        XAa = XA.ap()
        XAb = XAa.bitcast(BF16)
        B1a = B1.ap()
        B1f = B1a.bitcast(F32)
        SOa = SO.ap()
        XNa = XN.ap()
        WSa = WS.ap()
        ROWSa = ROWS.ap()
        THa = TH.ap()
        PAa = PA.ap()
        PBa = PB.ap()
        PTa = PT.ap()

        def xab(c0f32, nbf):
            return XAb[:, 2 * c0f32:2 * c0f32 + nbf]

        def row(p, j):
            slot = {(0, 0): 0, (32, 0): 1, (0, 1): 2, (64, 0): 3}[(p, j)]
            return ROWSa[0:1, slot * 1024:(slot + 1) * 1024]

        th_main = THa[0:4, 0:2048]
        th_t = THa[0:4, 2048:4096]
        th_u = THa[0:4, 2048:3072]
        th_d = th_t

        def th_scr(j):
            return THa[0:4, 4096 + j:4096 + j + 1]

        psum_last = {}

        def psum_wait(eng, *keys):
            for k in keys:
                if k in psum_last:
                    kk, v = psum_last[k]
                    eng.wait_ge(sems[kk], v)

        def psum_used(key, semname):
            psum_last[key] = (semname, cnt[semname])

        A0 = PAa[:, 0:512]
        A1 = PAa[:, 512:1024]
        A2 = PAa[:, 1024:1536]

        RV_ING, RV_INB, RV_PNG, RV_PNB, RV_D2B, RV_BCB, RV_AGB, RV_D1B = (
            0, D, 2 * D, 3 * D, 4 * D, 5 * D, 6 * D, 7 * D)

        for dst, src in [
            (identb_s[:], identb), (identf_s[:], identf), (onesb_s[:], onesb),
            (onesf_s[:], onesf), (pvecs_s[:], pvecs), (bselP_s[:], bselP),
            (selS_s[:], selS), (kdecC_s[:], kdecC),
            (XAa[:, XA_F:XA_F + 1024], cosF),
            (XAa[:, XA_F + 1024:XA_F + 2048], sinS),
            (THa[0:4, 0:2048], xlast),
            (consts_s[:], consts),
            (xinfo_s[:], xinfo),
            (nonce_s[:], nonces),
        ]:
            dinc("X", nc.sync.dma_start(out=dst, in_=src))
        base_loads = cnt["X"]
        nc.const_aps.register(F32, EPS1, consts_s[:, 0:1])
        nc.const_aps.register(F32, EPS2, consts_s[:, 1:2])
        for eng in (nc.tensor, nc.vector, nc.scalar, nc.gpsimd):
            eng.wait_ge(semX, base_loads)
        cosF_s = XAa[:, XA_F:XA_F + 1024]
        sinS_s = XAa[:, XA_F + 1024:XA_F + 2048]

        stats_mm_done = [4 * (dt + 1) for dt in range(NDT)]
        xbf_done = []
        for dt in range(NDT):
            slot = dt % 2
            bkey = "Be" if slot == 0 else "Bo"
            if dt >= 2:
                nc.gpsimd.wait_ge(semT, stats_mm_done[dt - 2])
            xbf_done.append((bkey, dinc(bkey, nc.gpsimd.dma_start(
                out=xab(XA_E + slot * 512, 1024),
                in_=xsT[dt * 128:(dt + 1) * 128, :]))))
        sq_done = []
        for dt in range(NDT):
            slot = dt % 2
            nc.scalar.wait_ge(sems[xbf_done[dt][0]], xbf_done[dt][1])
            if dt >= 2:
                nc.scalar.wait_ge(semT, stats_mm_done[dt - 2])
            sq_done.append(inc("S", nc.scalar.activation(
                xab(XA_E + 1024 + slot * 512, 1024),
                xab(XA_E + slot * 512, 1024), AF.Square)))
        for dt in range(NDT):
            slot = dt % 2
            nc.tensor.wait_ge(sems[xbf_done[dt][0]], xbf_done[dt][1])
            nc.tensor.wait_ge(semS, sq_done[dt])
            st, sp = (dt == 0), (dt == NDT - 1)
            xb = xab(XA_E + slot * 512, 1024)
            xq = xab(XA_E + 1024 + slot * 512, 1024)
            nc.tensor.matmul(PAa[0:1, 0:512], onesb_s[:], xb[:, 0:512], start=st, stop=sp)
            nc.tensor.matmul(PAa[0:1, 512:1024], onesb_s[:], xb[:, 512:1024], start=st, stop=sp)
            nc.tensor.matmul(PAa[0:1, 1024:1536], onesb_s[:], xq[:, 0:512], start=st, stop=sp)
            inc("T", nc.tensor.matmul(PBa[0:1, 0:512], onesb_s[:], xq[:, 512:1024],
                                      start=st, stop=sp), 4)
        assert cnt["T"] == 4 * NDT
        wt(nc.vector, "T")
        nc.vector.tensor_copy(row(0, 0)[:, 0:512], PAa[0:1, 0:512])
        nc.vector.tensor_copy(row(0, 0)[:, 512:1024], PAa[0:1, 512:1024])
        nc.vector.tensor_copy(row(32, 0)[:, 0:512], PAa[0:1, 1024:1536])
        inc("V", nc.vector.tensor_copy(row(32, 0)[:, 512:1024], PBa[0:1, 0:512]))
        for k in ("A0", "A1", "A2", "B0"):
            psum_used(k, "V")

        def ln_rows(srow, qrow, rrow, mrow, wrow, eps):
            wt(nc.scalar, "V")
            nc.scalar.mul(mrow, srow, 1.0 / D)
            inc("S", nc.scalar.mul(wrow, qrow, 1.0 / D))
            wt(nc.scalar, "S")
            inc("S", nc.scalar.activation(rrow, mrow, AF.Square))
            wt(nc.vector, "S")
            inc("V", nc.vector.tensor_tensor(wrow, wrow, rrow, ALU.subtract))
            wt(nc.scalar, "V")
            inc("S", nc.scalar.activation(rrow, wrow, AF.Ln, bias=eps))
            wt(nc.scalar, "S")
            inc("S", nc.scalar.activation(rrow, rrow, AF.Exp, scale=-0.5))
            wt(nc.vector, "S")
            inc("V", nc.vector.tensor_tensor(mrow, mrow, rrow, ALU.mult))
            wt(nc.scalar, "V")
            inc("S", nc.scalar.mul(mrow, mrow, -1.0))
            wt(nc.vector, "S")

        ln_rows(row(0, 0), row(32, 0), row(0, 1), row(64, 0), row(32, 0), EPS1)

        def bcast_row(r, dst, pk1, pk2, psl1, psl2):
            bp = r.base_partition()
            ones1 = onesf_s.ap()[bp:bp + 1, :]
            wt(nc.tensor, "S")
            wt(nc.tensor, "V")
            psum_wait(nc.tensor, pk1, pk2)
            nc.tensor.matmul(psl1, ones1, r[:, 0:512], start=True, stop=True)
            inc("T", nc.tensor.matmul(psl2, ones1, r[:, 512:1024], start=True, stop=True))
            wt(nc.vector, "T")
            nc.vector.tensor_copy(dst[:, 0:512], psl1)
            inc("V", nc.vector.tensor_copy(dst[:, 512:1024], psl2))
            psum_used(pk1, "V")
            psum_used(pk2, "V")

        bcast_row(row(0, 1), B1f[:, 5120:6144], "A0", "A1", A0, A1)
        bcast_row(row(64, 0), B1f[:, 6144:7168], "A0", "A1", A0, A1)
        rstd1bc = B1f[:, 5120:6144]
        nmr1bc = B1f[:, 6144:7168]
        ln1_bc = cnt["V"]

        xn_done = []
        xld_done = []
        for dt in range(NDT):
            slot = dt % 2
            if dt >= 2:
                nc.scalar.wait_ge(semV, ln1_bc + 2 * (dt - 1))
            else:
                nc.scalar.wait_ge(semT, stats_mm_done[NDT - 1])
                nc.scalar.wait_ge(semS, sq_done[NDT - 1])
            bkey = "Pe" if slot == 0 else "Po"
            xld_done.append((bkey, dinc(bkey, nc.scalar.dma_start(
                out=XAa[:, XA_E + slot * 1024:XA_E + (slot + 1) * 1024],
                in_=xsT[dt * 128:(dt + 1) * 128, :]))))
            nc.vector.wait_ge(sems[xld_done[dt][0]], xld_done[dt][1])
            xt = XAa[:, XA_E + slot * 1024:XA_E + (slot + 1) * 1024]
            inc("V", nc.vector.tensor_tensor(xt, xt, rstd1bc, ALU.mult))
            wt(nc.vector, "V")
            xn_done.append(inc("V", nc.vector.tensor_tensor(
                XNa[:, dt * 1024:(dt + 1) * 1024], xt, nmr1bc, ALU.add)))
        xn_all = cnt["V"]

        SHBa = SHB.ap()
        SHFa = SHF.ap()
        E_AGG, E_STATE, E_S1, E_S2, E_S3, E_S4, E_BC = range(7)

        nc.sync.wait_ge(semX, base_loads)
        mb_r = nc.sync.alloc_register("xr_mb")
        nc.sync.reg_load(mb_r, xinfo_s[0:1, 0:1])
        myp_r = nc.sync.alloc_register("xr_myp")
        nc.sync.reg_load(myp_r, xinfo_s[0:1, 1:2])
        pp_r = nc.sync.alloc_register("xr_pp")
        nc.sync.reg_load(pp_r, xinfo_s[0:1, 2:3])
        mb = RuntimeValue(mb_r, 0, 136)
        myp = RuntimeValue(myp_r, 0, 1)
        pp = RuntimeValue(pp_r, 0, 1)
        poll_nd = nc.sync.alloc_register("poll_nd")
        poll_v = nc.sync.alloc_register("poll_v")
        poll_n = nc.sync.alloc_register("poll_n")
        ap_w = {
            ("s", 1024): SHBa[bass.ds(mb, 4), 0:1024],
            ("s", 2048): SHBa[bass.ds(mb, 4), 0:2048],
            ("s1", 2048): SHBa[bass.ds(mb + 4, 4), 0:2048],
            ("st", 2048): SHBa[bass.ds(mb + 8, 128), 0:2048],
        }
        ap_fw = [SHFa[bass.ds(myp, 1), 0:1], SHFa[bass.ds(myp, 1), 1:2]]
        ap_fr = [SHFa[bass.ds(pp, 1), 0:1], SHFa[bass.ds(pp, 1), 1:2]]

        def sh_write(src_ap, wkey, exi):
            sl = exi % 2
            dinc("WX", nc.sync.dma_start(out=ap_w[wkey], in_=src_ap))
            nc.sync.wait_ge(semWX, cnt["WX"])
            dinc("WX", nc.sync.dma_start(
                out=ap_fw[sl], in_=nonce_s[0:1, exi:exi + 1]))
            nc.sync.wait_ge(semWX, cnt["WX"])

        def sh_poll(exi):
            sl = exi % 2
            nc.sync.reg_mov(poll_nd, 1)
            with nc.sync.While(poll_nd):
                nc.sync.sem_clear(semPF)
                nc.sync.dma_start(out=flag_s[:],
                                  in_=ap_fr[sl]).then_inc(semPF, 16)
                nc.sync.wait_ge(semPF, 16)
                nc.sync.reg_load(poll_v, flag_s[0:1, 0:1])
                nc.sync.reg_load(poll_n, nonce_s[0:1, exi:exi + 1])
                nc.sync.reg_alu(poll_nd, poll_v, poll_n, ALU.not_equal)

        def sh_read(dst_ap, plane, r0, nrows, width):
            row0 = plane * 136 + r0
            m = dinc("SH", nc.sync.dma_start(
                out=dst_ap, in_=SHBa[row0:row0 + nrows, 0:width]))
            nc.sync.wait_ge(semSH, m)
            return m

        def th_ln(src, dst, eps):
            wt(nc.vector, "S")
            wt(nc.vector, "V")
            inc("V", nc.vector.tensor_reduce(th_scr(0), src, AX.X, ALU.add))
            wt(nc.scalar, "V")
            wt(nc.scalar, "T")
            inc("S", nc.scalar.activation(dst, src, AF.Square))
            wt(nc.vector, "S")
            inc("V", nc.vector.tensor_reduce(th_scr(1), dst, AX.X, ALU.add))
            wt(nc.scalar, "V")
            nc.scalar.mul(th_scr(2), th_scr(0), -1.0 / D)
            nc.scalar.mul(th_scr(1), th_scr(1), 1.0 / D)
            inc("S", nc.scalar.mul(th_scr(3), th_scr(0), 1.0 / D))
            wt(nc.scalar, "S")
            inc("S", nc.scalar.activation(th_scr(0), th_scr(3), AF.Square))
            wt(nc.vector, "S")
            inc("V", nc.vector.tensor_tensor(th_scr(1), th_scr(1), th_scr(0), ALU.subtract))
            wt(nc.scalar, "V")
            inc("S", nc.scalar.activation(th_scr(1), th_scr(1), AF.Ln, bias=eps))
            wt(nc.scalar, "S")
            inc("S", nc.scalar.activation(th_scr(1), th_scr(1), AF.Exp, scale=-0.5))
            wt(nc.vector, "S")
            inc("V", nc.vector.tensor_scalar(dst, src, th_scr(2), th_scr(1), ALU.add, ALU.mult))

        rrow_buf = ROWSa[0:4, 4096:4608]

        def th_loadrow(off, n):
            wt(nc.sync, "V")
            return dinc("ROW", nc.sync.dma_start(out=rrow_buf[:, 0:n], in_=rvec[:, off:off + n]))

        def th_rowapply(dst, off, n, op):
            for q in range((n + 511) // 512):
                c0 = q * 512
                w = min(n, c0 + 512) - c0
                ldm = th_loadrow(off + c0, w)
                nc.vector.wait_ge(semROW, ldm)
                wt(nc.vector, "V")
                inc("V", nc.vector.tensor_tensor(dst[:, c0:c0 + w], dst[:, c0:c0 + w],
                                                 rrow_buf[:, 0:w], op))

        def th_transpose(src4, dstT, ncols):
            wt(nc.tensor, "V")
            wt(nc.tensor, "S")
            psum_wait(nc.tensor, "B3")
            bp = src4.base_partition()
            idsl = identf_s.ap()[bp:bp + 4, bp:bp + 4]
            for j in range(ncols):
                mmi = nc.tensor.transpose(PBa[:, 1536 + j * 4:1536 + (j + 1) * 4],
                                          src4[:, j * 128:(j + 1) * 128], idsl)
                if j == ncols - 1:
                    inc("T", mmi)
            wt(nc.vector, "T")
            inc("V", nc.vector.tensor_copy(dstT[:, 0:ncols * 4], PBa[:, 1536:1536 + ncols * 4]))
            psum_used("B3", "V")

        def ts_slot(i):
            return xab(XA_SP + i * 1024, 2048)

        THS = dict(slot_mm=[0, 0], slot_wb=[0, 0], nslot=0)

        def th_chunk(src_fp32, src_bf16, cache_dst, step, nmm_fn):
            i = THS["nslot"] % 2
            THS["nslot"] += 1
            wkey = "Te" if i == 0 else "To"
            ckey = "Ce" if i == 0 else "Co"
            if THS["slot_mm"][i]:
                nc.gpsimd.wait_ge(semT, THS["slot_mm"][i])
            if THS["slot_wb"][i]:
                nc.gpsimd.wait_ge(sems[ckey], THS["slot_wb"][i])
            if step == 0 and src_fp32 is not None:
                m = dinc(wkey, nc.gpsimd.dma_start(out=ts_slot(i), in_=src_fp32))
            else:
                m = dinc(wkey, nc.gpsimd.dma_start(out=ts_slot(i), in_=src_bf16))
            nc.tensor.wait_ge(sems[wkey], m)
            nmm_fn(ts_slot(i))
            THS["slot_mm"][i] = cnt["T"]
            if step == 0 and cache_dst is not None:
                nc.scalar.wait_ge(semT, cnt["T"])
                THS["slot_wb"][i] = dinc(ckey, nc.scalar.dma_start(
                    out=cache_dst, in_=ts_slot(i)))

        def th_pre():
            nc.vector.wait_ge(semX, base_loads)
            th_ln(th_main, th_t, EPS1)
            th_ln(th_t, th_main, EPS2)
            th_rowapply(th_main, RV_ING, 2048, ALU.mult)
            th_rowapply(th_main, RV_INB, 2048, ALU.add)
            th_transpose(th_main, xnlT_s.ap(), 16)
            psum_wait(nc.tensor, "B0", "B1")
            wt(nc.tensor, "V")
            for c8 in range(8):
                def mms(sl, c8=c8):
                    for k in range(16):
                        mmi = nc.tensor.matmul(
                            PBa[0:4, c8 * 128:(c8 + 1) * 128],
                            xnlT_s[:, k * 4:(k + 1) * 4],
                            sl[:, k * 128:(k + 1) * 128],
                            start=(k == 0), stop=(k == 15))
                        if k == 15:
                            inc("T", mmi)
                th_chunk(aggw[:, c8 * 128:(c8 + 1) * 128].rearrange(
                    "(k p) n -> p k n", p=128), None, None, 0, mms)
            wt(nc.vector, "T")
            inc("V", nc.vector.tensor_copy(th_d[:, 0:1024], PBa[0:4, 0:1024]))
            psum_used("B0", "V")
            psum_used("B1", "V")
            wt(nc.sync, "V")
            sh_write(th_d[:, 0:1024], ("s", 1024), E_AGG)
            sh_poll(E_AGG)
            sh_read(th_main[:, 0:1024], 0, 0, 4, 1024)
            m = sh_read(th_main[:, 1024:2048], 1, 0, 4, 1024)
            nc.vector.wait_ge(semSH, m)
            inc("V", nc.vector.tensor_copy(th_main, th_main))
            th_rowapply(th_main, RV_AGB, 2048, ALU.add)

        def th_step(step):
            ex = [E_S1, E_S2, E_S3, E_S4][step]
            r0 = (ex % 2) * 4
            th_ln(th_main, th_t, EPS2)
            th_rowapply(th_t, RV_PNG, 2048, ALU.mult)
            th_rowapply(th_t, RV_PNB, 2048, ALU.add)
            th_transpose(th_t, tT_s.ap(), 16)
            for nh in range(2):
                psum_wait(nc.tensor, "B0", "B1", "B2", "B3")
                wt(nc.tensor, "V")
                for k in range(16):
                    def mms(sl, k=k):
                        for q in range(4):
                            mmi = nc.tensor.matmul(
                                PBa[0:4, q * 512:(q + 1) * 512],
                                tT_s[:, k * 4:(k + 1) * 4],
                                sl[:, q * 512:(q + 1) * 512],
                                start=(k == 0), stop=(k == 15))
                            if q == 3:
                                inc("T", mmi)
                    th_chunk(
                        d1h[k * 128:(k + 1) * 128, nh * 2048:(nh + 1) * 2048],
                        dc1.ap()[k * 128:(k + 1) * 128, nh * 2048:(nh + 1) * 2048],
                        dc1.ap()[k * 128:(k + 1) * 128, nh * 2048:(nh + 1) * 2048],
                        step, mms)
                wt(nc.vector, "T")
                inc("V", nc.vector.tensor_copy(th_t, PBa[0:4, 0:2048]))
                for kk in ("B0", "B1", "B2", "B3"):
                    psum_used(kk, "V")
                th_rowapply(th_t, RV_D1B + nh * 2048, 2048, ALU.add)
                wt(nc.scalar, "V")
                inc("S", nc.scalar.activation(th_t, th_t, AF.Gelu))
                th_transpose(th_t, uT_s.ap()[:, nh * 64:(nh + 1) * 64], 16)
            psum_wait(nc.tensor, "B0", "B1", "B2", "B3")
            wt(nc.tensor, "V")
            for kt in range(32):
                def mms(sl, kt=kt):
                    for q in range(4):
                        mmi = nc.tensor.matmul(
                            PBa[0:4, q * 512:(q + 1) * 512],
                            uT_s[:, kt * 4:(kt + 1) * 4],
                            sl[:, q * 512:(q + 1) * 512],
                            start=(kt == 0), stop=(kt == 31))
                        if q == 3:
                            inc("T", mmi)
                th_chunk(d2h[kt * 128:(kt + 1) * 128, :],
                         dc2.ap()[kt * 128:(kt + 1) * 128, :],
                         dc2.ap()[kt * 128:(kt + 1) * 128, :], step, mms)
            wt(nc.vector, "T")
            inc("V", nc.vector.tensor_copy(th_d[:, 0:2048], PBa[0:4, 0:2048]))
            for kk in ("B0", "B1", "B2", "B3"):
                psum_used(kk, "V")
            wt(nc.sync, "V")
            sh_write(th_d[:, 0:2048], ("s", 2048) if r0 == 0 else ("s1", 2048), ex)
            sh_poll(ex)
            m = sh_read(th_d[:, 0:2048], 0, r0, 4, 2048)
            nc.vector.wait_ge(semSH, m)
            wt(nc.vector, "V")
            inc("V", nc.vector.tensor_tensor(th_main, th_main, th_d[:, 0:2048], ALU.add))
            wt(nc.sync, "V")
            m = sh_read(th_d[:, 0:2048], 1, r0, 4, 2048)
            nc.vector.wait_ge(semSH, m)
            wt(nc.vector, "V")
            inc("V", nc.vector.tensor_tensor(th_main, th_main, th_d[:, 0:2048], ALU.add))
            th_rowapply(th_main, RV_D2B, 2048, ALU.add)

        def th_bc():
            th_transpose(th_main, tT_s.ap(), 16)
            psum_wait(nc.tensor, "B0", "B1")
            wt(nc.tensor, "V")
            for c8 in range(8):
                def mms(sl, c8=c8):
                    for k in range(16):
                        mmi = nc.tensor.matmul(
                            PBa[0:4, c8 * 128:(c8 + 1) * 128],
                            tT_s[:, k * 4:(k + 1) * 4],
                            sl[:, k * 128:(k + 1) * 128],
                            start=(k == 0), stop=(k == 15))
                        if k == 15:
                            inc("T", mmi)
                th_chunk(bcw[:, c8 * 128:(c8 + 1) * 128].rearrange(
                    "(k p) n -> p k n", p=128), None, None, 0, mms)
            wt(nc.vector, "T")
            inc("V", nc.vector.tensor_copy(th_d[:, 0:1024], PBa[0:4, 0:1024]))
            psum_used("B0", "V")
            psum_used("B1", "V")
            wt(nc.sync, "V")
            sh_write(th_d[:, 0:1024], ("s", 1024), E_BC)
            sh_poll(E_BC)
            sh_read(th_d[:, 0:1024], 0, 0, 4, 1024)
            m = sh_read(th_d[:, 1024:2048], 1, 0, 4, 1024)
            nc.vector.wait_ge(semSH, m)
            inc("V", nc.vector.tensor_copy(th_d, th_d))
            th_rowapply(th_d, RV_BCB, 2048, ALU.add)
            th_transpose(th_d, bcT_s.ap(), 16)
            bcT_ap = bcT_s.ap()

            def bcT_slice(b):
                sl = bcT_ap[:, b:b + 61]
                return dataclasses.replace(sl, ap=[sl.ap[0], [4, 16]])
            wt(nc.vector, "V")
            inc("V", nc.vector.tensor_scalar_mul(bcP_s[:], bcT_slice(0), bselP_s[:, 0:1]))
            for b4 in range(1, 4):
                wt(nc.vector, "V")
                inc("V", nc.vector.scalar_tensor_tensor(bcP_s[:], bcT_slice(b4),
                                                        bselP_s[:, b4:b4 + 1],
                                                        bcP_s[:], ALU.mult, ALU.add))

        th_pre()

        wk_done = dinc("G4", nc.gpsimd.dma_start(
            out=SOa[:, 0:8192].rearrange("p (k n) -> p k n", k=16),
            in_=wkT.rearrange("(k p) n -> p k n", p=128)))
        wv_done = dinc("G5", nc.gpsimd.dma_start(
            out=SOa[:, 8192:16384].rearrange("p (k n) -> p k n", k=16),
            in_=wvT.rearrange("(k p) n -> p k n", p=128)))

        def proj_tok(wbase_b, dst_c0f32, wsem, wmark):
            evs = []
            for tt in range(NCH):
                pp = tt % 2
                psl, kname = (A0, "A0") if pp == 0 else (A1, "A1")
                nc.tensor.wait_ge(wsem, wmark)
                nc.tensor.wait_ge(semV, xn_all)
                psum_wait(nc.tensor, kname)
                for k in range(16):
                    st, sp = (k == 0), (k == 15)
                    mmi = nc.tensor.matmul(
                        psl, XNa[:, k * 1024 + tt * 128:k * 1024 + (tt + 1) * 128],
                        SOa[:, wbase_b + k * 512:wbase_b + (k + 1) * 512],
                        start=st, stop=sp)
                    if sp:
                        inc("T", mmi)
                wt(nc.scalar, "T")
                evs.append(inc("S", nc.scalar.copy(
                    xab(dst_c0f32 + tt * 256, 512), psl)))
                psum_used(kname, "S")
            return evs

        k_evac = proj_tok(0, XA_K, semG4, wk_done)
        v_evac = proj_tok(8192, XA_V, semG5, wv_done)
        kv_T_end = cnt["T"]
        v_S_end = cnt["S"]

        def transpose_128(src_ap, dst_ap, pairs, wait_keys):
            for i0 in range(0, len(pairs), 8):
                batch = pairs[i0:i0 + 8]
                for eng_key, val in wait_keys:
                    nc.tensor.wait_ge(sems[eng_key], val)
                psum_wait(nc.tensor, "PT")
                for j, (so, do) in enumerate(batch):
                    mmi = nc.tensor.transpose(PTa[:, j * 128:(j + 1) * 128],
                                              src_ap[:, so:so + 128], identb_s[:])
                    if j == len(batch) - 1:
                        inc("T", mmi)
                wt(nc.scalar, "T")
                for j, (so, do) in enumerate(batch):
                    mmi = nc.scalar.copy(dst_ap[:, do:do + 128], PTa[:, j * 128:(j + 1) * 128])
                    if j == len(batch) - 1:
                        inc("S", mmi)
                psum_used("PT", "S")

        kfwd = [(tt * 512 + kv * 128, kv * 1024 + tt * 128)
                for kv in range(HKV) for tt in range(NCH)]
        transpose_128(xab(XA_K, 4096), xab(XA_KT, 4096), kfwd,
                      [("S", v_S_end)])
        kT_raw_S = cnt["S"]

        def rope_featmaj(base_ap, nheads, swap_c0, a_c0):
            for h in range(nheads):
                tile = base_ap[:, h * 1024:(h + 1) * 1024]
                swp = xab(swap_c0, 1024)
                atmp = xab(a_c0, 1024)
                wt(nc.scalar, "S")
                wt(nc.scalar, "V")
                nc.scalar.copy(swp[0:64, :], tile[64:128, :])
                inc("S", nc.scalar.copy(swp[64:128, :], tile[0:64, :]))
                wt(nc.vector, "S")
                inc("V", nc.vector.tensor_tensor(atmp, tile, cosF_s, ALU.mult))
                wt(nc.vector, "V")
                inc("V", nc.vector.tensor_tensor(swp, swp, sinS_s, ALU.mult))
                wt(nc.vector, "V")
                inc("V", nc.vector.tensor_tensor(tile, atmp, swp, ALU.add))

        rope_featmaj(xab(XA_KT, 4096), HKV, XA_E, XA_E + 512)
        kT_rope_V = cnt["V"]

        kbwd = [(kv * 1024 + tt * 128, tt * 512 + kv * 128)
                for kv in range(HKV) for tt in range(NCH)]
        transpose_128(xab(XA_KT, 4096), xab(XA_K, 4096), kbwd,
                      [("V", kT_rope_V)])
        k_rope_S = cnt["S"]

        state = XAa[:, XA_ST:XA_ST + 2048]
        nc.vector.wait_ge(semT, kv_T_end)
        so_z = inc("V", nc.vector.memset(SOa[:, 0:2048], 0.0))
        inc("V", nc.vector.memset(state, 0.0))
        state_V = cnt["V"]
        ktil = xab(XA_G0, 2048)
        for c in range(NCH):
            nc.vector.wait_ge(semS, k_rope_S)
            wt(nc.vector, "T")
            for h in range(H):
                kv = h // (H // HKV)
                mmi = nc.vector.tensor_scalar_mul(
                    ktil[:, h * 128:(h + 1) * 128],
                    XAb[:, 2 * XA_K + c * 512 + kv * 128: 2 * XA_K + c * 512 + (kv + 1) * 128],
                    kdecC_s[:, h:h + 1])
                if h == H - 1:
                    inc("V", mmi)
            ktil_V = cnt["V"]
            nc.tensor.wait_ge(semV, ktil_V)
            nc.tensor.wait_ge(semS, v_S_end)
            psum_wait(nc.tensor, "B0", "B1", "B2", "B3")
            for h in range(H):
                kv = h // (H // HKV)
                mmi = nc.tensor.matmul(
                    PBa[:, h * 128:(h + 1) * 128],
                    ktil[:, h * 128:(h + 1) * 128],
                    XAb[:, 2 * XA_V + c * 512 + kv * 128: 2 * XA_V + c * 512 + (kv + 1) * 128],
                    start=True, stop=True)
                if h == H - 1:
                    inc("T", mmi)
            U_T = cnt["T"]
            nc.vector.wait_ge(semT, U_T)
            nc.vector.wait_ge(semV, state_V)
            nc.vector.wait_ge(semS, cnt["S"])
            for h in range(H):
                mmi = nc.vector.tensor_scalar_mul(state[:, h * 128:(h + 1) * 128],
                                                  state[:, h * 128:(h + 1) * 128], blk_f[h])
                if h == H - 1:
                    inc("V", mmi)
            wt(nc.vector, "V")
            inc("V", nc.vector.tensor_tensor(state, state, PBa[:, 0:2048], ALU.add))
            state_V = cnt["V"]
            for kk in ("B0", "B1", "B2", "B3"):
                psum_used(kk, "V")
            if c < NCH - 1:
                wt(nc.scalar, "V")
                inc("S", nc.scalar.copy(SOa[:, (c + 1) * 2048:(c + 2) * 2048], state))
        scan_S = cnt["S"]
        scan_V = cnt["V"]

        wt(nc.sync, "V")
        sh_write(state, ("st", 2048), E_STATE)
        sh_poll(E_STATE)
        Cadj = XAa[:, XA_E:XA_E + 2048]
        cd_ld = sh_read(Cadj, 0, 8, 128, 2048)
        nc.vector.wait_ge(semSH, cd_ld)
        inc("V", nc.vector.tensor_scalar_mul(Cadj, Cadj, selS_s[:]))
        for c in range(NCH):
            wt(nc.vector, "V")
            if c == 0:
                nc.vector.wait_ge(semS, scan_S)
                nc.vector.wait_ge(semV, so_z)
            inc("V", nc.vector.tensor_tensor(SOa[:, c * 2048:(c + 1) * 2048],
                                             SOa[:, c * 2048:(c + 1) * 2048], Cadj, ALU.add))
            if c < NCH - 1:
                wt(nc.vector, "V")
                for h in range(H):
                    mmi = nc.vector.tensor_scalar_mul(Cadj[:, h * 128:(h + 1) * 128],
                                                      Cadj[:, h * 128:(h + 1) * 128], blk_f[h])
                    if h == H - 1:
                        inc("V", mmi)
        adj_V = cnt["V"]

        th_step(0)
        th_step(1)

        wq_marks = []
        for i in range(NDT):
            slot = i % 2
            if i >= 2:
                nc.gpsimd.wait_ge(semT, wq_marks[i - 2][1])
            wkey = "We" if slot == 0 else "Wo"
            ld = dinc(wkey, nc.gpsimd.dma_start(
                out=WSa[:, slot * 2048:(slot + 1) * 2048].rearrange("p (k n) -> p k n", k=16),
                in_=wqT[:, i * 128:(i + 1) * 128].rearrange("(k p) n -> p k n", p=128)))
            wq_marks.append((ld, None))
            for th2 in range(2):
                pp = th2
                psl, kname = (A0, "A0") if pp == 0 else (A1, "A1")
                nc.tensor.wait_ge(sems[wkey], ld)
                if i == 0 and th2 == 0:
                    nc.tensor.wait_ge(semV, adj_V)
                psum_wait(nc.tensor, kname)
                for k in range(16):
                    st, sp = (k == 0), (k == 15)
                    mmi = nc.tensor.matmul(
                        psl, WSa[:, slot * 2048 + k * 128:slot * 2048 + (k + 1) * 128],
                        XNa[:, k * 1024 + th2 * 512:k * 1024 + (th2 + 1) * 512],
                        start=st, stop=sp)
                    if sp:
                        inc("T", mmi)
                wq_marks[i] = (ld, cnt["T"])
                wt(nc.scalar, "T")
                if i == 0 and th2 == 0:
                    nc.scalar.wait_ge(semV, xn_all)
                inc("S", nc.scalar.copy(B1a[:, i * 1024 + th2 * 512:i * 1024 + (th2 + 1) * 512], psl))
                psum_used(kname, "S")
        q_proj_S = cnt["S"]
        th_step(2)
        nc.vector.wait_ge(semS, q_proj_S)
        rope_featmaj(B1a, H, XA_G0, XA_G0 + 512)
        q_rope_V = cnt["V"]

        nc.scalar.wait_ge(semV, cnt["V"])
        dg_ld = dinc("X", nc.scalar.dma_start(out=XAa[:, XA_K:XA_K + 2048], in_=diagT))
        qd_ld = dinc("X", nc.scalar.dma_start(out=XAa[:, XA_ST:XA_ST + 2048], in_=qdecT))
        diagT_s = XAa[:, XA_K:XA_K + 2048]
        qdecT_s = XAa[:, XA_ST:XA_ST + 2048]
        sc_tmp = xab(XA_G1, 1024)
        qt_tmp = xab(XA_G1 + 512, 1024)
        o_evac_S = None
        for c in range(NCH):
            for hg in range(2):
                nc.tensor.wait_ge(semS, k_rope_S)
                nc.tensor.wait_ge(semV, q_rope_V)
                psum_wait(nc.tensor, "A0", "A1")
                for j in range(8):
                    h = hg * 8 + j
                    kv = h // (H // HKV)
                    mmi = nc.tensor.matmul(
                        PAa[:, j * 128:(j + 1) * 128],
                        XAb[:, 2 * XA_KT + kv * 1024 + c * 128: 2 * XA_KT + kv * 1024 + (c + 1) * 128],
                        B1a[:, h * 1024 + c * 128:h * 1024 + (c + 1) * 128],
                        start=True, stop=True)
                    if j == 7:
                        inc("T", mmi)
                sc_T = cnt["T"]
                nc.vector.wait_ge(semT, sc_T)
                nc.vector.wait_ge(semX, qd_ld)
                inc("V", nc.vector.tensor_tensor(sc_tmp, PAa[:, 0:1024],
                                                 diagT_s[:, hg * 1024:(hg + 1) * 1024], ALU.mult))
                psum_used("A0", "V")
                psum_used("A1", "V")
                nc.vector.wait_ge(semX, qd_ld)
                qsl = B1a[:, hg * 8192 + c * 128:hg * 8192 + c * 128 + 128]
                qsl = dataclasses.replace(qsl, ap=[qsl.ap[0], [1024, 8], [1, 128]])
                dsl = qdecT_s[:, hg * 1024:hg * 1024 + 128]
                dsl = dataclasses.replace(dsl, ap=[dsl.ap[0], [128, 8], [1, 128]])
                inc("V", nc.vector.tensor_tensor(qt_tmp, qsl, dsl, ALU.mult))
                ev_V = cnt["V"]
                nc.tensor.wait_ge(semV, ev_V)
                if o_evac_S is not None:
                    nc.tensor.wait_ge(semS, o_evac_S)
                pk = ("B0", "B1") if hg == 0 else ("B2", "B3")
                psum_wait(nc.tensor, *pk)
                for j in range(8):
                    h = hg * 8 + j
                    kv = h // (H // HKV)
                    po = PBa[:, hg * 1024 + j * 128:hg * 1024 + (j + 1) * 128]
                    nc.tensor.matmul(
                        po,
                        XAb[:, 2 * XA_V + c * 512 + kv * 128: 2 * XA_V + c * 512 + (kv + 1) * 128],
                        sc_tmp[:, j * 128:(j + 1) * 128], start=True, stop=False)
                    mmi = nc.tensor.matmul(
                        po,
                        SOa[:, c * 2048 + h * 128:c * 2048 + (h + 1) * 128],
                        qt_tmp[:, j * 128:(j + 1) * 128], start=False, stop=True)
                    if j == 7:
                        inc("T", mmi)
            o_T = cnt["T"]
            nc.scalar.wait_ge(semT, o_T)
            nc.scalar.wait_ge(semV, adj_V)
            o_evac_S = inc("S", nc.scalar.copy(SOa[:, c * 2048:(c + 1) * 2048], PBa[:, 0:2048]))
            for kk in ("B0", "B1", "B2", "B3"):
                psum_used(kk, "S")
        o_done_S = cnt["S"]

        th_step(3)
        th_bc()

        B_XRE = 0
        B_G = 1024
        B_XNT = 2048
        B_T1 = 3072
        B_OUT = 4096
        B_BC1 = 5120
        nc.vector.wait_ge(semS, o_done_S)
        nc.vector.wait_ge(semT, cnt["T"])
        bcast_row(row(0, 1), B1f[:, B_BC1:B_BC1 + 1024], "A0", "A1", A0, A1)
        bcast_row(row(64, 0), B1f[:, B_BC1 + 1024:B_BC1 + 2048], "A0", "A1", A0, A1)
        rstd1t = B1f[:, B_BC1:B_BC1 + 1024]
        nmr1t = B1f[:, B_BC1 + 1024:B_BC1 + 2048]
        tail_bc_V = cnt["V"]

        gate_marks = []
        pre_V = []
        ln2_mm = 0
        for i in range(NDT):
            slot = i % 2
            if i >= 2:
                nc.gpsimd.wait_ge(semT, gate_marks[i - 2])
            else:
                nc.gpsimd.wait_ge(semT, cnt["T"])
            wkey = "We" if slot == 0 else "Wo"
            gld = dinc(wkey, nc.gpsimd.dma_start(
                out=WSa[:, slot * 2048:(slot + 1) * 2048].rearrange("p (k n) -> p k n", k=16),
                in_=gw[:, i * 128:(i + 1) * 128].rearrange("(k p) n -> p k n", p=128)))
            for th2 in range(2):
                sslot = (2 * i + th2) % 2
                psl, kname = (A0, "A0") if sslot == 0 else (A1, "A1")
                if len(pre_V) >= 2:
                    nc.scalar.wait_ge(semV, pre_V[-2])
                bkey = "Re" if sslot == 0 else "Ro"
                xre = B1f[:, B_XRE + sslot * 512:B_XRE + (sslot + 1) * 512]
                xrm = dinc(bkey, nc.scalar.dma_start(
                    out=xre, in_=xsT[i * 128:(i + 1) * 128, th2 * 512:(th2 + 1) * 512]))
                nc.tensor.wait_ge(sems[wkey], gld)
                psum_wait(nc.tensor, kname)
                for k in range(16):
                    st, sp = (k == 0), (k == 15)
                    mmi = nc.tensor.matmul(
                        psl, WSa[:, slot * 2048 + k * 128:slot * 2048 + (k + 1) * 128],
                        XNa[:, k * 1024 + th2 * 512:k * 1024 + (th2 + 1) * 512],
                        start=st, stop=sp)
                    if sp:
                        inc("T", mmi)
                gate_marks_val = cnt["T"]
                wt(nc.scalar, "T")
                gt = B1f[:, B_G + sslot * 512:B_G + (sslot + 1) * 512]
                inc("S", nc.scalar.activation(gt, psl, AF.Sigmoid, bias=pvecs_s[:, i:i + 1]))
                psum_used(kname, "S")
                nc.vector.wait_ge(sems[bkey], xrm)
                nc.vector.wait_ge(semV, tail_bc_V)
                xnt = B1f[:, B_XNT + sslot * 512:B_XNT + (sslot + 1) * 512]
                inc("V", nc.vector.tensor_tensor(xnt, xre,
                                                 rstd1t[:, th2 * 512:(th2 + 1) * 512], ALU.mult))
                wt(nc.vector, "V")
                inc("V", nc.vector.tensor_tensor(xnt, xnt,
                                                 nmr1t[:, th2 * 512:(th2 + 1) * 512], ALU.add))
                wt(nc.vector, "S")
                wt(nc.vector, "V")
                pre_sl = XAa[:, i * 1024 + th2 * 512:i * 1024 + (th2 + 1) * 512]
                pre_V.append(inc("V", nc.vector.scalar_tensor_tensor(
                    pre_sl, gt, bcP_s[:, i:i + 1], xnt, ALU.mult, ALU.add)))
                wt(nc.scalar, "V")
                pb16 = B1a[:, 14336 + sslot * 512:14336 + (sslot + 1) * 512]
                inc("S", nc.scalar.copy(pb16, pre_sl))
                wt(nc.scalar, "S")
                sq16 = B1a[:, 15360 + sslot * 512:15360 + (sslot + 1) * 512]
                inc("S", nc.scalar.activation(sq16, pb16, AF.Square))
                wt(nc.tensor, "S")
                st16 = (i == 0)
                sp16 = (i == NDT - 1)
                if st16 and th2 == 0:
                    psum_wait(nc.tensor, "B0", "B1", "B2", "B3")
                nc.tensor.matmul(PBa[0:1, th2 * 512:(th2 + 1) * 512], onesb_s[:], pb16,
                                 start=st16, stop=sp16)
                inc("T", nc.tensor.matmul(PBa[0:1, 1024 + th2 * 512:1024 + (th2 + 1) * 512],
                                          onesb_s[:], sq16, start=st16, stop=sp16))
            gate_marks.append(cnt["T"])
        ln2_T = cnt["T"]

        wt(nc.vector, "T")
        nc.vector.tensor_copy(row(0, 0)[:, 0:512], PBa[0:1, 0:512])
        nc.vector.tensor_copy(row(0, 0)[:, 512:1024], PBa[0:1, 512:1024])
        nc.vector.tensor_copy(row(32, 0)[:, 0:512], PBa[0:1, 1024:1536])
        inc("V", nc.vector.tensor_copy(row(32, 0)[:, 512:1024], PBa[0:1, 1536:2048]))
        for kk in ("B0", "B1", "B2", "B3"):
            psum_used(kk, "V")
        ln_rows(row(0, 0), row(32, 0), row(0, 1), row(64, 0), row(32, 0), EPS2)
        bcast_row(row(0, 1), B1f[:, B_BC1:B_BC1 + 1024], "A0", "A1", A0, A1)
        bcast_row(row(64, 0), B1f[:, B_BC1 + 1024:B_BC1 + 2048], "A0", "A1", A0, A1)
        rstd2t = B1f[:, B_BC1:B_BC1 + 1024]
        nmr2t = B1f[:, B_BC1 + 1024:B_BC1 + 2048]
        tail2_bc_V = cnt["V"]

        wo_marks = []
        out_V = []
        out_marks = [0, 0]
        for i in range(NDT):
            slot = i % 2
            if i >= 2:
                nc.gpsimd.wait_ge(semT, wo_marks[i - 2])
            else:
                nc.gpsimd.wait_ge(semT, ln2_T)
            wkey = "We" if slot == 0 else "Wo"
            wld = dinc(wkey, nc.gpsimd.dma_start(
                out=WSa[:, slot * 2048:(slot + 1) * 2048].rearrange("p (k n) -> p k n", k=16),
                in_=woT[:, i * 128:(i + 1) * 128].rearrange("(k p) n -> p k n", p=128)))
            for th2 in range(2):
                sslot = (2 * i + th2) % 2
                psl, kname = (A0, "A0") if sslot == 0 else (A1, "A1")
                if len(out_V) >= 2:
                    nc.scalar.wait_ge(semV, out_V[-2])
                else:
                    nc.scalar.wait_ge(semV, tail2_bc_V)
                bkey = "Re" if sslot == 0 else "Ro"
                xre = B1f[:, B_XRE + sslot * 512:B_XRE + (sslot + 1) * 512]
                xrm = dinc(bkey, nc.scalar.dma_start(
                    out=xre, in_=xsT[i * 128:(i + 1) * 128, th2 * 512:(th2 + 1) * 512]))
                nc.tensor.wait_ge(sems[wkey], wld)
                nc.tensor.wait_ge(semS, o_done_S)
                psum_wait(nc.tensor, kname)
                for k in range(16):
                    st, sp = (k == 0), (k == 15)
                    osl = SOa[:, th2 * 8192 + k * 128:th2 * 8192 + k * 128 + 128]
                    osl = dataclasses.replace(osl, ap=[osl.ap[0], [2048, 4], [1, 128]])
                    mmi = nc.tensor.matmul(
                        psl, WSa[:, slot * 2048 + k * 128:slot * 2048 + (k + 1) * 128],
                        osl, start=st, stop=sp)
                    if sp:
                        inc("T", mmi)
                wo_mark_val = cnt["T"]
                wt(nc.vector, "T")
                nc.vector.wait_ge(semV, tail2_bc_V)
                nc.vector.wait_ge(sems[bkey], xrm)
                t1 = B1f[:, B_T1 + sslot * 512:B_T1 + (sslot + 1) * 512]
                pre_sl = XAa[:, i * 1024 + th2 * 512:i * 1024 + (th2 + 1) * 512]
                inc("V", nc.vector.tensor_tensor(t1, pre_sl,
                                                 rstd2t[:, th2 * 512:(th2 + 1) * 512], ALU.mult))
                wt(nc.vector, "V")
                inc("V", nc.vector.tensor_tensor(t1, t1, nmr2t[:, th2 * 512:(th2 + 1) * 512], ALU.add))
                wt(nc.vector, "V")
                inc("V", nc.vector.tensor_scalar(t1, t1, pvecs_s[:, 16 + i:17 + i],
                                                 pvecs_s[:, 32 + i:33 + i], ALU.mult, ALU.add))
                wt(nc.vector, "V")
                inc("V", nc.vector.tensor_tensor(t1, t1, xre, ALU.add))
                wt(nc.vector, "V")
                okey = "Oe" if sslot == 0 else "Oo"
                if out_marks[sslot]:
                    nc.vector.wait_ge(sems[okey], out_marks[sslot])
                osl2 = B1f[:, B_OUT + sslot * 512:B_OUT + (sslot + 1) * 512]
                out_V.append(inc("V", nc.vector.tensor_tensor(osl2, t1, psl, ALU.add)))
                psum_used(kname, "V")
                okey = "Oe" if sslot == 0 else "Oo"
                wt(nc.scalar, "V")
                out_marks[sslot] = dinc(okey, nc.scalar.dma_start(
                    out=outT[i * 128:(i + 1) * 128, th2 * 512:(th2 + 1) * 512], in_=osl2))
            wo_marks.append(cnt["T"])

        nc.scalar.wait_ge(semOe, cnt["Oe"])
        nc.scalar.wait_ge(semOo, cnt["Oo"])
        nc.sync.wait_ge(semOe, cnt["Oe"])
        nc.sync.wait_ge(semOo, cnt["Oo"])

    return nc


def _prep_inmaps(inputs):
    x = np.asarray(inputs["L0_attn_x"], dtype=np.float32)
    wq = np.asarray(inputs["L0_attn_wq"], dtype=np.float32)
    wk = np.asarray(inputs["L0_attn_wk"], dtype=np.float32)
    wv = np.asarray(inputs["L0_attn_wv"], dtype=np.float32)
    wo = np.asarray(inputs["L0_attn_wo"], dtype=np.float32)
    wqT = np.ascontiguousarray(wq.T)
    wkT = np.ascontiguousarray(wk.T)
    wvT = np.ascontiguousarray(wv.T)
    woT = np.ascontiguousarray(wo.T)
    gwv = np.ascontiguousarray(np.asarray(inputs["L0_ffn_gate_w"], dtype=np.float32))
    aggw_f = np.asarray(inputs["L0_ffn_agg_w"], dtype=np.float32)
    bcw_f = np.asarray(inputs["L0_ffn_bc_w"], dtype=np.float32)
    d1w = np.asarray(inputs["L0_ffn_d1_w"], dtype=np.float32)
    d2w = np.asarray(inputs["L0_ffn_d2_w"], dtype=np.float32)
    d1b = np.asarray(inputs["L0_ffn_d1_b"], dtype=np.float32)

    ident_b = np.eye(128, dtype=bfl)
    ident_f = np.eye(128, dtype=np.float32)[:, 0:4].copy()
    ones_b = np.ones((128, 1), dtype=bfl)
    ones_f = np.ones((128, 128), dtype=np.float32)

    def pcol(v):
        return np.ascontiguousarray(v.reshape(16, 128).T.astype(np.float32))

    pv = np.concatenate([
        pcol(np.asarray(inputs["L0_ffn_gate_b"], dtype=np.float32)),
        pcol(np.asarray(inputs["L0_ffn_out_gamma"], dtype=np.float32)),
        pcol(np.asarray(inputs["L0_ffn_out_beta"], dtype=np.float32)),
    ], axis=1)

    lam = np.exp2(-8.0 * np.arange(1, H + 1, dtype=np.float64) / H)
    r = np.arange(CH, dtype=np.float64)
    dm = r[None, None, :] - r[None, :, None]
    dmat = np.where(dm >= 0, np.exp(-lam[:, None, None] * np.maximum(dm, 0)), 0.0)
    diag_t = np.ascontiguousarray(
        dmat.transpose(1, 0, 2).reshape(CH, H * CH).astype(np.float32))
    qdec = np.exp(-lam[:, None] * (r[None, :] + 1.0))
    qdec_t = np.ascontiguousarray(
        np.broadcast_to(qdec.reshape(1, H * CH), (128, H * CH)).astype(np.float32))
    kdec = np.exp(-lam[None, :] * (CH - 1.0 - r[:, None]))
    kdec_c = np.ascontiguousarray(kdec.astype(np.float32))

    inv_freq = 1.0 / 10000.0 ** (np.arange(0, DH, 2, dtype=np.float64) / DH)

    rv = np.concatenate([
        np.asarray(inputs["L0_ffn_in_gamma"], dtype=np.float32),
        np.asarray(inputs["L0_ffn_in_beta"], dtype=np.float32),
        np.asarray(inputs["L0_ffn_pn_gamma"], dtype=np.float32),
        np.asarray(inputs["L0_ffn_pn_beta"], dtype=np.float32),
        np.asarray(inputs["L0_ffn_d2_b"], dtype=np.float32),
        np.asarray(inputs["L0_ffn_bc_b"], dtype=np.float32),
        np.asarray(inputs["L0_ffn_agg_b"], dtype=np.float32),
    ]).reshape(1, -1)

    xlast_np = np.ascontiguousarray(x[:, S - 1, :])
    nonce_np = (np.random.randint(1, 2 ** 27, size=(1, 8)) * 8 +
                np.arange(8).reshape(1, 8)).astype(np.int32)

    in_maps = []
    for c in range(8):
        b, half = c // 2, c % 2
        xsT_np = np.ascontiguousarray(x[b, half * TLOC:(half + 1) * TLOC, :].T)
        pos = np.arange(half * TLOC, (half + 1) * TLOC, dtype=np.float64)
        fr = pos[None, :] * inv_freq[:, None]
        cosf = np.cos(fr)
        sinf = np.sin(fr)
        cosF_np = np.ascontiguousarray(
            np.concatenate([cosf, cosf], axis=0).astype(np.float32))
        sinS_np = np.ascontiguousarray(
            np.concatenate([-sinf, sinf], axis=0).astype(np.float32))
        bsel = np.zeros((128, 4), dtype=np.float32)
        bsel[:, b] = 1.0
        sel = np.full((128, 1), 1.0 if half == 1 else 0.0, dtype=np.float32)
        hh = c % 2
        rvc = np.concatenate(
            [rv, d1b[hh * 4096:(hh + 1) * 4096].reshape(1, -1)], axis=1)
        xinfo_np = np.array([[hh * 136, hh, 1 - hh] + [0] * 21],
                            dtype=np.int32)
        in_maps.append({
            "xsT": xsT_np,
            "xlast": xlast_np,
            "wqT": wqT, "wkT": wkT, "wvT": wvT, "woT": woT, "gw": gwv,
            "aggw": np.ascontiguousarray(aggw_f[:, hh * 1024:(hh + 1) * 1024]),
            "bcw": np.ascontiguousarray(bcw_f[:, hh * 1024:(hh + 1) * 1024]),
            "d1h": np.ascontiguousarray(d1w[:, hh * 4096:(hh + 1) * 4096]),
            "d2h": np.ascontiguousarray(d2w[hh * 4096:(hh + 1) * 4096, :]),
            "xinfo": xinfo_np,
            "nonces": nonce_np,
            "rvec": np.ascontiguousarray(np.tile(rvc, (4, 1))),
            "pvecs": pv,
            "cosF": cosF_np, "sinS": sinS_np,
            "diagT": diag_t, "qdecT": qdec_t, "kdecC": kdec_c,
            "identb": ident_b, "identf": ident_f,
            "onesb": ones_b, "onesf": ones_f,
            "bselP": bsel, "selS": sel,
            "consts": np.tile(np.array([[EPS1, EPS2]], dtype=np.float32), (128, 1)),
        })
    return in_maps


def kernel(**inputs):
    in_maps = _prep_inmaps(inputs)
    nc = build_nc()
    res = run_bass_kernel_spmd(nc, in_maps, core_ids=list(range(8)))
    out = np.empty((B, S, D), dtype=np.float32)
    for c in range(8):
        b, half = c // 2, c % 2
        out[b, half * TLOC:(half + 1) * TLOC, :] = res.results[c]["outT"].T
    return out


# version 06
# speedup vs baseline: 3.0885x; 3.0885x over previous
import math
import numpy as np
import ml_dtypes
import dataclasses
import concourse.bass as bass
from concourse.expressions import RuntimeValue
import concourse.mybir as mybir
from concourse.bass_utils import run_bass_kernel_spmd

F32 = mybir.dt.float32
BF16 = mybir.dt.bfloat16
AF = mybir.ActivationFunctionType
ALU = mybir.AluOpType
AX = mybir.AxisListType
bfl = ml_dtypes.bfloat16

B, S, D = 4, 2048, 2048
H, HKV, DH = 16, 4, 128
TLOC = 1024
NDT = 16
NCH = 8
CH = 128
HD = 8192
HDS = HD // 8
EPS1 = 1e-5
EPS2 = 1e-6

XA_K = 0
XA_KT = 2048
XA_V = 4096
XA_G0 = 6144
XA_G1 = 7168
XA_ST = 8192
XA_E = 10240
XA_F = 12288
XA_SP = 14336

lam_np = np.exp2(-8.0 * np.arange(1, H + 1, dtype=np.float64) / H)
blk_f = [float(np.exp(-lam_np[h] * CH)) for h in range(H)]


def build_nc():
    nc = bass.Bass("TRN2", target_bir_lowering=False, debug=False,
                   detect_race_conditions=False, enable_asserts=False)
    xsT = nc.dram_tensor("xsT", [D, TLOC], F32, kind="ExternalInput").ap()
    xlast = nc.dram_tensor("xlast", [4, D], F32, kind="ExternalInput").ap()
    wqT = nc.dram_tensor("wqT", [D, D], F32, kind="ExternalInput").ap()
    wkT = nc.dram_tensor("wkT", [D, 512], F32, kind="ExternalInput").ap()
    wvT = nc.dram_tensor("wvT", [D, 512], F32, kind="ExternalInput").ap()
    woT = nc.dram_tensor("woT", [D, D], F32, kind="ExternalInput").ap()
    gw = nc.dram_tensor("gw", [D, D], F32, kind="ExternalInput").ap()
    aggw = nc.dram_tensor("aggw", [D, 1024], F32, kind="ExternalInput").ap()
    bcw = nc.dram_tensor("bcw", [D, 1024], F32, kind="ExternalInput").ap()
    d1h = nc.dram_tensor("d1h", [D, 4096], F32, kind="ExternalInput").ap()
    d2h = nc.dram_tensor("d2h", [4096, D], F32, kind="ExternalInput").ap()
    rvec = nc.dram_tensor("rvec", [4, 7 * D + 4096], F32, kind="ExternalInput").ap()
    xinfo = nc.dram_tensor("xinfo", [1, 24], mybir.dt.int32, kind="ExternalInput").ap()
    nonces = nc.dram_tensor("nonces", [1, 8], mybir.dt.int32, kind="ExternalInput").ap()
    pvecs = nc.dram_tensor("pvecs", [128, 48], F32, kind="ExternalInput").ap()
    cosF = nc.dram_tensor("cosF", [128, 1024], F32, kind="ExternalInput").ap()
    sinS = nc.dram_tensor("sinS", [128, 1024], F32, kind="ExternalInput").ap()
    diagT = nc.dram_tensor("diagT", [128, 2048], F32, kind="ExternalInput").ap()
    qdecT = nc.dram_tensor("qdecT", [128, 2048], F32, kind="ExternalInput").ap()
    kdecC = nc.dram_tensor("kdecC", [128, 16], F32, kind="ExternalInput").ap()
    identb = nc.dram_tensor("identb", [128, 128], BF16, kind="ExternalInput").ap()
    identf = nc.dram_tensor("identf", [128, 4], F32, kind="ExternalInput").ap()
    onesb = nc.dram_tensor("onesb", [128, 1], BF16, kind="ExternalInput").ap()
    onesf = nc.dram_tensor("onesf", [128, 128], F32, kind="ExternalInput").ap()
    bselP = nc.dram_tensor("bselP", [128, 4], F32, kind="ExternalInput").ap()
    selS = nc.dram_tensor("selS", [128, 1], F32, kind="ExternalInput").ap()
    consts = nc.dram_tensor("consts", [128, 2], F32, kind="ExternalInput").ap()
    outT = nc.dram_tensor("outT", [D, TLOC], F32, kind="ExternalOutput").ap()
    SHB = nc.dram_tensor("SHB", [272, 2048], F32, addr_space="Shared")
    SHF = nc.dram_tensor("SHF", [2, 2], mybir.dt.int32, addr_space="Shared")
    dc1 = nc.dram_tensor("dc1", [D, 4096], BF16)
    dc2 = nc.dram_tensor("dc2", [4096, D], BF16)

    from contextlib import ExitStack
    with ExitStack() as _es:
        XA = _es.enter_context(nc.sbuf_tensor("XA", [128, 16384], F32))
        B1 = _es.enter_context(nc.sbuf_tensor("B1", [128, 16384], BF16))
        SO = _es.enter_context(nc.sbuf_tensor("SO", [128, 16384], BF16))
        XN = _es.enter_context(nc.sbuf_tensor("XN", [128, 16384], BF16))
        WS = _es.enter_context(nc.sbuf_tensor("WS", [128, 4096], BF16))
        ROWS = _es.enter_context(nc.sbuf_tensor("ROWS", [128, 4608], F32))
        TH = _es.enter_context(nc.sbuf_tensor("TH", [128, 4104], F32))
        identb_s = _es.enter_context(nc.sbuf_tensor("identb_s", [128, 128], BF16))
        identf_s = _es.enter_context(nc.sbuf_tensor("identf_s", [128, 4], F32))
        onesb_s = _es.enter_context(nc.sbuf_tensor("onesb_s", [128, 1], BF16))
        onesf_s = _es.enter_context(nc.sbuf_tensor("onesf_s", [128, 128], F32))
        pvecs_s = _es.enter_context(nc.sbuf_tensor("pvecs_s", [128, 48], F32))
        bselP_s = _es.enter_context(nc.sbuf_tensor("bselP_s", [128, 4], F32))
        selS_s = _es.enter_context(nc.sbuf_tensor("selS_s", [128, 1], F32))
        kdecC_s = _es.enter_context(nc.sbuf_tensor("kdecC_s", [128, 16], F32))
        xnlT_s = _es.enter_context(nc.sbuf_tensor("xnlT_s", [128, 64], BF16))
        tT_s = _es.enter_context(nc.sbuf_tensor("tT_s", [128, 64], BF16))
        uT_s = _es.enter_context(nc.sbuf_tensor("uT_s", [128, 128], BF16))
        bcT_s = _es.enter_context(nc.sbuf_tensor("bcT_s", [128, 64], F32))
        bcP_s = _es.enter_context(nc.sbuf_tensor("bcP_s", [128, 16], F32))
        consts_s = _es.enter_context(nc.sbuf_tensor("consts_s", [128, 2], F32))
        xinfo_s = _es.enter_context(nc.sbuf_tensor("xinfo_s", [1, 24], mybir.dt.int32))
        nonce_s = _es.enter_context(nc.sbuf_tensor("nonce_s", [1, 8], mybir.dt.int32))
        flag_s = _es.enter_context(nc.sbuf_tensor("flag_s", [1, 1], mybir.dt.int32))
        PA = _es.enter_context(nc.psum_tensor("PA", [128, 1536], F32))
        PB = _es.enter_context(nc.psum_tensor("PB", [128, 2048], F32))
        PT = _es.enter_context(nc.psum_tensor("PT", [128, 1024], BF16))
        semX = _es.enter_context(nc.semaphore("sX"))
        semW = _es.enter_context(nc.semaphore("sW"))
        semR = _es.enter_context(nc.semaphore("sR"))
        semT = _es.enter_context(nc.semaphore("sT"))
        semV = _es.enter_context(nc.semaphore("sV"))
        semS = _es.enter_context(nc.semaphore("sS"))
        semG = _es.enter_context(nc.semaphore("sG"))
        semC = _es.enter_context(nc.semaphore("sC"))
        semBe = _es.enter_context(nc.semaphore("sBe"))
        semBo = _es.enter_context(nc.semaphore("sBo"))
        semWe = _es.enter_context(nc.semaphore("sWe"))
        semWo = _es.enter_context(nc.semaphore("sWo"))
        semROW = _es.enter_context(nc.semaphore("sROW"))
        semCD = _es.enter_context(nc.semaphore("sCD"))
        semSTG = _es.enter_context(nc.semaphore("sSTG"))
        semG1 = _es.enter_context(nc.semaphore("sG1"))
        semG2 = _es.enter_context(nc.semaphore("sG2"))
        semG3 = _es.enter_context(nc.semaphore("sG3"))
        semG4 = _es.enter_context(nc.semaphore("sG4"))
        semG5 = _es.enter_context(nc.semaphore("sG5"))
        semOe = _es.enter_context(nc.semaphore("sOe"))
        semOo = _es.enter_context(nc.semaphore("sOo"))
        semCS = _es.enter_context(nc.semaphore("sCS"))
        semPe = _es.enter_context(nc.semaphore("sPe"))
        semPo = _es.enter_context(nc.semaphore("sPo"))
        semRe = _es.enter_context(nc.semaphore("sRe"))
        semRo = _es.enter_context(nc.semaphore("sRo"))
        semWX = _es.enter_context(nc.semaphore("sWX"))
        semPF = _es.enter_context(nc.semaphore("sPF"))
        semSH = _es.enter_context(nc.semaphore("sSH"))
        semTe = _es.enter_context(nc.semaphore("sTe"))
        semTo = _es.enter_context(nc.semaphore("sTo"))
        semCe = _es.enter_context(nc.semaphore("sCe"))
        semCo = _es.enter_context(nc.semaphore("sCo"))
        semtT = _es.enter_context(nc.semaphore("stT"))
        semtV = _es.enter_context(nc.semaphore("stV"))
        semtS = _es.enter_context(nc.semaphore("stS"))
        semDG = _es.enter_context(nc.semaphore("sDG"))
        blk = _es.enter_context(nc.Block())
        cnt = dict(X=0, W=0, R=0, T=0, V=0, S=0, G=0, C=0,
                   Be=0, Bo=0, We=0, Wo=0, ROW=0, CD=0, STG=0,
                   G1=0, G2=0, G3=0, G4=0, G5=0, Oe=0, Oo=0, CS=0,
                   Pe=0, Po=0, Re=0, Ro=0,
                   WX=0, PF=0, SH=0, Te=0, To=0, Ce=0, Co=0,
                   tT=0, tV=0, tS=0, DG=0)
        sems = dict(X=semX, W=semW, R=semR, T=semT, V=semV, S=semS, G=semG, C=semC,
                    Be=semBe, Bo=semBo, We=semWe, Wo=semWo, ROW=semROW, CD=semCD,
                    STG=semSTG, G1=semG1, G2=semG2, G3=semG3, G4=semG4, G5=semG5,
                    Oe=semOe, Oo=semOo, CS=semCS,
                    Pe=semPe, Po=semPo, Re=semRe, Ro=semRo,
                    WX=semWX, PF=semPF, SH=semSH, Te=semTe, To=semTo,
                    Ce=semCe, Co=semCo,
                    tT=semtT, tV=semtV, tS=semtS, DG=semDG)

        def inc(key, inst, k=1):
            inst.then_inc(sems[key], k)
            cnt[key] += k
            return cnt[key]

        def dinc(key, inst):
            return inc(key, inst, 16)

        def wt(eng, key, val=None):
            eng.wait_ge(sems[key], cnt[key] if val is None else val)

        XAa = XA.ap()
        XAb = XAa.bitcast(BF16)
        B1a = B1.ap()
        B1f = B1a.bitcast(F32)
        SOa = SO.ap()
        XNa = XN.ap()
        WSa = WS.ap()
        ROWSa = ROWS.ap()
        THa = TH.ap()
        PAa = PA.ap()
        PBa = PB.ap()
        PTa = PT.ap()

        def xab(c0f32, nbf):
            return XAb[:, 2 * c0f32:2 * c0f32 + nbf]

        def row(p, j):
            slot = {(0, 0): 0, (32, 0): 1, (0, 1): 2, (64, 0): 3}[(p, j)]
            return ROWSa[0:1, slot * 1024:(slot + 1) * 1024]

        th_main = THa[0:4, 0:2048]
        th_t = THa[0:4, 2048:4096]
        th_u = THa[0:4, 2048:3072]
        th_d = th_t

        def th_scr(j):
            return THa[0:4, 4096 + j:4096 + j + 1]

        psum_last = {}

        def psum_wait(eng, *keys):
            for k in keys:
                if k in psum_last:
                    kk, v = psum_last[k]
                    eng.wait_ge(sems[kk], v)

        def psum_used(key, semname):
            psum_last[key] = (semname, cnt[semname])

        A0 = PAa[:, 0:512]
        A1 = PAa[:, 512:1024]
        A2 = PAa[:, 1024:1536]

        RV_ING, RV_INB, RV_PNG, RV_PNB, RV_D2B, RV_BCB, RV_AGB, RV_D1B = (
            0, D, 2 * D, 3 * D, 4 * D, 5 * D, 6 * D, 7 * D)

        for dst, src in [
            (identb_s[:], identb), (identf_s[:], identf), (onesb_s[:], onesb),
            (onesf_s[:], onesf), (pvecs_s[:], pvecs), (bselP_s[:], bselP),
            (selS_s[:], selS), (kdecC_s[:], kdecC),
            (XAa[:, XA_F:XA_F + 1024], cosF),
            (XAa[:, XA_F + 1024:XA_F + 2048], sinS),
            (THa[0:4, 0:2048], xlast),
            (consts_s[:], consts),
            (xinfo_s[:], xinfo),
            (nonce_s[:], nonces),
        ]:
            dinc("X", nc.sync.dma_start(out=dst, in_=src))
        base_loads = cnt["X"]
        nc.const_aps.register(F32, EPS1, consts_s[:, 0:1])
        nc.const_aps.register(F32, EPS2, consts_s[:, 1:2])
        for eng in (nc.tensor, nc.vector, nc.scalar, nc.gpsimd):
            eng.wait_ge(semX, base_loads)
        cosF_s = XAa[:, XA_F:XA_F + 1024]
        sinS_s = XAa[:, XA_F + 1024:XA_F + 2048]

        stats_mm_done = [4 * (dt + 1) for dt in range(NDT)]
        xbf_done = []
        for dt in range(NDT):
            slot = dt % 2
            bkey = "Be" if slot == 0 else "Bo"
            if dt >= 2:
                nc.gpsimd.wait_ge(semT, stats_mm_done[dt - 2])
            xbf_done.append((bkey, dinc(bkey, nc.gpsimd.dma_start(
                out=xab(XA_E + slot * 512, 1024),
                in_=xsT[dt * 128:(dt + 1) * 128, :]))))
        sq_done = []
        for dt in range(NDT):
            slot = dt % 2
            nc.scalar.wait_ge(sems[xbf_done[dt][0]], xbf_done[dt][1])
            if dt >= 2:
                nc.scalar.wait_ge(semT, stats_mm_done[dt - 2])
            sq_done.append(inc("S", nc.scalar.activation(
                xab(XA_E + 1024 + slot * 512, 1024),
                xab(XA_E + slot * 512, 1024), AF.Square)))
        for dt in range(NDT):
            slot = dt % 2
            nc.tensor.wait_ge(sems[xbf_done[dt][0]], xbf_done[dt][1])
            nc.tensor.wait_ge(semS, sq_done[dt])
            st, sp = (dt == 0), (dt == NDT - 1)
            xb = xab(XA_E + slot * 512, 1024)
            xq = xab(XA_E + 1024 + slot * 512, 1024)
            nc.tensor.matmul(PAa[0:1, 0:512], onesb_s[:], xb[:, 0:512], start=st, stop=sp)
            nc.tensor.matmul(PAa[0:1, 512:1024], onesb_s[:], xb[:, 512:1024], start=st, stop=sp)
            nc.tensor.matmul(PAa[0:1, 1024:1536], onesb_s[:], xq[:, 0:512], start=st, stop=sp)
            inc("T", nc.tensor.matmul(PBa[0:1, 0:512], onesb_s[:], xq[:, 512:1024],
                                      start=st, stop=sp), 4)
        assert cnt["T"] == 4 * NDT
        wt(nc.vector, "T")
        nc.vector.tensor_copy(row(0, 0)[:, 0:512], PAa[0:1, 0:512])
        nc.vector.tensor_copy(row(0, 0)[:, 512:1024], PAa[0:1, 512:1024])
        nc.vector.tensor_copy(row(32, 0)[:, 0:512], PAa[0:1, 1024:1536])
        inc("V", nc.vector.tensor_copy(row(32, 0)[:, 512:1024], PBa[0:1, 0:512]))
        for k in ("A0", "A1", "A2", "B0"):
            psum_used(k, "V")

        def ln_rows(srow, qrow, rrow, mrow, wrow, eps):
            wt(nc.scalar, "V")
            nc.scalar.mul(mrow, srow, 1.0 / D)
            inc("S", nc.scalar.mul(wrow, qrow, 1.0 / D))
            wt(nc.scalar, "S")
            inc("S", nc.scalar.activation(rrow, mrow, AF.Square))
            wt(nc.vector, "S")
            inc("V", nc.vector.tensor_tensor(wrow, wrow, rrow, ALU.subtract))
            wt(nc.scalar, "V")
            inc("S", nc.scalar.activation(rrow, wrow, AF.Ln, bias=eps))
            wt(nc.scalar, "S")
            inc("S", nc.scalar.activation(rrow, rrow, AF.Exp, scale=-0.5))
            wt(nc.vector, "S")
            inc("V", nc.vector.tensor_tensor(mrow, mrow, rrow, ALU.mult))
            wt(nc.scalar, "V")
            inc("S", nc.scalar.mul(mrow, mrow, -1.0))
            wt(nc.vector, "S")

        ln_rows(row(0, 0), row(32, 0), row(0, 1), row(64, 0), row(32, 0), EPS1)

        def bcast_row(r, dst, pk1, pk2, psl1, psl2):
            bp = r.base_partition()
            ones1 = onesf_s.ap()[bp:bp + 1, :]
            wt(nc.tensor, "S")
            wt(nc.tensor, "V")
            psum_wait(nc.tensor, pk1, pk2)
            nc.tensor.matmul(psl1, ones1, r[:, 0:512], start=True, stop=True)
            inc("T", nc.tensor.matmul(psl2, ones1, r[:, 512:1024], start=True, stop=True))
            wt(nc.vector, "T")
            nc.vector.tensor_copy(dst[:, 0:512], psl1)
            inc("V", nc.vector.tensor_copy(dst[:, 512:1024], psl2))
            psum_used(pk1, "V")
            psum_used(pk2, "V")

        bcast_row(row(0, 1), B1f[:, 5120:6144], "A0", "A1", A0, A1)
        bcast_row(row(64, 0), B1f[:, 6144:7168], "A0", "A1", A0, A1)
        rstd1bc = B1f[:, 5120:6144]
        nmr1bc = B1f[:, 6144:7168]
        ln1_bc = cnt["V"]

        xn_done = []
        xld_done = []
        for dt in range(NDT):
            slot = dt % 2
            if dt >= 2:
                nc.scalar.wait_ge(semV, ln1_bc + 2 * (dt - 1))
            else:
                nc.scalar.wait_ge(semT, stats_mm_done[NDT - 1])
                nc.scalar.wait_ge(semS, sq_done[NDT - 1])
            bkey = "Pe" if slot == 0 else "Po"
            xld_done.append((bkey, dinc(bkey, nc.scalar.dma_start(
                out=XAa[:, XA_E + slot * 1024:XA_E + (slot + 1) * 1024],
                in_=xsT[dt * 128:(dt + 1) * 128, :]))))
            nc.vector.wait_ge(sems[xld_done[dt][0]], xld_done[dt][1])
            xt = XAa[:, XA_E + slot * 1024:XA_E + (slot + 1) * 1024]
            inc("V", nc.vector.tensor_tensor(xt, xt, rstd1bc, ALU.mult))
            wt(nc.vector, "V")
            xn_done.append(inc("V", nc.vector.tensor_tensor(
                XNa[:, dt * 1024:(dt + 1) * 1024], xt, nmr1bc, ALU.add)))
        xn_all = cnt["V"]

        SHBa = SHB.ap()
        SHFa = SHF.ap()
        E_AGG, E_STATE, E_S1, E_S2, E_S3, E_S4, E_BC = range(7)

        nc.sync.wait_ge(semX, base_loads)
        mb_r = nc.sync.alloc_register("xr_mb")
        nc.sync.reg_load(mb_r, xinfo_s[0:1, 0:1])
        myp_r = nc.sync.alloc_register("xr_myp")
        nc.sync.reg_load(myp_r, xinfo_s[0:1, 1:2])
        pp_r = nc.sync.alloc_register("xr_pp")
        nc.sync.reg_load(pp_r, xinfo_s[0:1, 2:3])
        mb = RuntimeValue(mb_r, 0, 136)
        myp = RuntimeValue(myp_r, 0, 1)
        pp = RuntimeValue(pp_r, 0, 1)
        poll_nd = nc.sync.alloc_register("poll_nd")
        poll_v = nc.sync.alloc_register("poll_v")
        poll_n = nc.sync.alloc_register("poll_n")
        ap_w = {
            (0, 1024): SHBa[bass.ds(mb, 4), 0:1024],
            (0, 2048): SHBa[bass.ds(mb, 4), 0:2048],
            (1, 1024): SHBa[bass.ds(mb + 4, 4), 0:1024],
            (1, 2048): SHBa[bass.ds(mb + 4, 4), 0:2048],
            "st": SHBa[bass.ds(mb + 8, 128), 0:2048],
        }
        ap_fw = [SHFa[bass.ds(myp, 1), 0:1], SHFa[bass.ds(myp, 1), 1:2]]
        ap_fr = [SHFa[bass.ds(pp, 1), 0:1], SHFa[bass.ds(pp, 1), 1:2]]
        EXC = {"n": 0}

        def ex_next():
            i = EXC["n"]
            EXC["n"] += 1
            return i

        def sh_write(src_ap, width, exi):
            sl = exi % 2
            wkey = "st" if width == "st" else (sl, width)
            dinc("WX", nc.sync.dma_start(out=ap_w[wkey], in_=src_ap))
            nc.sync.wait_ge(semWX, cnt["WX"])
            dinc("WX", nc.sync.dma_start(
                out=ap_fw[sl], in_=nonce_s[0:1, exi:exi + 1]))
            nc.sync.wait_ge(semWX, cnt["WX"])

        def sh_poll(exi):
            sl = exi % 2
            nc.sync.reg_mov(poll_nd, 1)
            with nc.sync.While(poll_nd):
                nc.sync.sem_clear(semPF)
                nc.sync.dma_start(out=flag_s[:],
                                  in_=ap_fr[sl]).then_inc(semPF, 16)
                nc.sync.wait_ge(semPF, 16)
                nc.sync.reg_load(poll_v, flag_s[0:1, 0:1])
                nc.sync.reg_load(poll_n, nonce_s[0:1, exi:exi + 1])
                nc.sync.reg_alu(poll_nd, poll_v, poll_n, ALU.not_equal)

        def sh_read(dst_ap, plane, r0, nrows, width):
            row0 = plane * 136 + r0
            m = dinc("SH", nc.sync.dma_start(
                out=dst_ap, in_=SHBa[row0:row0 + nrows, 0:width]))
            nc.sync.wait_ge(semSH, m)
            return m

        def th_ln(src, dst, eps):
            wt(nc.vector, "tS")
            wt(nc.vector, "tV")
            inc("tV", nc.vector.tensor_reduce(th_scr(0), src, AX.X, ALU.add))
            wt(nc.scalar, "tV")
            wt(nc.scalar, "tT")
            inc("tS", nc.scalar.activation(dst, src, AF.Square))
            wt(nc.vector, "tS")
            inc("tV", nc.vector.tensor_reduce(th_scr(1), dst, AX.X, ALU.add))
            yield
            wt(nc.scalar, "tV")
            nc.scalar.mul(th_scr(2), th_scr(0), -1.0 / D)
            nc.scalar.mul(th_scr(1), th_scr(1), 1.0 / D)
            inc("tS", nc.scalar.mul(th_scr(3), th_scr(0), 1.0 / D))
            wt(nc.scalar, "tS")
            inc("tS", nc.scalar.activation(th_scr(0), th_scr(3), AF.Square))
            wt(nc.vector, "tS")
            inc("tV", nc.vector.tensor_tensor(th_scr(1), th_scr(1), th_scr(0), ALU.subtract))
            wt(nc.scalar, "tV")
            inc("tS", nc.scalar.activation(th_scr(1), th_scr(1), AF.Ln, bias=eps))
            wt(nc.scalar, "tS")
            inc("tS", nc.scalar.activation(th_scr(1), th_scr(1), AF.Exp, scale=-0.5))
            wt(nc.vector, "tS")
            inc("tV", nc.vector.tensor_scalar(dst, src, th_scr(2), th_scr(1), ALU.add, ALU.mult))
            yield

        rrow_buf = ROWSa[0:4, 4096:4608]

        def th_loadrow(off, n):
            wt(nc.sync, "tV")
            return dinc("ROW", nc.sync.dma_start(out=rrow_buf[:, 0:n], in_=rvec[:, off:off + n]))

        def th_rowapply(dst, off, n, op):
            for q in range((n + 511) // 512):
                c0 = q * 512
                w = min(n, c0 + 512) - c0
                ldm = th_loadrow(off + c0, w)
                nc.vector.wait_ge(semROW, ldm)
                wt(nc.vector, "tV")
                inc("tV", nc.vector.tensor_tensor(dst[:, c0:c0 + w], dst[:, c0:c0 + w],
                                                  rrow_buf[:, 0:w], op))
                yield

        def th_transpose(src4, dstT, ncols):
            wt(nc.tensor, "tV")
            wt(nc.tensor, "tS")
            psum_wait(nc.tensor, "B3")
            bp = src4.base_partition()
            idsl = identf_s.ap()[bp:bp + 4, bp:bp + 4]
            for j in range(ncols):
                mmi = nc.tensor.transpose(PBa[:, 1536 + j * 4:1536 + (j + 1) * 4],
                                          src4[:, j * 128:(j + 1) * 128], idsl)
                if j == ncols - 1:
                    inc("tT", mmi)
            wt(nc.vector, "tT")
            inc("tV", nc.vector.tensor_copy(dstT[:, 0:ncols * 4], PBa[:, 1536:1536 + ncols * 4]))
            psum_used("B3", "tV")
            yield

        def ts_slot(i):
            return xab(XA_SP + i * 1024, 2048)

        THS = dict(slot_mm=[0, 0], slot_wb=[0, 0], nslot=0, dg_prev=0)

        def th_chunk(src_fp32, src_bf16, cache_dst, step, nmm_fn):
            i = THS["nslot"] % 2
            THS["nslot"] += 1
            wkey = "Te" if i == 0 else "To"
            ckey = "Ce" if i == 0 else "Co"
            if THS["slot_mm"][i]:
                nc.gpsimd.wait_ge(semtT, THS["slot_mm"][i])
            if THS["slot_wb"][i]:
                nc.gpsimd.wait_ge(sems[ckey], THS["slot_wb"][i])
            if step == 0 and src_fp32 is not None:
                m = dinc(wkey, nc.gpsimd.dma_start(out=ts_slot(i), in_=src_fp32))
            else:
                m = dinc(wkey, nc.gpsimd.dma_start(out=ts_slot(i), in_=src_bf16))
            nc.tensor.wait_ge(sems[wkey], m)
            nmm_fn(ts_slot(i))
            THS["slot_mm"][i] = cnt["tT"]
            if step == 0 and cache_dst is not None:
                nc.scalar.wait_ge(sems[wkey], m)
                THS["slot_wb"][i] = dinc(ckey, nc.scalar.dma_start(
                    out=cache_dst, in_=ts_slot(i)))

        TB = (0, 32, 64, 0)

        def acc_ap(q):
            cc = 1024 if q < 3 else 1536
            return PBa[TB[q]:TB[q] + 4, cc:cc + 512]

        def th_gather(dst_c0):
            wt(nc.vector, "tT")
            if THS["dg_prev"]:
                nc.vector.wait_ge(semDG, THS["dg_prev"])
            dst0 = THa[0:4, dst_c0:dst_c0 + 512]
            dst0 = dataclasses.replace(dst0, ap=[dst0.ap[0], [1536, 2], [1, 512]])
            src0 = PBa[0:4, 1024:1536]
            src0 = dataclasses.replace(src0, ap=[src0.ap[0], [512, 2], [1, 512]])
            nc.vector.tensor_copy(dst0, src0)
            nc.vector.tensor_copy(THa[32:36, 0:512], PBa[32:36, 1024:1536])
            inc("tV", nc.vector.tensor_copy(THa[64:68, 0:512], PBa[64:68, 1024:1536]))
            psum_used("B2", "tV")
            psum_used("B3", "tV")
            wt(nc.sync, "tV")
            dinc("DG", nc.sync.dma_start(
                out=THa[0:4, dst_c0 + 512:dst_c0 + 1024], in_=THa[32:36, 0:512]))
            dinc("DG", nc.sync.dma_start(
                out=THa[0:4, dst_c0 + 1024:dst_c0 + 1536], in_=THa[64:68, 0:512]))
            THS["dg_prev"] = cnt["DG"]
            nc.sync.wait_ge(semDG, cnt["DG"])

        def th_pre_gen():
            nc.vector.wait_ge(semX, base_loads)
            yield from th_ln(th_main, th_t, EPS1)
            yield from th_ln(th_t, th_main, EPS2)
            yield from th_rowapply(th_main, RV_ING, 2048, ALU.mult)
            yield from th_rowapply(th_main, RV_INB, 2048, ALU.add)
            yield from th_transpose(th_main, xnlT_s.ap(), 16)
            psum_wait(nc.tensor, "B2")
            wt(nc.tensor, "tV")
            for c8 in range(8):
                def mms(sl, c8=c8):
                    for k in range(16):
                        mmi = nc.tensor.matmul(
                            PBa[0:4, 1024 + c8 * 128:1024 + (c8 + 1) * 128],
                            xnlT_s[:, k * 4:(k + 1) * 4],
                            sl[:, k * 128:(k + 1) * 128],
                            start=(k == 0), stop=(k == 15),
                            skip_group_check=True)
                        if k == 15:
                            inc("tT", mmi)
                th_chunk(aggw[:, c8 * 128:(c8 + 1) * 128].rearrange(
                    "(k p) n -> p k n", p=128), None, None, 0, mms)
                yield
            wt(nc.vector, "tT")
            inc("tV", nc.vector.tensor_copy(th_d[:, 0:1024], PBa[0:4, 1024:2048]))
            psum_used("B2", "tV")
            psum_used("B3", "tV")
            exi = ex_next()
            wt(nc.sync, "tV")
            sh_write(th_d[:, 0:1024], 1024, exi)
            sh_poll(exi)
            r0 = (exi % 2) * 4
            sh_read(th_main[:, 0:1024], 0, r0, 4, 1024)
            m = sh_read(th_main[:, 1024:2048], 1, r0, 4, 1024)
            yield
            nc.vector.wait_ge(semSH, m)
            inc("tV", nc.vector.tensor_copy(th_main, th_main))
            yield from th_rowapply(th_main, RV_AGB, 2048, ALU.add)

        def th_step_gen(step):
            if step == 1:
                THS["ce_tot"] = cnt["Ce"]
                THS["co_tot"] = cnt["Co"]
                nc.gpsimd.wait_ge(semCe, THS["ce_tot"])
                nc.gpsimd.wait_ge(semCo, THS["co_tot"])
            yield from th_ln(th_main, th_t, EPS2)
            yield from th_rowapply(th_t, RV_PNG, 2048, ALU.mult)
            yield from th_rowapply(th_t, RV_PNB, 2048, ALU.add)
            yield from th_transpose(th_t, tT_s.ap(), 16)
            for nh in range(2):
                psum_wait(nc.tensor, "B2", "B3")
                for k in range(16):
                    def mms(sl, k=k):
                        for q in range(4):
                            mmi = nc.tensor.matmul(
                                acc_ap(q), tT_s[:, k * 4:(k + 1) * 4],
                                sl[:, q * 512:(q + 1) * 512],
                                start=(k == 0), stop=(k == 15),
                                skip_group_check=True)
                            if q == 3:
                                inc("tT", mmi)
                    th_chunk(
                        d1h[k * 128:(k + 1) * 128, nh * 2048:(nh + 1) * 2048],
                        dc1.ap()[k * 128:(k + 1) * 128, nh * 2048:(nh + 1) * 2048],
                        dc1.ap()[k * 128:(k + 1) * 128, nh * 2048:(nh + 1) * 2048],
                        step, mms)
                    yield
                th_gather(2048)
                yield
                nc.vector.wait_ge(semDG, cnt["DG"])
                yield from th_rowapply(th_t, RV_D1B + nh * 2048, 2048, ALU.add)
                wt(nc.scalar, "tV")
                inc("tS", nc.scalar.activation(th_t, th_t, AF.Gelu))
                yield from th_transpose(th_t, uT_s.ap()[:, nh * 64:(nh + 1) * 64], 16)
            psum_wait(nc.tensor, "B2", "B3")
            for kt in range(32):
                def mms(sl, kt=kt):
                    for g in range(4):
                        mmi = nc.tensor.matmul(
                            acc_ap(g), uT_s[:, kt * 4:(kt + 1) * 4],
                            sl[:, g * 512:(g + 1) * 512],
                            start=(kt == 0), stop=(kt == 31),
                            skip_group_check=True)
                        if g == 3:
                            inc("tT", mmi)
                th_chunk(d2h[kt * 128:(kt + 1) * 128, :],
                         dc2.ap()[kt * 128:(kt + 1) * 128, :],
                         dc2.ap()[kt * 128:(kt + 1) * 128, :], step, mms)
                yield
            th_gather(2048)
            exi = ex_next()
            sh_write(th_d[:, 0:2048], 2048, exi)
            sh_poll(exi)
            r0 = (exi % 2) * 4
            m = sh_read(th_d[:, 0:2048], 0, r0, 4, 2048)
            yield
            nc.vector.wait_ge(semSH, m)
            wt(nc.vector, "tV")
            inc("tV", nc.vector.tensor_tensor(th_main, th_main, th_d[:, 0:2048], ALU.add))
            wt(nc.sync, "tV")
            m = sh_read(th_d[:, 0:2048], 1, r0, 4, 2048)
            nc.vector.wait_ge(semSH, m)
            wt(nc.vector, "tV")
            inc("tV", nc.vector.tensor_tensor(th_main, th_main, th_d[:, 0:2048], ALU.add))
            yield
            yield from th_rowapply(th_main, RV_D2B, 2048, ALU.add)

        def th_bc_gen():
            yield from th_transpose(th_main, tT_s.ap(), 16)
            psum_wait(nc.tensor, "B2")
            for c8 in range(8):
                def mms(sl, c8=c8):
                    for k in range(16):
                        mmi = nc.tensor.matmul(
                            PBa[0:4, 1024 + c8 * 128:1024 + (c8 + 1) * 128],
                            tT_s[:, k * 4:(k + 1) * 4],
                            sl[:, k * 128:(k + 1) * 128],
                            start=(k == 0), stop=(k == 15),
                            skip_group_check=True)
                        if k == 15:
                            inc("tT", mmi)
                th_chunk(bcw[:, c8 * 128:(c8 + 1) * 128].rearrange(
                    "(k p) n -> p k n", p=128), None, None, 0, mms)
                yield
            wt(nc.vector, "tT")
            inc("tV", nc.vector.tensor_copy(th_d[:, 0:1024], PBa[0:4, 1024:2048]))
            psum_used("B2", "tV")
            psum_used("B3", "tV")
            exi = ex_next()
            wt(nc.sync, "tV")
            sh_write(th_d[:, 0:1024], 1024, exi)
            sh_poll(exi)
            r0 = (exi % 2) * 4
            sh_read(th_d[:, 0:1024], 0, r0, 4, 1024)
            m = sh_read(th_d[:, 1024:2048], 1, r0, 4, 1024)
            yield
            nc.vector.wait_ge(semSH, m)
            inc("tV", nc.vector.tensor_copy(th_d, th_d))
            yield from th_rowapply(th_d, RV_BCB, 2048, ALU.add)
            yield from th_transpose(th_d, bcT_s.ap(), 16)
            bcT_ap = bcT_s.ap()

            def bcT_slice(b):
                sl = bcT_ap[:, b:b + 61]
                return dataclasses.replace(sl, ap=[sl.ap[0], [4, 16]])
            wt(nc.vector, "tV")
            inc("tV", nc.vector.tensor_scalar_mul(bcP_s[:], bcT_slice(0), bselP_s[:, 0:1]))
            for b4 in range(1, 4):
                wt(nc.vector, "tV")
                inc("tV", nc.vector.scalar_tensor_tensor(bcP_s[:], bcT_slice(b4),
                                                         bselP_s[:, b4:b4 + 1],
                                                         bcP_s[:], ALU.mult, ALU.add))
            TH_ST["bc_tV"] = cnt["tV"]
            TH_ST["end_tT"] = cnt["tT"]

        def th_all_gen():
            yield from th_pre_gen()
            for s in range(4):
                yield from th_step_gen(s)
            yield from th_bc_gen()

        TH_ST = {"done": False, "bc_tV": 0, "end_tT": 0}
        TH_IT = th_all_gen()

        def pump(n=1):
            if TH_ST["done"]:
                return
            try:
                for _ in range(n):
                    next(TH_IT)
            except StopIteration:
                TH_ST["done"] = True

        def pump_all():
            while not TH_ST["done"]:
                pump(64)

        wk_done = dinc("G4", nc.gpsimd.dma_start(
            out=SOa[:, 0:8192].rearrange("p (k n) -> p k n", k=16),
            in_=wkT.rearrange("(k p) n -> p k n", p=128)))
        wv_done = dinc("G5", nc.gpsimd.dma_start(
            out=SOa[:, 8192:16384].rearrange("p (k n) -> p k n", k=16),
            in_=wvT.rearrange("(k p) n -> p k n", p=128)))

        def proj_tok(wbase_b, dst_c0f32, wsem, wmark):
            evs = []
            for tt in range(NCH):
                pp = tt % 2
                psl, kname = (A0, "A0") if pp == 0 else (A1, "A1")
                nc.tensor.wait_ge(wsem, wmark)
                nc.tensor.wait_ge(semV, xn_all)
                psum_wait(nc.tensor, kname)
                for k in range(16):
                    st, sp = (k == 0), (k == 15)
                    mmi = nc.tensor.matmul(
                        psl, XNa[:, k * 1024 + tt * 128:k * 1024 + (tt + 1) * 128],
                        SOa[:, wbase_b + k * 512:wbase_b + (k + 1) * 512],
                        start=st, stop=sp)
                    if sp:
                        inc("T", mmi)
                wt(nc.scalar, "T")
                evs.append(inc("S", nc.scalar.copy(
                    xab(dst_c0f32 + tt * 256, 512), psl)))
                psum_used(kname, "S")
                pump(2)
            return evs

        k_evac = proj_tok(0, XA_K, semG4, wk_done)
        v_evac = proj_tok(8192, XA_V, semG5, wv_done)
        kv_T_end = cnt["T"]
        v_S_end = cnt["S"]

        def transpose_128(src_ap, dst_ap, pairs, wait_keys):
            for i0 in range(0, len(pairs), 8):
                batch = pairs[i0:i0 + 8]
                for eng_key, val in wait_keys:
                    nc.tensor.wait_ge(sems[eng_key], val)
                psum_wait(nc.tensor, "PT")
                for j, (so, do) in enumerate(batch):
                    mmi = nc.tensor.transpose(PTa[:, j * 128:(j + 1) * 128],
                                              src_ap[:, so:so + 128], identb_s[:])
                    if j == len(batch) - 1:
                        inc("T", mmi)
                wt(nc.scalar, "T")
                for j, (so, do) in enumerate(batch):
                    mmi = nc.scalar.copy(dst_ap[:, do:do + 128], PTa[:, j * 128:(j + 1) * 128])
                    if j == len(batch) - 1:
                        inc("S", mmi)
                psum_used("PT", "S")
                pump(1)

        kfwd = [(tt * 512 + kv * 128, kv * 1024 + tt * 128)
                for kv in range(HKV) for tt in range(NCH)]
        transpose_128(xab(XA_K, 4096), xab(XA_KT, 4096), kfwd,
                      [("S", v_S_end)])
        kT_raw_S = cnt["S"]

        def rope_featmaj(base_ap, nheads, swap_c0, a_c0):
            for h in range(nheads):
                tile = base_ap[:, h * 1024:(h + 1) * 1024]
                swp = xab(swap_c0, 1024)
                atmp = xab(a_c0, 1024)
                wt(nc.scalar, "S")
                wt(nc.scalar, "V")
                nc.scalar.copy(swp[0:64, :], tile[64:128, :])
                inc("S", nc.scalar.copy(swp[64:128, :], tile[0:64, :]))
                wt(nc.vector, "S")
                inc("V", nc.vector.tensor_tensor(atmp, tile, cosF_s, ALU.mult))
                wt(nc.vector, "V")
                inc("V", nc.vector.tensor_tensor(swp, swp, sinS_s, ALU.mult))
                wt(nc.vector, "V")
                inc("V", nc.vector.tensor_tensor(tile, atmp, swp, ALU.add))
                pump(1)

        rope_featmaj(xab(XA_KT, 4096), HKV, XA_E, XA_E + 512)
        kT_rope_V = cnt["V"]

        kbwd = [(kv * 1024 + tt * 128, tt * 512 + kv * 128)
                for kv in range(HKV) for tt in range(NCH)]
        transpose_128(xab(XA_KT, 4096), xab(XA_K, 4096), kbwd,
                      [("V", kT_rope_V)])
        k_rope_S = cnt["S"]

        state = XAa[:, XA_ST:XA_ST + 2048]
        nc.vector.wait_ge(semT, kv_T_end)
        so_z = inc("V", nc.vector.memset(SOa[:, 0:2048], 0.0))
        inc("V", nc.vector.memset(state, 0.0))
        state_V = cnt["V"]
        ktil = xab(XA_G0, 2048)
        for c in range(NCH):
            nc.vector.wait_ge(semS, k_rope_S)
            wt(nc.vector, "T")
            for h in range(H):
                kv = h // (H // HKV)
                mmi = nc.vector.tensor_scalar_mul(
                    ktil[:, h * 128:(h + 1) * 128],
                    XAb[:, 2 * XA_K + c * 512 + kv * 128: 2 * XA_K + c * 512 + (kv + 1) * 128],
                    kdecC_s[:, h:h + 1])
                if h == H - 1:
                    inc("V", mmi)
            ktil_V = cnt["V"]
            nc.tensor.wait_ge(semV, ktil_V)
            if c == 0:
                nc.tensor.wait_ge(semS, v_S_end)
            for half in range(2):
                psum_wait(nc.tensor, "B0", "B1")
                for j in range(8):
                    h = half * 8 + j
                    kv = h // (H // HKV)
                    mmi = nc.tensor.matmul(
                        PBa[:, j * 128:(j + 1) * 128],
                        ktil[:, h * 128:(h + 1) * 128],
                        XAb[:, 2 * XA_V + c * 512 + kv * 128: 2 * XA_V + c * 512 + (kv + 1) * 128],
                        start=True, stop=True)
                    if j == 7:
                        inc("T", mmi)
                U_T = cnt["T"]
                nc.vector.wait_ge(semT, U_T)
                if half == 0:
                    nc.vector.wait_ge(semV, state_V)
                    nc.vector.wait_ge(semS, cnt["S"])
                shalf = state[:, half * 1024:(half + 1) * 1024]
                for j in range(8):
                    h = half * 8 + j
                    mmi = nc.vector.tensor_scalar_mul(shalf[:, j * 128:(j + 1) * 128],
                                                      shalf[:, j * 128:(j + 1) * 128], blk_f[h])
                    if j == 7:
                        inc("V", mmi)
                wt(nc.vector, "V")
                inc("V", nc.vector.tensor_tensor(shalf, shalf, PBa[:, 0:1024], ALU.add))
                psum_used("B0", "V")
                psum_used("B1", "V")
            state_V = cnt["V"]
            if c < NCH - 1:
                wt(nc.scalar, "V")
                inc("S", nc.scalar.copy(SOa[:, (c + 1) * 2048:(c + 2) * 2048], state))
            pump(6)
        scan_S = cnt["S"]
        scan_V = cnt["V"]

        exi_st = ex_next()
        wt(nc.sync, "V")
        sh_write(state, "st", exi_st)
        sh_poll(exi_st)
        Cadj = XAa[:, XA_E:XA_E + 2048]
        cd_ld = sh_read(Cadj, 0, 8, 128, 2048)
        nc.vector.wait_ge(semSH, cd_ld)
        inc("V", nc.vector.tensor_scalar_mul(Cadj, Cadj, selS_s[:]))
        for c in range(NCH):
            wt(nc.vector, "V")
            if c == 0:
                nc.vector.wait_ge(semS, scan_S)
                nc.vector.wait_ge(semV, so_z)
            inc("V", nc.vector.tensor_tensor(SOa[:, c * 2048:(c + 1) * 2048],
                                             SOa[:, c * 2048:(c + 1) * 2048], Cadj, ALU.add))
            if c < NCH - 1:
                wt(nc.vector, "V")
                for h in range(H):
                    mmi = nc.vector.tensor_scalar_mul(Cadj[:, h * 128:(h + 1) * 128],
                                                      Cadj[:, h * 128:(h + 1) * 128], blk_f[h])
                    if h == H - 1:
                        inc("V", mmi)
            pump(2)
        adj_V = cnt["V"]

        wq_marks = []
        for i in range(NDT):
            slot = i % 2
            if i >= 2:
                nc.gpsimd.wait_ge(semT, wq_marks[i - 2][1])
            wkey = "We" if slot == 0 else "Wo"
            ld = dinc(wkey, nc.gpsimd.dma_start(
                out=WSa[:, slot * 2048:(slot + 1) * 2048].rearrange("p (k n) -> p k n", k=16),
                in_=wqT[:, i * 128:(i + 1) * 128].rearrange("(k p) n -> p k n", p=128)))
            wq_marks.append((ld, None))
            for th2 in range(2):
                pp = th2
                psl, kname = (A0, "A0") if pp == 0 else (A1, "A1")
                nc.tensor.wait_ge(sems[wkey], ld)
                if i == 0 and th2 == 0:
                    nc.tensor.wait_ge(semV, adj_V)
                psum_wait(nc.tensor, kname)
                for k in range(16):
                    st, sp = (k == 0), (k == 15)
                    mmi = nc.tensor.matmul(
                        psl, WSa[:, slot * 2048 + k * 128:slot * 2048 + (k + 1) * 128],
                        XNa[:, k * 1024 + th2 * 512:k * 1024 + (th2 + 1) * 512],
                        start=st, stop=sp)
                    if sp:
                        inc("T", mmi)
                wq_marks[i] = (ld, cnt["T"])
                wt(nc.scalar, "T")
                if i == 0 and th2 == 0:
                    nc.scalar.wait_ge(semV, xn_all)
                inc("S", nc.scalar.copy(B1a[:, i * 1024 + th2 * 512:i * 1024 + (th2 + 1) * 512], psl))
                psum_used(kname, "S")
                pump(2)
        q_proj_S = cnt["S"]
        nc.vector.wait_ge(semS, q_proj_S)
        rope_featmaj(B1a, H, XA_G0, XA_G0 + 512)
        q_rope_V = cnt["V"]

        nc.scalar.wait_ge(semV, cnt["V"])
        dg_ld = dinc("X", nc.scalar.dma_start(out=XAa[:, XA_K:XA_K + 2048], in_=diagT))
        qd_ld = dinc("X", nc.scalar.dma_start(out=XAa[:, XA_ST:XA_ST + 2048], in_=qdecT))
        diagT_s = XAa[:, XA_K:XA_K + 2048]
        qdecT_s = XAa[:, XA_ST:XA_ST + 2048]
        sc_tmp = xab(XA_G1, 1024)
        qt_tmp = xab(XA_G1 + 512, 1024)
        o_evac_S = None
        for c in range(NCH):
            for hg in range(2):
                nc.tensor.wait_ge(semS, k_rope_S)
                nc.tensor.wait_ge(semV, q_rope_V)
                psum_wait(nc.tensor, "A0", "A1")
                for j in range(8):
                    h = hg * 8 + j
                    kv = h // (H // HKV)
                    mmi = nc.tensor.matmul(
                        PAa[:, j * 128:(j + 1) * 128],
                        XAb[:, 2 * XA_KT + kv * 1024 + c * 128: 2 * XA_KT + kv * 1024 + (c + 1) * 128],
                        B1a[:, h * 1024 + c * 128:h * 1024 + (c + 1) * 128],
                        start=True, stop=True)
                    if j == 7:
                        inc("T", mmi)
                sc_T = cnt["T"]
                nc.vector.wait_ge(semT, sc_T)
                nc.vector.wait_ge(semX, qd_ld)
                inc("V", nc.vector.tensor_tensor(sc_tmp, PAa[:, 0:1024],
                                                 diagT_s[:, hg * 1024:(hg + 1) * 1024], ALU.mult))
                psum_used("A0", "V")
                psum_used("A1", "V")
                nc.vector.wait_ge(semX, qd_ld)
                qsl = B1a[:, hg * 8192 + c * 128:hg * 8192 + c * 128 + 128]
                qsl = dataclasses.replace(qsl, ap=[qsl.ap[0], [1024, 8], [1, 128]])
                dsl = qdecT_s[:, hg * 1024:hg * 1024 + 128]
                dsl = dataclasses.replace(dsl, ap=[dsl.ap[0], [128, 8], [1, 128]])
                inc("V", nc.vector.tensor_tensor(qt_tmp, qsl, dsl, ALU.mult))
                ev_V = cnt["V"]
                nc.tensor.wait_ge(semV, ev_V)
                psum_wait(nc.tensor, "B0", "B1")
                for j in range(8):
                    h = hg * 8 + j
                    kv = h // (H // HKV)
                    po = PBa[:, j * 128:(j + 1) * 128]
                    nc.tensor.matmul(
                        po,
                        XAb[:, 2 * XA_V + c * 512 + kv * 128: 2 * XA_V + c * 512 + (kv + 1) * 128],
                        sc_tmp[:, j * 128:(j + 1) * 128], start=True, stop=False)
                    mmi = nc.tensor.matmul(
                        po,
                        SOa[:, c * 2048 + h * 128:c * 2048 + (h + 1) * 128],
                        qt_tmp[:, j * 128:(j + 1) * 128], start=False, stop=True)
                    if j == 7:
                        inc("T", mmi)
                o_T = cnt["T"]
                nc.scalar.wait_ge(semT, o_T)
                if o_evac_S is None:
                    nc.scalar.wait_ge(semV, adj_V)
                o_evac_S = inc("S", nc.scalar.copy(
                    SOa[:, c * 2048 + hg * 1024:c * 2048 + (hg + 1) * 1024],
                    PBa[:, 0:1024]))
                psum_used("B0", "S")
                psum_used("B1", "S")
                pump(3)
        o_done_S = cnt["S"]
        pump_all()

        B_XRE = 0
        B_G = 1024
        B_XNT = 2048
        B_T1 = 3072
        B_OUT = 4096
        B_BC1 = 5120
        nc.vector.wait_ge(semS, o_done_S)
        nc.vector.wait_ge(semT, cnt["T"])
        bcast_row(row(0, 1), B1f[:, B_BC1:B_BC1 + 1024], "A0", "A1", A0, A1)
        bcast_row(row(64, 0), B1f[:, B_BC1 + 1024:B_BC1 + 2048], "A0", "A1", A0, A1)
        rstd1t = B1f[:, B_BC1:B_BC1 + 1024]
        nmr1t = B1f[:, B_BC1 + 1024:B_BC1 + 2048]
        tail_bc_V = cnt["V"]

        gate_marks = []
        pre_V = []
        ln2_mm = 0
        for i in range(NDT):
            slot = i % 2
            if i >= 2:
                nc.gpsimd.wait_ge(semT, gate_marks[i - 2])
            else:
                nc.gpsimd.wait_ge(semT, cnt["T"])
            wkey = "We" if slot == 0 else "Wo"
            gld = dinc(wkey, nc.gpsimd.dma_start(
                out=WSa[:, slot * 2048:(slot + 1) * 2048].rearrange("p (k n) -> p k n", k=16),
                in_=gw[:, i * 128:(i + 1) * 128].rearrange("(k p) n -> p k n", p=128)))
            for th2 in range(2):
                sslot = (2 * i + th2) % 2
                psl, kname = (A0, "A0") if sslot == 0 else (A1, "A1")
                if len(pre_V) >= 2:
                    nc.scalar.wait_ge(semV, pre_V[-2])
                bkey = "Re" if sslot == 0 else "Ro"
                xre = B1f[:, B_XRE + sslot * 512:B_XRE + (sslot + 1) * 512]
                xrm = dinc(bkey, nc.scalar.dma_start(
                    out=xre, in_=xsT[i * 128:(i + 1) * 128, th2 * 512:(th2 + 1) * 512]))
                nc.tensor.wait_ge(sems[wkey], gld)
                psum_wait(nc.tensor, kname)
                for k in range(16):
                    st, sp = (k == 0), (k == 15)
                    mmi = nc.tensor.matmul(
                        psl, WSa[:, slot * 2048 + k * 128:slot * 2048 + (k + 1) * 128],
                        XNa[:, k * 1024 + th2 * 512:k * 1024 + (th2 + 1) * 512],
                        start=st, stop=sp)
                    if sp:
                        inc("T", mmi)
                gate_marks_val = cnt["T"]
                wt(nc.scalar, "T")
                gt = B1f[:, B_G + sslot * 512:B_G + (sslot + 1) * 512]
                inc("S", nc.scalar.activation(gt, psl, AF.Sigmoid, bias=pvecs_s[:, i:i + 1]))
                psum_used(kname, "S")
                nc.vector.wait_ge(sems[bkey], xrm)
                nc.vector.wait_ge(semV, tail_bc_V)
                xnt = B1f[:, B_XNT + sslot * 512:B_XNT + (sslot + 1) * 512]
                inc("V", nc.vector.tensor_tensor(xnt, xre,
                                                 rstd1t[:, th2 * 512:(th2 + 1) * 512], ALU.mult))
                wt(nc.vector, "V")
                inc("V", nc.vector.tensor_tensor(xnt, xnt,
                                                 nmr1t[:, th2 * 512:(th2 + 1) * 512], ALU.add))
                wt(nc.vector, "S")
                wt(nc.vector, "V")
                if i == 0 and th2 == 0:
                    nc.vector.wait_ge(semtV, TH_ST["bc_tV"])
                if i == 14 and th2 == 0:
                    nc.vector.wait_ge(semtT, TH_ST["end_tT"])
                pre_sl = XAa[:, i * 1024 + th2 * 512:i * 1024 + (th2 + 1) * 512]
                pre_V.append(inc("V", nc.vector.scalar_tensor_tensor(
                    pre_sl, gt, bcP_s[:, i:i + 1], xnt, ALU.mult, ALU.add)))
                wt(nc.scalar, "V")
                pb16 = B1a[:, 14336 + sslot * 512:14336 + (sslot + 1) * 512]
                inc("S", nc.scalar.copy(pb16, pre_sl))
                wt(nc.scalar, "S")
                sq16 = B1a[:, 15360 + sslot * 512:15360 + (sslot + 1) * 512]
                inc("S", nc.scalar.activation(sq16, pb16, AF.Square))
                wt(nc.tensor, "S")
                st16 = (i == 0)
                sp16 = (i == NDT - 1)
                if st16 and th2 == 0:
                    psum_wait(nc.tensor, "B0", "B1", "B2", "B3")
                nc.tensor.matmul(PBa[0:1, th2 * 512:(th2 + 1) * 512], onesb_s[:], pb16,
                                 start=st16, stop=sp16)
                inc("T", nc.tensor.matmul(PBa[0:1, 1024 + th2 * 512:1024 + (th2 + 1) * 512],
                                          onesb_s[:], sq16, start=st16, stop=sp16))
            gate_marks.append(cnt["T"])
        ln2_T = cnt["T"]

        wt(nc.vector, "T")
        nc.vector.tensor_copy(row(0, 0)[:, 0:512], PBa[0:1, 0:512])
        nc.vector.tensor_copy(row(0, 0)[:, 512:1024], PBa[0:1, 512:1024])
        nc.vector.tensor_copy(row(32, 0)[:, 0:512], PBa[0:1, 1024:1536])
        inc("V", nc.vector.tensor_copy(row(32, 0)[:, 512:1024], PBa[0:1, 1536:2048]))
        for kk in ("B0", "B1", "B2", "B3"):
            psum_used(kk, "V")
        ln_rows(row(0, 0), row(32, 0), row(0, 1), row(64, 0), row(32, 0), EPS2)
        bcast_row(row(0, 1), B1f[:, B_BC1:B_BC1 + 1024], "A0", "A1", A0, A1)
        bcast_row(row(64, 0), B1f[:, B_BC1 + 1024:B_BC1 + 2048], "A0", "A1", A0, A1)
        rstd2t = B1f[:, B_BC1:B_BC1 + 1024]
        nmr2t = B1f[:, B_BC1 + 1024:B_BC1 + 2048]
        tail2_bc_V = cnt["V"]

        wo_marks = []
        out_V = []
        out_marks = [0, 0]
        for i in range(NDT):
            slot = i % 2
            if i >= 2:
                nc.gpsimd.wait_ge(semT, wo_marks[i - 2])
            else:
                nc.gpsimd.wait_ge(semT, ln2_T)
            wkey = "We" if slot == 0 else "Wo"
            wld = dinc(wkey, nc.gpsimd.dma_start(
                out=WSa[:, slot * 2048:(slot + 1) * 2048].rearrange("p (k n) -> p k n", k=16),
                in_=woT[:, i * 128:(i + 1) * 128].rearrange("(k p) n -> p k n", p=128)))
            for th2 in range(2):
                sslot = (2 * i + th2) % 2
                psl, kname = (A0, "A0") if sslot == 0 else (A1, "A1")
                if len(out_V) >= 2:
                    nc.scalar.wait_ge(semV, out_V[-2])
                else:
                    nc.scalar.wait_ge(semV, tail2_bc_V)
                bkey = "Re" if sslot == 0 else "Ro"
                xre = B1f[:, B_XRE + sslot * 512:B_XRE + (sslot + 1) * 512]
                xrm = dinc(bkey, nc.scalar.dma_start(
                    out=xre, in_=xsT[i * 128:(i + 1) * 128, th2 * 512:(th2 + 1) * 512]))
                nc.tensor.wait_ge(sems[wkey], wld)
                nc.tensor.wait_ge(semS, o_done_S)
                psum_wait(nc.tensor, kname)
                for k in range(16):
                    st, sp = (k == 0), (k == 15)
                    osl = SOa[:, th2 * 8192 + k * 128:th2 * 8192 + k * 128 + 128]
                    osl = dataclasses.replace(osl, ap=[osl.ap[0], [2048, 4], [1, 128]])
                    mmi = nc.tensor.matmul(
                        psl, WSa[:, slot * 2048 + k * 128:slot * 2048 + (k + 1) * 128],
                        osl, start=st, stop=sp)
                    if sp:
                        inc("T", mmi)
                wo_mark_val = cnt["T"]
                wt(nc.vector, "T")
                nc.vector.wait_ge(semV, tail2_bc_V)
                nc.vector.wait_ge(sems[bkey], xrm)
                t1 = B1f[:, B_T1 + sslot * 512:B_T1 + (sslot + 1) * 512]
                pre_sl = XAa[:, i * 1024 + th2 * 512:i * 1024 + (th2 + 1) * 512]
                inc("V", nc.vector.tensor_tensor(t1, pre_sl,
                                                 rstd2t[:, th2 * 512:(th2 + 1) * 512], ALU.mult))
                wt(nc.vector, "V")
                inc("V", nc.vector.tensor_tensor(t1, t1, nmr2t[:, th2 * 512:(th2 + 1) * 512], ALU.add))
                wt(nc.vector, "V")
                inc("V", nc.vector.tensor_scalar(t1, t1, pvecs_s[:, 16 + i:17 + i],
                                                 pvecs_s[:, 32 + i:33 + i], ALU.mult, ALU.add))
                wt(nc.vector, "V")
                inc("V", nc.vector.tensor_tensor(t1, t1, xre, ALU.add))
                wt(nc.vector, "V")
                okey = "Oe" if sslot == 0 else "Oo"
                if out_marks[sslot]:
                    nc.vector.wait_ge(sems[okey], out_marks[sslot])
                osl2 = B1f[:, B_OUT + sslot * 512:B_OUT + (sslot + 1) * 512]
                out_V.append(inc("V", nc.vector.tensor_tensor(osl2, t1, psl, ALU.add)))
                psum_used(kname, "V")
                okey = "Oe" if sslot == 0 else "Oo"
                wt(nc.scalar, "V")
                out_marks[sslot] = dinc(okey, nc.scalar.dma_start(
                    out=outT[i * 128:(i + 1) * 128, th2 * 512:(th2 + 1) * 512], in_=osl2))
            wo_marks.append(cnt["T"])

        nc.scalar.wait_ge(semOe, cnt["Oe"])
        nc.scalar.wait_ge(semOo, cnt["Oo"])
        nc.sync.wait_ge(semOe, cnt["Oe"])
        nc.sync.wait_ge(semOo, cnt["Oo"])

    return nc


def _prep_inmaps(inputs):
    x = np.asarray(inputs["L0_attn_x"], dtype=np.float32)
    wq = np.asarray(inputs["L0_attn_wq"], dtype=np.float32)
    wk = np.asarray(inputs["L0_attn_wk"], dtype=np.float32)
    wv = np.asarray(inputs["L0_attn_wv"], dtype=np.float32)
    wo = np.asarray(inputs["L0_attn_wo"], dtype=np.float32)
    wqT = np.ascontiguousarray(wq.T)
    wkT = np.ascontiguousarray(wk.T)
    wvT = np.ascontiguousarray(wv.T)
    woT = np.ascontiguousarray(wo.T)
    gwv = np.ascontiguousarray(np.asarray(inputs["L0_ffn_gate_w"], dtype=np.float32))
    aggw_f = np.asarray(inputs["L0_ffn_agg_w"], dtype=np.float32)
    bcw_f = np.asarray(inputs["L0_ffn_bc_w"], dtype=np.float32)
    d1w = np.asarray(inputs["L0_ffn_d1_w"], dtype=np.float32)
    d2w = np.asarray(inputs["L0_ffn_d2_w"], dtype=np.float32)
    d1b = np.asarray(inputs["L0_ffn_d1_b"], dtype=np.float32)

    ident_b = np.eye(128, dtype=bfl)
    ident_f = np.eye(128, dtype=np.float32)[:, 0:4].copy()
    ones_b = np.ones((128, 1), dtype=bfl)
    ones_f = np.ones((128, 128), dtype=np.float32)

    def pcol(v):
        return np.ascontiguousarray(v.reshape(16, 128).T.astype(np.float32))

    pv = np.concatenate([
        pcol(np.asarray(inputs["L0_ffn_gate_b"], dtype=np.float32)),
        pcol(np.asarray(inputs["L0_ffn_out_gamma"], dtype=np.float32)),
        pcol(np.asarray(inputs["L0_ffn_out_beta"], dtype=np.float32)),
    ], axis=1)

    lam = np.exp2(-8.0 * np.arange(1, H + 1, dtype=np.float64) / H)
    r = np.arange(CH, dtype=np.float64)
    dm = r[None, None, :] - r[None, :, None]
    dmat = np.where(dm >= 0, np.exp(-lam[:, None, None] * np.maximum(dm, 0)), 0.0)
    diag_t = np.ascontiguousarray(
        dmat.transpose(1, 0, 2).reshape(CH, H * CH).astype(np.float32))
    qdec = np.exp(-lam[:, None] * (r[None, :] + 1.0))
    qdec_t = np.ascontiguousarray(
        np.broadcast_to(qdec.reshape(1, H * CH), (128, H * CH)).astype(np.float32))
    kdec = np.exp(-lam[None, :] * (CH - 1.0 - r[:, None]))
    kdec_c = np.ascontiguousarray(kdec.astype(np.float32))

    inv_freq = 1.0 / 10000.0 ** (np.arange(0, DH, 2, dtype=np.float64) / DH)

    rv = np.concatenate([
        np.asarray(inputs["L0_ffn_in_gamma"], dtype=np.float32),
        np.asarray(inputs["L0_ffn_in_beta"], dtype=np.float32),
        np.asarray(inputs["L0_ffn_pn_gamma"], dtype=np.float32),
        np.asarray(inputs["L0_ffn_pn_beta"], dtype=np.float32),
        np.asarray(inputs["L0_ffn_d2_b"], dtype=np.float32),
        np.asarray(inputs["L0_ffn_bc_b"], dtype=np.float32),
        np.asarray(inputs["L0_ffn_agg_b"], dtype=np.float32),
    ]).reshape(1, -1)

    xlast_np = np.ascontiguousarray(x[:, S - 1, :])
    nonce_np = (np.random.randint(1, 2 ** 27, size=(1, 8)) * 8 +
                np.arange(8).reshape(1, 8)).astype(np.int32)

    in_maps = []
    for c in range(8):
        b, half = c // 2, c % 2
        xsT_np = np.ascontiguousarray(x[b, half * TLOC:(half + 1) * TLOC, :].T)
        pos = np.arange(half * TLOC, (half + 1) * TLOC, dtype=np.float64)
        fr = pos[None, :] * inv_freq[:, None]
        cosf = np.cos(fr)
        sinf = np.sin(fr)
        cosF_np = np.ascontiguousarray(
            np.concatenate([cosf, cosf], axis=0).astype(np.float32))
        sinS_np = np.ascontiguousarray(
            np.concatenate([-sinf, sinf], axis=0).astype(np.float32))
        bsel = np.zeros((128, 4), dtype=np.float32)
        bsel[:, b] = 1.0
        sel = np.full((128, 1), 1.0 if half == 1 else 0.0, dtype=np.float32)
        hh = c % 2
        rvc = np.concatenate(
            [rv, d1b[hh * 4096:(hh + 1) * 4096].reshape(1, -1)], axis=1)
        xinfo_np = np.array([[hh * 136, hh, 1 - hh] + [0] * 21],
                            dtype=np.int32)
        in_maps.append({
            "xsT": xsT_np,
            "xlast": xlast_np,
            "wqT": wqT, "wkT": wkT, "wvT": wvT, "woT": woT, "gw": gwv,
            "aggw": np.ascontiguousarray(aggw_f[:, hh * 1024:(hh + 1) * 1024]),
            "bcw": np.ascontiguousarray(bcw_f[:, hh * 1024:(hh + 1) * 1024]),
            "d1h": np.ascontiguousarray(d1w[:, hh * 4096:(hh + 1) * 4096]),
            "d2h": np.ascontiguousarray(d2w[hh * 4096:(hh + 1) * 4096, :]),
            "xinfo": xinfo_np,
            "nonces": nonce_np,
            "rvec": np.ascontiguousarray(np.tile(rvc, (4, 1))),
            "pvecs": pv,
            "cosF": cosF_np, "sinS": sinS_np,
            "diagT": diag_t, "qdecT": qdec_t, "kdecC": kdec_c,
            "identb": ident_b, "identf": ident_f,
            "onesb": ones_b, "onesf": ones_f,
            "bselP": bsel, "selS": sel,
            "consts": np.tile(np.array([[EPS1, EPS2]], dtype=np.float32), (128, 1)),
        })
    return in_maps


def kernel(**inputs):
    in_maps = _prep_inmaps(inputs)
    nc = build_nc()
    res = run_bass_kernel_spmd(nc, in_maps, core_ids=list(range(8)))
    out = np.empty((B, S, D), dtype=np.float32)
    for c in range(8):
        b, half = c // 2, c % 2
        out[b, half * TLOC:(half + 1) * TLOC, :] = res.results[c]["outT"].T
    return out


# version 09 (final)
# speedup vs baseline: 3.8842x; 1.2577x over previous
import math
import numpy as np
import ml_dtypes
import dataclasses
import concourse.bass as bass
from concourse.expressions import RuntimeValue
import concourse.mybir as mybir
from concourse.bass_utils import run_bass_kernel_spmd

F32 = mybir.dt.float32
BF16 = mybir.dt.bfloat16
AF = mybir.ActivationFunctionType
ALU = mybir.AluOpType
AX = mybir.AxisListType
bfl = ml_dtypes.bfloat16

B, S, D = 4, 2048, 2048
H, HKV, DH = 16, 4, 128
TLOC = 1024
NDT = 16
NCH = 8
CH = 128
HD = 8192
HDS = HD // 8
EPS1 = 1e-5
EPS2 = 1e-6

XA_K = 0
XA_KT = 2048
XA_V = 4096
XA_G0 = 6144
XA_G1 = 7168
XA_ST = 8192
XA_E = 10240
XA_F = 12288
XA_SP = 14336

lam_np = np.exp2(-8.0 * np.arange(1, H + 1, dtype=np.float64) / H)
blk_f = [float(np.exp(-lam_np[h] * CH)) for h in range(H)]


def build_nc():
    nc = bass.Bass("TRN2", target_bir_lowering=False, debug=False,
                   detect_race_conditions=False, enable_asserts=False)
    xsT = nc.dram_tensor("xsT", [D, TLOC], F32, kind="ExternalInput").ap()
    xlast = nc.dram_tensor("xlast", [4, D], F32, kind="ExternalInput").ap()
    wqT = nc.dram_tensor("wqT", [D, D], F32, kind="ExternalInput").ap()
    wkT = nc.dram_tensor("wkT", [D, 512], F32, kind="ExternalInput").ap()
    wvT = nc.dram_tensor("wvT", [D, 512], F32, kind="ExternalInput").ap()
    woT = nc.dram_tensor("woT", [D, D], F32, kind="ExternalInput").ap()
    gw = nc.dram_tensor("gw", [D, D], F32, kind="ExternalInput").ap()
    aggw = nc.dram_tensor("aggw", [D, 1024], F32, kind="ExternalInput").ap()
    bcw = nc.dram_tensor("bcw", [D, 1024], F32, kind="ExternalInput").ap()
    d1h = nc.dram_tensor("d1h", [D, 4096], F32, kind="ExternalInput").ap()
    d2h = nc.dram_tensor("d2h", [4096, D], F32, kind="ExternalInput").ap()
    rvec = nc.dram_tensor("rvec", [4, 7 * D + 4096], F32, kind="ExternalInput").ap()
    xinfo = nc.dram_tensor("xinfo", [1, 24], mybir.dt.int32, kind="ExternalInput").ap()
    nonces = nc.dram_tensor("nonces", [1, 8], mybir.dt.int32, kind="ExternalInput").ap()
    pvecs = nc.dram_tensor("pvecs", [128, 48], F32, kind="ExternalInput").ap()
    cosF = nc.dram_tensor("cosF", [128, 1024], F32, kind="ExternalInput").ap()
    sinS = nc.dram_tensor("sinS", [128, 1024], F32, kind="ExternalInput").ap()
    diagT = nc.dram_tensor("diagT", [128, 2048], F32, kind="ExternalInput").ap()
    qdecT = nc.dram_tensor("qdecT", [128, 2048], F32, kind="ExternalInput").ap()
    kdecC = nc.dram_tensor("kdecC", [128, 16], F32, kind="ExternalInput").ap()
    identb = nc.dram_tensor("identb", [128, 128], BF16, kind="ExternalInput").ap()
    identf = nc.dram_tensor("identf", [128, 4], F32, kind="ExternalInput").ap()
    onesb = nc.dram_tensor("onesb", [128, 1], BF16, kind="ExternalInput").ap()
    onesf = nc.dram_tensor("onesf", [128, 128], F32, kind="ExternalInput").ap()
    bselP = nc.dram_tensor("bselP", [128, 4], F32, kind="ExternalInput").ap()
    selS = nc.dram_tensor("selS", [128, 1], F32, kind="ExternalInput").ap()
    consts = nc.dram_tensor("consts", [128, 2], F32, kind="ExternalInput").ap()
    outT = nc.dram_tensor("outT", [D, TLOC], F32, kind="ExternalOutput").ap()
    SHB = nc.dram_tensor("SHB", [272, 2048], F32, addr_space="Shared")
    SHF = nc.dram_tensor("SHF", [2, 2], mybir.dt.int32, addr_space="Shared")
    dc1 = nc.dram_tensor("dc1", [D, 4096], BF16)
    dc2 = nc.dram_tensor("dc2", [4096, D], BF16)

    from contextlib import ExitStack
    with ExitStack() as _es:
        XA = _es.enter_context(nc.sbuf_tensor("XA", [128, 16384], F32))
        B1 = _es.enter_context(nc.sbuf_tensor("B1", [128, 16384], BF16))
        SO = _es.enter_context(nc.sbuf_tensor("SO", [128, 16384], BF16))
        XN = _es.enter_context(nc.sbuf_tensor("XN", [128, 16384], BF16))
        WS = _es.enter_context(nc.sbuf_tensor("WS", [128, 4096], BF16))
        ROWS = _es.enter_context(nc.sbuf_tensor("ROWS", [128, 4608], F32))
        TH = _es.enter_context(nc.sbuf_tensor("TH", [128, 4104], F32))
        identb_s = _es.enter_context(nc.sbuf_tensor("identb_s", [128, 128], BF16))
        identf_s = _es.enter_context(nc.sbuf_tensor("identf_s", [128, 4], F32))
        onesb_s = _es.enter_context(nc.sbuf_tensor("onesb_s", [128, 1], BF16))
        onesf_s = _es.enter_context(nc.sbuf_tensor("onesf_s", [128, 128], F32))
        pvecs_s = _es.enter_context(nc.sbuf_tensor("pvecs_s", [128, 48], F32))
        bselP_s = _es.enter_context(nc.sbuf_tensor("bselP_s", [128, 4], F32))
        selS_s = _es.enter_context(nc.sbuf_tensor("selS_s", [128, 1], F32))
        kdecC_s = _es.enter_context(nc.sbuf_tensor("kdecC_s", [128, 16], F32))
        xnlT_s = _es.enter_context(nc.sbuf_tensor("xnlT_s", [128, 64], BF16))
        tT_s = _es.enter_context(nc.sbuf_tensor("tT_s", [128, 64], BF16))
        uT_s = _es.enter_context(nc.sbuf_tensor("uT_s", [128, 128], BF16))
        bcT_s = _es.enter_context(nc.sbuf_tensor("bcT_s", [128, 64], F32))
        bcP_s = _es.enter_context(nc.sbuf_tensor("bcP_s", [128, 16], F32))
        consts_s = _es.enter_context(nc.sbuf_tensor("consts_s", [128, 2], F32))
        xinfo_s = _es.enter_context(nc.sbuf_tensor("xinfo_s", [1, 24], mybir.dt.int32))
        nonce_s = _es.enter_context(nc.sbuf_tensor("nonce_s", [1, 8], mybir.dt.int32))
        flag_s = _es.enter_context(nc.sbuf_tensor("flag_s", [1, 1], mybir.dt.int32))
        PA = _es.enter_context(nc.psum_tensor("PA", [128, 1536], F32))
        PB = _es.enter_context(nc.psum_tensor("PB", [128, 2048], F32))
        PT = _es.enter_context(nc.psum_tensor("PT", [128, 1024], BF16))
        semX = _es.enter_context(nc.semaphore("sX"))
        semW = _es.enter_context(nc.semaphore("sW"))
        semR = _es.enter_context(nc.semaphore("sR"))
        semT = _es.enter_context(nc.semaphore("sT"))
        semV = _es.enter_context(nc.semaphore("sV"))
        semS = _es.enter_context(nc.semaphore("sS"))
        semG = _es.enter_context(nc.semaphore("sG"))
        semC = _es.enter_context(nc.semaphore("sC"))
        semBe = _es.enter_context(nc.semaphore("sBe"))
        semBo = _es.enter_context(nc.semaphore("sBo"))
        semWe = _es.enter_context(nc.semaphore("sWe"))
        semWo = _es.enter_context(nc.semaphore("sWo"))
        semROW = _es.enter_context(nc.semaphore("sROW"))
        semCD = _es.enter_context(nc.semaphore("sCD"))
        semSTG = _es.enter_context(nc.semaphore("sSTG"))
        semG1 = _es.enter_context(nc.semaphore("sG1"))
        semG2 = _es.enter_context(nc.semaphore("sG2"))
        semG3 = _es.enter_context(nc.semaphore("sG3"))
        semG4 = _es.enter_context(nc.semaphore("sG4"))
        semG5 = _es.enter_context(nc.semaphore("sG5"))
        semOe = _es.enter_context(nc.semaphore("sOe"))
        semOo = _es.enter_context(nc.semaphore("sOo"))
        semCS = _es.enter_context(nc.semaphore("sCS"))
        semPe = _es.enter_context(nc.semaphore("sPe"))
        semPo = _es.enter_context(nc.semaphore("sPo"))
        semRe = _es.enter_context(nc.semaphore("sRe"))
        semRo = _es.enter_context(nc.semaphore("sRo"))
        semWX = _es.enter_context(nc.semaphore("sWX"))
        semPF = _es.enter_context(nc.semaphore("sPF"))
        semSH = _es.enter_context(nc.semaphore("sSH"))
        semTe = _es.enter_context(nc.semaphore("sTe"))
        semTo = _es.enter_context(nc.semaphore("sTo"))
        semCe = _es.enter_context(nc.semaphore("sCe"))
        semCo = _es.enter_context(nc.semaphore("sCo"))
        semTh0 = _es.enter_context(nc.semaphore("sTh0"))
        semTh1 = _es.enter_context(nc.semaphore("sTh1"))
        semT3 = _es.enter_context(nc.semaphore("sT3"))
        semT4 = _es.enter_context(nc.semaphore("sT4"))
        semT5 = _es.enter_context(nc.semaphore("sT5"))
        semT6 = _es.enter_context(nc.semaphore("sT6"))
        semT7 = _es.enter_context(nc.semaphore("sT7"))
        semtT = _es.enter_context(nc.semaphore("stT"))
        semtV = _es.enter_context(nc.semaphore("stV"))
        semtS = _es.enter_context(nc.semaphore("stS"))
        semDG = _es.enter_context(nc.semaphore("sDG"))
        blk = _es.enter_context(nc.Block())
        cnt = dict(X=0, W=0, R=0, T=0, V=0, S=0, G=0, C=0,
                   Be=0, Bo=0, We=0, Wo=0, ROW=0, CD=0, STG=0,
                   G1=0, G2=0, G3=0, G4=0, G5=0, Oe=0, Oo=0, CS=0,
                   Pe=0, Po=0, Re=0, Ro=0,
                   WX=0, PF=0, SH=0, Te=0, To=0, Ce=0, Co=0,
                   tT=0, tV=0, tS=0, DG=0, Th0=0, Th1=0, T3=0, T4=0, T5=0,
                   T6=0, T7=0)
        sems = dict(X=semX, W=semW, R=semR, T=semT, V=semV, S=semS, G=semG, C=semC,
                    Be=semBe, Bo=semBo, We=semWe, Wo=semWo, ROW=semROW, CD=semCD,
                    STG=semSTG, G1=semG1, G2=semG2, G3=semG3, G4=semG4, G5=semG5,
                    Oe=semOe, Oo=semOo, CS=semCS,
                    Pe=semPe, Po=semPo, Re=semRe, Ro=semRo,
                    WX=semWX, PF=semPF, SH=semSH, Te=semTe, To=semTo,
                    Ce=semCe, Co=semCo,
                    tT=semtT, tV=semtV, tS=semtS, DG=semDG,
                    Th0=semTh0, Th1=semTh1, T3=semT3, T4=semT4, T5=semT5,
                    T6=semT6, T7=semT7)

        def inc(key, inst, k=1):
            inst.then_inc(sems[key], k)
            cnt[key] += k
            return cnt[key]

        def dinc(key, inst):
            return inc(key, inst, 16)

        def wt(eng, key, val=None):
            eng.wait_ge(sems[key], cnt[key] if val is None else val)

        XAa = XA.ap()
        XAb = XAa.bitcast(BF16)
        B1a = B1.ap()
        B1f = B1a.bitcast(F32)
        SOa = SO.ap()
        XNa = XN.ap()
        WSa = WS.ap()
        ROWSa = ROWS.ap()
        THa = TH.ap()
        PAa = PA.ap()
        PBa = PB.ap()
        PTa = PT.ap()

        def xab(c0f32, nbf):
            return XAb[:, 2 * c0f32:2 * c0f32 + nbf]

        def row(p, j):
            slot = {(0, 0): 0, (32, 0): 1, (0, 1): 2, (64, 0): 3}[(p, j)]
            return ROWSa[0:1, slot * 1024:(slot + 1) * 1024]

        th_main = THa[0:4, 0:2048]
        th_t = THa[0:4, 2048:4096]
        th_u = THa[0:4, 2048:3072]
        th_d = th_t

        def th_scr(j):
            return THa[0:4, 4096 + j:4096 + j + 1]

        psum_last = {}

        def psum_wait(eng, *keys):
            for k in keys:
                if k in psum_last:
                    kk, v = psum_last[k]
                    eng.wait_ge(sems[kk], v)

        def psum_used(key, semname):
            psum_last[key] = (semname, cnt[semname])

        A0 = PAa[:, 0:512]
        A1 = PAa[:, 512:1024]
        A2 = PAa[:, 1024:1536]

        RV_ING, RV_INB, RV_PNG, RV_PNB, RV_D2B, RV_BCB, RV_AGB, RV_D1B = (
            0, D, 2 * D, 3 * D, 4 * D, 5 * D, 6 * D, 7 * D)

        for dst, src in [
            (identb_s[:], identb), (identf_s[:], identf), (onesb_s[:], onesb),
            (onesf_s[:], onesf), (pvecs_s[:], pvecs), (bselP_s[:], bselP),
            (selS_s[:], selS), (kdecC_s[:], kdecC),
            (XAa[:, XA_F:XA_F + 1024], cosF),
            (XAa[:, XA_F + 1024:XA_F + 2048], sinS),
            (THa[0:4, 0:2048], xlast),
            (consts_s[:], consts),
            (xinfo_s[:], xinfo),
            (nonce_s[:], nonces),
        ]:
            dinc("X", nc.sync.dma_start(out=dst, in_=src))
        base_loads = cnt["X"]
        nc.const_aps.register(F32, EPS1, consts_s[:, 0:1])
        nc.const_aps.register(F32, EPS2, consts_s[:, 1:2])
        for eng in (nc.tensor, nc.vector, nc.scalar, nc.gpsimd):
            eng.wait_ge(semX, base_loads)
        cosF_s = XAa[:, XA_F:XA_F + 1024]
        sinS_s = XAa[:, XA_F + 1024:XA_F + 2048]

        SHBa = SHB.ap()
        SHFa = SHF.ap()
        E_AGG, E_STATE, E_S1, E_S2, E_S3, E_S4, E_BC = range(7)

        nc.sync.wait_ge(semX, base_loads)
        mb_r = nc.sync.alloc_register("xr_mb")
        nc.sync.reg_load(mb_r, xinfo_s[0:1, 0:1])
        myp_r = nc.sync.alloc_register("xr_myp")
        nc.sync.reg_load(myp_r, xinfo_s[0:1, 1:2])
        pp_r = nc.sync.alloc_register("xr_pp")
        nc.sync.reg_load(pp_r, xinfo_s[0:1, 2:3])
        mb = RuntimeValue(mb_r, 0, 136)
        myp = RuntimeValue(myp_r, 0, 1)
        pp = RuntimeValue(pp_r, 0, 1)
        poll_nd = nc.sync.alloc_register("poll_nd")
        poll_v = nc.sync.alloc_register("poll_v")
        poll_n = nc.sync.alloc_register("poll_n")
        ap_w = {
            (0, 1024): SHBa[bass.ds(mb, 4), 0:1024],
            (0, 2048): SHBa[bass.ds(mb, 4), 0:2048],
            (1, 1024): SHBa[bass.ds(mb + 4, 4), 0:1024],
            (1, 2048): SHBa[bass.ds(mb + 4, 4), 0:2048],
            "st": SHBa[bass.ds(mb + 8, 128), 0:2048],
        }
        ap_fw = [SHFa[bass.ds(myp, 1), 0:1], SHFa[bass.ds(myp, 1), 1:2]]
        ap_fr = [SHFa[bass.ds(pp, 1), 0:1], SHFa[bass.ds(pp, 1), 1:2]]
        EXC = {"n": 0}

        def ex_next():
            i = EXC["n"]
            EXC["n"] += 1
            return i

        def sh_write(src_ap, width, exi):
            sl = exi % 2
            wkey = "st" if width == "st" else (sl, width)
            dinc("WX", nc.sync.dma_start(out=ap_w[wkey], in_=src_ap))
            nc.sync.wait_ge(semWX, cnt["WX"])
            dinc("WX", nc.sync.dma_start(
                out=ap_fw[sl], in_=nonce_s[0:1, exi:exi + 1]))
            nc.sync.wait_ge(semWX, cnt["WX"])

        def sh_poll(exi):
            sl = exi % 2
            nc.sync.reg_mov(poll_nd, 1)
            with nc.sync.While(poll_nd):
                nc.sync.sem_clear(semPF)
                nc.sync.dma_start(out=flag_s[:],
                                  in_=ap_fr[sl]).then_inc(semPF, 16)
                nc.sync.wait_ge(semPF, 16)
                nc.sync.reg_load(poll_v, flag_s[0:1, 0:1])
                nc.sync.reg_load(poll_n, nonce_s[0:1, exi:exi + 1])
                nc.sync.reg_alu(poll_nd, poll_v, poll_n, ALU.not_equal)

        def sh_read(dst_ap, plane, r0, nrows, width):
            row0 = plane * 136 + r0
            m = dinc("SH", nc.sync.dma_start(
                out=dst_ap, in_=SHBa[row0:row0 + nrows, 0:width]))
            nc.sync.wait_ge(semSH, m)
            return m

        def th_ln(src, dst, eps):
            wt(nc.vector, "tS")
            wt(nc.vector, "tV")
            inc("tV", nc.vector.tensor_reduce(th_scr(0), src, AX.X, ALU.add))
            wt(nc.scalar, "tV")
            wt(nc.scalar, "tT")
            inc("tS", nc.scalar.activation(dst, src, AF.Square))
            wt(nc.vector, "tS")
            inc("tV", nc.vector.tensor_reduce(th_scr(1), dst, AX.X, ALU.add))
            yield
            wt(nc.scalar, "tV")
            nc.scalar.mul(th_scr(2), th_scr(0), -1.0 / D)
            nc.scalar.mul(th_scr(1), th_scr(1), 1.0 / D)
            inc("tS", nc.scalar.mul(th_scr(3), th_scr(0), 1.0 / D))
            wt(nc.scalar, "tS")
            inc("tS", nc.scalar.activation(th_scr(0), th_scr(3), AF.Square))
            wt(nc.gpsimd, "tS")
            inc("tV", nc.gpsimd.tensor_tensor(th_scr(1), th_scr(1), th_scr(0), ALU.subtract))
            wt(nc.scalar, "tV")
            inc("tS", nc.scalar.activation(th_scr(1), th_scr(1), AF.Ln, bias=eps))
            wt(nc.scalar, "tS")
            inc("tS", nc.scalar.activation(th_scr(1), th_scr(1), AF.Exp, scale=-0.5))
            wt(nc.vector, "tS")
            inc("tV", nc.vector.tensor_scalar(dst, src, th_scr(2), th_scr(1), ALU.add, ALU.mult))
            yield

        rrow_buf = ROWSa[0:4, 4096:4608]

        def th_loadrow(off, n):
            wt(nc.sync, "tV")
            return dinc("ROW", nc.sync.dma_start(out=rrow_buf[:, 0:n], in_=rvec[:, off:off + n]))

        def th_rowapply(dst, off, n, op):
            for q in range((n + 511) // 512):
                c0 = q * 512
                w = min(n, c0 + 512) - c0
                ldm = th_loadrow(off + c0, w)
                nc.gpsimd.wait_ge(semROW, ldm)
                wt(nc.gpsimd, "tV")
                inc("tV", nc.gpsimd.tensor_tensor(dst[:, c0:c0 + w], dst[:, c0:c0 + w],
                                                  rrow_buf[:, 0:w], op))
                yield

        def th_transpose(src4, dstT, ncols):
            wt(nc.tensor, "tV")
            wt(nc.tensor, "tS")
            psum_wait(nc.tensor, "B3")
            bp = src4.base_partition()
            idsl = identf_s.ap()[bp:bp + 4, bp:bp + 4]
            for j in range(ncols):
                mmi = nc.tensor.transpose(PBa[:, 1536 + j * 4:1536 + (j + 1) * 4],
                                          src4[:, j * 128:(j + 1) * 128], idsl)
                if j == ncols - 1:
                    inc("tT", mmi)
            wt(nc.vector, "tT")
            inc("tV", nc.vector.tensor_copy(dstT[:, 0:ncols * 4], PBa[:, 1536:1536 + ncols * 4]))
            psum_used("B3", "tV")
            yield

        MAIN_MARKS = {}

        def mk_slot(c0f32, sw_key, hw_eng, hw_key, ck=None):
            return dict(ap=xab(c0f32, 2048), sw=sw_key, he=hw_eng, hk=hw_key,
                        ck=ck, mm=0, wb=0, m=0, mk=None, guard=None)

        SL_SP0 = mk_slot(XA_SP, "Te", None, None, ck="Ce")
        SL_SP1 = mk_slot(XA_SP + 1024, "To", nc.sync, "Th0", ck="Co")
        SL_G1 = mk_slot(XA_G1, "T7", nc.scalar, "T6", ck="CS")
        SL_E0 = mk_slot(XA_E, None, nc.scalar, "Th1")
        SL_E1 = mk_slot(XA_E + 1024, "T3", None, None)
        SL_F0 = mk_slot(XA_F, None, nc.sync, "T4")
        SL_F1 = mk_slot(XA_F + 1024, None, nc.scalar, "T5")
        SL_E0["guard"] = SL_E1["guard"] = "adj_V"
        SL_F0["guard"] = SL_F1["guard"] = "q_rope_V"
        THS = dict(dg_prev=0, ce_tot=0, co_tot=0)

        def th_units(units, step, slots):
            chunk_pos = [i for i, u in enumerate(units) if u[0] == "c"]
            n = len(chunk_pos)
            issued = [0]

            def issue_one():
                ci = issued[0]
                sl = slots[ci % len(slots)]
                _, src_f, src_b, cache, _fn = units[chunk_pos[ci]]
                if step == 0 and src_f is not None:
                    eng, wk, src = nc.gpsimd, sl["sw"], src_f
                elif sl["he"] is None:
                    eng, wk, src = nc.gpsimd, sl["sw"], src_b
                else:
                    eng, wk, src = sl["he"], sl["hk"], src_b
                if sl["guard"] is not None:
                    eng.wait_ge(semV, MAIN_MARKS[sl["guard"]])
                    sl["guard"] = None
                if sl["mm"]:
                    eng.wait_ge(semtT, sl["mm"])
                if sl["wb"]:
                    eng.wait_ge(sems[sl["ck"]], sl["wb"])
                sl["m"] = dinc(wk, eng.dma_start(out=sl["ap"], in_=src))
                sl["mk"] = wk
                issued[0] += 1

            while issued[0] < min(len(slots), n):
                issue_one()
            ci = 0
            for u in units:
                if u[0] == "g":
                    yield from u[1]
                    continue
                sl = slots[ci % len(slots)]
                nc.tensor.wait_ge(sems[sl["mk"]], sl["m"])
                u[4](sl["ap"])
                sl["mm"] = cnt["tT"]
                if step == 0 and u[3] is not None:
                    nc.scalar.wait_ge(sems[sl["mk"]], sl["m"])
                    sl["wb"] = dinc(sl["ck"], nc.scalar.dma_start(
                        out=u[3], in_=sl["ap"]))
                ci += 1
                if issued[0] < n:
                    issue_one()
                yield

        def _emit1(fn):
            fn()
            yield

        TB = (0, 32, 64, 0)

        def acc_ap(q):
            cc = 1024 if q < 3 else 1536
            return PBa[TB[q]:TB[q] + 4, cc:cc + 512]

        def th_gather(dst_c0):
            wt(nc.vector, "tT")
            if THS["dg_prev"]:
                nc.vector.wait_ge(semDG, THS["dg_prev"])
            dst0 = THa[0:4, dst_c0:dst_c0 + 512]
            dst0 = dataclasses.replace(dst0, ap=[dst0.ap[0], [1536, 2], [1, 512]])
            src0 = PBa[0:4, 1024:1536]
            src0 = dataclasses.replace(src0, ap=[src0.ap[0], [512, 2], [1, 512]])
            nc.vector.tensor_copy(dst0, src0)
            nc.vector.tensor_copy(THa[32:36, 0:512], PBa[32:36, 1024:1536])
            inc("tV", nc.vector.tensor_copy(THa[64:68, 0:512], PBa[64:68, 1024:1536]))
            psum_used("B2", "tV")
            psum_used("B3", "tV")
            wt(nc.sync, "tV")
            dinc("DG", nc.sync.dma_start(
                out=THa[0:4, dst_c0 + 512:dst_c0 + 1024], in_=THa[32:36, 0:512]))
            dinc("DG", nc.sync.dma_start(
                out=THa[0:4, dst_c0 + 1024:dst_c0 + 1536], in_=THa[64:68, 0:512]))
            THS["dg_prev"] = cnt["DG"]
            nc.sync.wait_ge(semDG, cnt["DG"])

        def _gate(key):
            while key not in MAIN_MARKS:
                yield

        def agg_mms(c8, lhsT):
            def mms(sl):
                for k in range(16):
                    mmi = nc.tensor.matmul(
                        PBa[0:4, 1024 + c8 * 128:1024 + (c8 + 1) * 128],
                        lhsT[:, k * 4:(k + 1) * 4],
                        sl[:, k * 128:(k + 1) * 128],
                        start=(k == 0), stop=(k == 15),
                        skip_group_check=True)
                    if k == 15:
                        inc("tT", mmi)
            return mms

        def _pre_head():
            nc.gpsimd.wait_ge(semX, base_loads)
            yield from th_ln(th_main, th_t, EPS1)
            yield from th_ln(th_t, th_main, EPS2)
            yield from th_rowapply(th_main, RV_ING, 2048, ALU.mult)
            yield from th_rowapply(th_main, RV_INB, 2048, ALU.add)
            yield from th_transpose(th_main, xnlT_s.ap(), 16)
            psum_wait(nc.tensor, "B2")
            wt(nc.tensor, "tV")

        def _agg_exchange(dst_lo, dst_hi, bar_ap):
            wt(nc.vector, "tT")
            inc("tV", nc.vector.tensor_copy(th_d[:, 0:1024], PBa[0:4, 1024:2048]))
            psum_used("B2", "tV")
            psum_used("B3", "tV")
            exi = ex_next()
            wt(nc.sync, "tV")
            sh_write(th_d[:, 0:1024], 1024, exi)
            sh_poll(exi)
            r0 = (exi % 2) * 4
            sh_read(dst_lo, 0, r0, 4, 1024)
            m = sh_read(dst_hi, 1, r0, 4, 1024)
            yield
            nc.gpsimd.wait_ge(semSH, m)
            inc("tV", nc.gpsimd.tensor_copy(bar_ap, bar_ap))

        def _pre_tail():
            yield from _agg_exchange(th_main[:, 0:1024],
                                     th_main[:, 1024:2048], th_main)
            yield from th_rowapply(th_main, RV_AGB, 2048, ALU.add)

        def _step_head(step):
            yield from th_ln(th_main, th_t, EPS2)
            yield from th_rowapply(th_t, RV_PNG, 2048, ALU.mult)
            yield from th_rowapply(th_t, RV_PNB, 2048, ALU.add)
            yield from th_transpose(th_t, tT_s.ap(), 16)
            psum_wait(nc.tensor, "B2", "B3")
            wt(nc.tensor, "tV")

        def _d1_post(step, nh):
            th_gather(2048)
            yield
            nc.gpsimd.wait_ge(semDG, cnt["DG"])
            yield from th_rowapply(th_t, RV_D1B + nh * 2048, 2048, ALU.add)
            wt(nc.scalar, "tV")
            inc("tS", nc.scalar.activation(th_t, th_t, AF.Gelu))
            yield from th_transpose(th_t, uT_s.ap()[:, nh * 64:(nh + 1) * 64], 16)
            psum_wait(nc.tensor, "B2", "B3")
            wt(nc.tensor, "tV")

        def _step_tail(step):
            th_gather(2048)
            exi = ex_next()
            sh_write(th_d[:, 0:2048], 2048, exi)
            sh_poll(exi)
            r0 = (exi % 2) * 4
            m = sh_read(th_d[:, 0:2048], 0, r0, 4, 2048)
            yield
            nc.gpsimd.wait_ge(semSH, m)
            wt(nc.gpsimd, "tV")
            inc("tV", nc.gpsimd.tensor_tensor(th_main, th_main, th_d[:, 0:2048], ALU.add))
            wt(nc.sync, "tV")
            m = sh_read(th_d[:, 0:2048], 1, r0, 4, 2048)
            nc.gpsimd.wait_ge(semSH, m)
            wt(nc.gpsimd, "tV")
            inc("tV", nc.gpsimd.tensor_tensor(th_main, th_main, th_d[:, 0:2048], ALU.add))
            yield
            yield from th_rowapply(th_main, RV_D2B, 2048, ALU.add)

        def d1_units(step):
            units = []
            for nh in range(2):
                for k in range(16):
                    def mms(sl, k=k):
                        for q in range(4):
                            mmi = nc.tensor.matmul(
                                acc_ap(q), tT_s[:, k * 4:(k + 1) * 4],
                                sl[:, q * 512:(q + 1) * 512],
                                start=(k == 0), stop=(k == 15),
                                skip_group_check=True)
                            if q == 3:
                                inc("tT", mmi)
                    units.append((
                        "c",
                        d1h[k * 128:(k + 1) * 128, nh * 2048:(nh + 1) * 2048],
                        dc1.ap()[k * 128:(k + 1) * 128, nh * 2048:(nh + 1) * 2048],
                        dc1.ap()[k * 128:(k + 1) * 128, nh * 2048:(nh + 1) * 2048],
                        mms))
                units.append(("g", _d1_post(step, nh)))
            return units

        def d2_units(step):
            units = []
            for kt in range(32):
                def mms(sl, kt=kt):
                    for g in range(4):
                        mmi = nc.tensor.matmul(
                            acc_ap(g), uT_s[:, kt * 4:(kt + 1) * 4],
                            sl[:, g * 512:(g + 1) * 512],
                            start=(kt == 0), stop=(kt == 31),
                            skip_group_check=True)
                        if g == 3:
                            inc("tT", mmi)
                units.append(("c", d2h[kt * 128:(kt + 1) * 128, :],
                              dc2.ap()[kt * 128:(kt + 1) * 128, :],
                              dc2.ap()[kt * 128:(kt + 1) * 128, :], mms))
            return units

        def _bc_tail():
            bcT_ap = bcT_s.ap()
            yield from th_rowapply(th_d, RV_BCB, 2048, ALU.add)
            yield from th_transpose(th_d, bcT_ap, 16)

            def bcT_slice(b):
                sl = bcT_ap[:, b:b + 61]
                return dataclasses.replace(sl, ap=[sl.ap[0], [4, 16]])
            wt(nc.vector, "tV")
            inc("tV", nc.vector.tensor_scalar_mul(bcP_s[:], bcT_slice(0), bselP_s[:, 0:1]))
            for b4 in range(1, 4):
                wt(nc.vector, "tV")
                inc("tV", nc.vector.scalar_tensor_tensor(bcP_s[:], bcT_slice(b4),
                                                         bselP_s[:, b4:b4 + 1],
                                                         bcP_s[:], ALU.mult, ALU.add))
            TH_ST["bc_tV"] = cnt["tV"]
            TH_ST["end_tT"] = cnt["tT"]

        def th_all_gen():
            units = [("g", _pre_head())]
            for c8 in range(8):
                units.append(("c", aggw[:, c8 * 128:(c8 + 1) * 128].rearrange(
                    "(k p) n -> p k n", p=128), None, None, agg_mms(c8, xnlT_s)))
            yield from th_units(units, 0, [SL_SP0, SL_SP1, SL_G1])
            units = [("g", _pre_tail()),
                     ("g", _step_head(0))] + d1_units(0) + d2_units(0)
            yield from th_units(units, 0, [SL_SP0, SL_SP1, SL_G1])
            THS["ce_tot"] = cnt["Ce"]
            THS["co_tot"] = cnt["Co"]
            for eng in (nc.gpsimd, nc.sync, nc.scalar):
                eng.wait_ge(semCe, THS["ce_tot"])
                eng.wait_ge(semCo, THS["co_tot"])
            units = [("g", _step_tail(0)), ("g", _step_head(1))] + \
                d1_units(1) + d2_units(1)
            yield from th_units(units, 1, [SL_SP0, SL_SP1, SL_G1])
            yield from _gate("adj_V")
            units = [("g", _step_tail(1)), ("g", _step_head(2))] + \
                d1_units(2) + d2_units(2)
            yield from th_units(units, 2, [SL_SP0, SL_SP1, SL_G1, SL_E0, SL_E1])
            TH_ST["g1_mm"] = SL_G1["mm"]
            yield from _gate("q_rope_V")
            units = [("g", _step_tail(2)), ("g", _step_head(3))] + \
                d1_units(3) + d2_units(3)
            yield from th_units(units, 3, [SL_SP0, SL_SP1, SL_E0, SL_E1,
                                           SL_F0, SL_F1])
            units = [("g", _step_tail(3)),
                     ("g", th_transpose(th_main, tT_s.ap(), 16)),
                     ("g", _emit1(lambda: (psum_wait(nc.tensor, "B2"),
                                           wt(nc.tensor, "tV"))))]
            for c8 in range(8):
                units.append(("c", bcw[:, c8 * 128:(c8 + 1) * 128].rearrange(
                    "(k p) n -> p k n", p=128), None, None, agg_mms(c8, tT_s)))
            yield from th_units(units, 0, [SL_SP0, SL_SP1])
            yield from _agg_exchange(th_d[:, 0:1024], th_d[:, 1024:2048],
                                     th_d)
            yield from _bc_tail()

        TH_ST = {"done": False, "bc_tV": 0, "end_tT": 0, "g1_mm": 0}
        TH_IT = th_all_gen()

        def pump(n=1):
            if TH_ST["done"]:
                return
            try:
                for _ in range(n):
                    next(TH_IT)
            except StopIteration:
                TH_ST["done"] = True

        def pump_all():
            while not TH_ST["done"]:
                pump(64)

        stats_mm_done = [4 * (dt + 1) for dt in range(NDT)]
        xbf_done = []
        for dt in range(NDT):
            slot = dt % 2
            bkey = "Be" if slot == 0 else "Bo"
            if dt >= 2:
                nc.gpsimd.wait_ge(semT, stats_mm_done[dt - 2])
            xbf_done.append((bkey, dinc(bkey, nc.gpsimd.dma_start(
                out=xab(XA_E + slot * 512, 1024),
                in_=xsT[dt * 128:(dt + 1) * 128, :]))))
        sq_done = []
        for dt in range(NDT):
            slot = dt % 2
            nc.scalar.wait_ge(sems[xbf_done[dt][0]], xbf_done[dt][1])
            if dt >= 2:
                nc.scalar.wait_ge(semT, stats_mm_done[dt - 2])
            sq_done.append(inc("S", nc.scalar.activation(
                xab(XA_E + 1024 + slot * 512, 1024),
                xab(XA_E + slot * 512, 1024), AF.Square)))
        for dt in range(NDT):
            slot = dt % 2
            nc.tensor.wait_ge(sems[xbf_done[dt][0]], xbf_done[dt][1])
            nc.tensor.wait_ge(semS, sq_done[dt])
            st, sp = (dt == 0), (dt == NDT - 1)
            xb = xab(XA_E + slot * 512, 1024)
            xq = xab(XA_E + 1024 + slot * 512, 1024)
            nc.tensor.matmul(PAa[0:1, 0:512], onesb_s[:], xb[:, 0:512], start=st, stop=sp)
            nc.tensor.matmul(PAa[0:1, 512:1024], onesb_s[:], xb[:, 512:1024], start=st, stop=sp)
            nc.tensor.matmul(PAa[0:1, 1024:1536], onesb_s[:], xq[:, 0:512], start=st, stop=sp)
            inc("T", nc.tensor.matmul(PBa[0:1, 0:512], onesb_s[:], xq[:, 512:1024],
                                      start=st, stop=sp), 4)
        assert cnt["T"] == 4 * NDT
        wt(nc.vector, "T")
        nc.vector.tensor_copy(row(0, 0)[:, 0:512], PAa[0:1, 0:512])
        nc.vector.tensor_copy(row(0, 0)[:, 512:1024], PAa[0:1, 512:1024])
        nc.vector.tensor_copy(row(32, 0)[:, 0:512], PAa[0:1, 1024:1536])
        inc("V", nc.vector.tensor_copy(row(32, 0)[:, 512:1024], PBa[0:1, 0:512]))
        for k in ("A0", "A1", "A2", "B0"):
            psum_used(k, "V")

        def ln_rows(srow, qrow, rrow, mrow, wrow, eps):
            wt(nc.scalar, "V")
            nc.scalar.mul(mrow, srow, 1.0 / D)
            inc("S", nc.scalar.mul(wrow, qrow, 1.0 / D))
            wt(nc.scalar, "S")
            inc("S", nc.scalar.activation(rrow, mrow, AF.Square))
            wt(nc.vector, "S")
            inc("V", nc.vector.tensor_tensor(wrow, wrow, rrow, ALU.subtract))
            wt(nc.scalar, "V")
            inc("S", nc.scalar.activation(rrow, wrow, AF.Ln, bias=eps))
            wt(nc.scalar, "S")
            inc("S", nc.scalar.activation(rrow, rrow, AF.Exp, scale=-0.5))
            wt(nc.vector, "S")
            inc("V", nc.vector.tensor_tensor(mrow, mrow, rrow, ALU.mult))
            wt(nc.scalar, "V")
            inc("S", nc.scalar.mul(mrow, mrow, -1.0))
            wt(nc.vector, "S")

        ln_rows(row(0, 0), row(32, 0), row(0, 1), row(64, 0), row(32, 0), EPS1)

        def bcast_row(r, dst, pk1, pk2, psl1, psl2):
            bp = r.base_partition()
            ones1 = onesf_s.ap()[bp:bp + 1, :]
            wt(nc.tensor, "S")
            wt(nc.tensor, "V")
            psum_wait(nc.tensor, pk1, pk2)
            nc.tensor.matmul(psl1, ones1, r[:, 0:512], start=True, stop=True)
            inc("T", nc.tensor.matmul(psl2, ones1, r[:, 512:1024], start=True, stop=True))
            wt(nc.vector, "T")
            nc.vector.tensor_copy(dst[:, 0:512], psl1)
            inc("V", nc.vector.tensor_copy(dst[:, 512:1024], psl2))
            psum_used(pk1, "V")
            psum_used(pk2, "V")

        bcast_row(row(0, 1), B1f[:, 5120:6144], "A0", "A1", A0, A1)
        bcast_row(row(64, 0), B1f[:, 6144:7168], "A0", "A1", A0, A1)
        rstd1bc = B1f[:, 5120:6144]
        nmr1bc = B1f[:, 6144:7168]
        ln1_bc = cnt["V"]

        xn_done = []
        xld_done = []
        for dt in range(NDT):
            slot = dt % 2
            if dt >= 2:
                nc.scalar.wait_ge(semV, ln1_bc + 2 * (dt - 1))
            else:
                nc.scalar.wait_ge(semT, stats_mm_done[NDT - 1])
                nc.scalar.wait_ge(semS, sq_done[NDT - 1])
            bkey = "Pe" if slot == 0 else "Po"
            xld_done.append((bkey, dinc(bkey, nc.scalar.dma_start(
                out=XAa[:, XA_E + slot * 1024:XA_E + (slot + 1) * 1024],
                in_=xsT[dt * 128:(dt + 1) * 128, :]))))
            nc.vector.wait_ge(sems[xld_done[dt][0]], xld_done[dt][1])
            xt = XAa[:, XA_E + slot * 1024:XA_E + (slot + 1) * 1024]
            inc("V", nc.vector.tensor_tensor(xt, xt, rstd1bc, ALU.mult))
            wt(nc.vector, "V")
            xn_done.append(inc("V", nc.vector.tensor_tensor(
                XNa[:, dt * 1024:(dt + 1) * 1024], xt, nmr1bc, ALU.add)))
            pump(1)
        xn_all = cnt["V"]


        wk_done = dinc("G4", nc.gpsimd.dma_start(
            out=SOa[:, 0:8192].rearrange("p (k n) -> p k n", k=16),
            in_=wkT.rearrange("(k p) n -> p k n", p=128)))
        wv_done = dinc("G5", nc.gpsimd.dma_start(
            out=SOa[:, 8192:16384].rearrange("p (k n) -> p k n", k=16),
            in_=wvT.rearrange("(k p) n -> p k n", p=128)))

        def proj_tok(wbase_b, dst_c0f32, wsem, wmark):
            evs = []
            for tt in range(NCH):
                pp = tt % 2
                psl, kname = (A0, "A0") if pp == 0 else (A1, "A1")
                nc.tensor.wait_ge(wsem, wmark)
                nc.tensor.wait_ge(semV, xn_all)
                psum_wait(nc.tensor, kname)
                for k in range(16):
                    st, sp = (k == 0), (k == 15)
                    mmi = nc.tensor.matmul(
                        psl, XNa[:, k * 1024 + tt * 128:k * 1024 + (tt + 1) * 128],
                        SOa[:, wbase_b + k * 512:wbase_b + (k + 1) * 512],
                        start=st, stop=sp)
                    if sp:
                        inc("T", mmi)
                wt(nc.scalar, "T")
                evs.append(inc("S", nc.scalar.copy(
                    xab(dst_c0f32 + tt * 256, 512), psl)))
                psum_used(kname, "S")
                pump(2)
            return evs

        k_evac = proj_tok(0, XA_K, semG4, wk_done)
        v_evac = proj_tok(8192, XA_V, semG5, wv_done)
        kv_T_end = cnt["T"]
        v_S_end = cnt["S"]

        def transpose_128(src_ap, dst_ap, pairs, wait_keys):
            for i0 in range(0, len(pairs), 8):
                batch = pairs[i0:i0 + 8]
                for eng_key, val in wait_keys:
                    nc.tensor.wait_ge(sems[eng_key], val)
                psum_wait(nc.tensor, "PT")
                for j, (so, do) in enumerate(batch):
                    mmi = nc.tensor.transpose(PTa[:, j * 128:(j + 1) * 128],
                                              src_ap[:, so:so + 128], identb_s[:])
                    if j == len(batch) - 1:
                        inc("T", mmi)
                wt(nc.scalar, "T")
                for j, (so, do) in enumerate(batch):
                    mmi = nc.scalar.copy(dst_ap[:, do:do + 128], PTa[:, j * 128:(j + 1) * 128])
                    if j == len(batch) - 1:
                        inc("S", mmi)
                psum_used("PT", "S")
                pump(1)

        kfwd = [(tt * 512 + kv * 128, kv * 1024 + tt * 128)
                for kv in range(HKV) for tt in range(NCH)]
        transpose_128(xab(XA_K, 4096), xab(XA_KT, 4096), kfwd,
                      [("S", v_S_end)])
        kT_raw_S = cnt["S"]

        def rope_featmaj(base_ap, nheads, swap_c0, a_c0):
            for h in range(nheads):
                tile = base_ap[:, h * 1024:(h + 1) * 1024]
                swp = xab(swap_c0, 1024)
                atmp = xab(a_c0, 1024)
                wt(nc.scalar, "S")
                wt(nc.scalar, "V")
                nc.scalar.copy(swp[0:64, :], tile[64:128, :])
                inc("S", nc.scalar.copy(swp[64:128, :], tile[0:64, :]))
                wt(nc.vector, "S")
                inc("V", nc.vector.tensor_tensor(atmp, tile, cosF_s, ALU.mult))
                wt(nc.vector, "V")
                inc("V", nc.vector.tensor_tensor(swp, swp, sinS_s, ALU.mult))
                wt(nc.vector, "V")
                inc("V", nc.vector.tensor_tensor(tile, atmp, swp, ALU.add))
                pump(1)

        rope_featmaj(xab(XA_KT, 4096), HKV, XA_E, XA_E + 512)
        kT_rope_V = cnt["V"]

        kbwd = [(kv * 1024 + tt * 128, tt * 512 + kv * 128)
                for kv in range(HKV) for tt in range(NCH)]
        transpose_128(xab(XA_KT, 4096), xab(XA_K, 4096), kbwd,
                      [("V", kT_rope_V)])
        k_rope_S = cnt["S"]

        state = XAa[:, XA_ST:XA_ST + 2048]
        nc.vector.wait_ge(semT, kv_T_end)
        so_z = inc("V", nc.vector.memset(SOa[:, 0:2048], 0.0))
        inc("V", nc.vector.memset(state, 0.0))
        state_V = cnt["V"]
        ktil = xab(XA_G0, 2048)
        for c in range(NCH):
            nc.vector.wait_ge(semS, k_rope_S)
            wt(nc.vector, "T")
            for h in range(H):
                kv = h // (H // HKV)
                mmi = nc.vector.tensor_scalar_mul(
                    ktil[:, h * 128:(h + 1) * 128],
                    XAb[:, 2 * XA_K + c * 512 + kv * 128: 2 * XA_K + c * 512 + (kv + 1) * 128],
                    kdecC_s[:, h:h + 1])
                if h == H - 1:
                    inc("V", mmi)
            ktil_V = cnt["V"]
            nc.tensor.wait_ge(semV, ktil_V)
            if c == 0:
                nc.tensor.wait_ge(semS, v_S_end)
            for half in range(2):
                psum_wait(nc.tensor, "B0", "B1")
                for j in range(8):
                    h = half * 8 + j
                    kv = h // (H // HKV)
                    mmi = nc.tensor.matmul(
                        PBa[:, j * 128:(j + 1) * 128],
                        ktil[:, h * 128:(h + 1) * 128],
                        XAb[:, 2 * XA_V + c * 512 + kv * 128: 2 * XA_V + c * 512 + (kv + 1) * 128],
                        start=True, stop=True)
                    if j == 7:
                        inc("T", mmi)
                U_T = cnt["T"]
                nc.vector.wait_ge(semT, U_T)
                if half == 0:
                    nc.vector.wait_ge(semV, state_V)
                    nc.vector.wait_ge(semS, cnt["S"])
                shalf = state[:, half * 1024:(half + 1) * 1024]
                for j in range(8):
                    h = half * 8 + j
                    mmi = nc.vector.tensor_scalar_mul(shalf[:, j * 128:(j + 1) * 128],
                                                      shalf[:, j * 128:(j + 1) * 128], blk_f[h])
                    if j == 7:
                        inc("V", mmi)
                wt(nc.vector, "V")
                inc("V", nc.vector.tensor_tensor(shalf, shalf, PBa[:, 0:1024], ALU.add))
                psum_used("B0", "V")
                psum_used("B1", "V")
            state_V = cnt["V"]
            if c < NCH - 1:
                wt(nc.scalar, "V")
                inc("S", nc.scalar.copy(SOa[:, (c + 1) * 2048:(c + 2) * 2048], state))
            pump(6)
        scan_S = cnt["S"]
        scan_V = cnt["V"]

        exi_st = ex_next()
        wt(nc.sync, "V")
        sh_write(state, "st", exi_st)
        sh_poll(exi_st)
        Cadj = XAa[:, XA_E:XA_E + 2048]
        cd_ld = sh_read(Cadj, 0, 8, 128, 2048)
        nc.vector.wait_ge(semSH, cd_ld)
        inc("V", nc.vector.tensor_scalar_mul(Cadj, Cadj, selS_s[:]))
        for c in range(NCH):
            wt(nc.vector, "V")
            if c == 0:
                nc.vector.wait_ge(semS, scan_S)
                nc.vector.wait_ge(semV, so_z)
            inc("V", nc.vector.tensor_tensor(SOa[:, c * 2048:(c + 1) * 2048],
                                             SOa[:, c * 2048:(c + 1) * 2048], Cadj, ALU.add))
            if c < NCH - 1:
                wt(nc.vector, "V")
                for h in range(H):
                    mmi = nc.vector.tensor_scalar_mul(Cadj[:, h * 128:(h + 1) * 128],
                                                      Cadj[:, h * 128:(h + 1) * 128], blk_f[h])
                    if h == H - 1:
                        inc("V", mmi)
            pump(2)
        adj_V = cnt["V"]
        MAIN_MARKS["adj_V"] = adj_V

        wq_marks = []
        for i in range(NDT):
            slot = i % 2
            if i >= 2:
                nc.gpsimd.wait_ge(semT, wq_marks[i - 2][1])
            wkey = "We" if slot == 0 else "Wo"
            ld = dinc(wkey, nc.gpsimd.dma_start(
                out=WSa[:, slot * 2048:(slot + 1) * 2048].rearrange("p (k n) -> p k n", k=16),
                in_=wqT[:, i * 128:(i + 1) * 128].rearrange("(k p) n -> p k n", p=128)))
            wq_marks.append((ld, None))
            for th2 in range(2):
                pp = th2
                psl, kname = (A0, "A0") if pp == 0 else (A1, "A1")
                nc.tensor.wait_ge(sems[wkey], ld)
                if i == 0 and th2 == 0:
                    nc.tensor.wait_ge(semV, adj_V)
                psum_wait(nc.tensor, kname)
                for k in range(16):
                    st, sp = (k == 0), (k == 15)
                    mmi = nc.tensor.matmul(
                        psl, WSa[:, slot * 2048 + k * 128:slot * 2048 + (k + 1) * 128],
                        XNa[:, k * 1024 + th2 * 512:k * 1024 + (th2 + 1) * 512],
                        start=st, stop=sp)
                    if sp:
                        inc("T", mmi)
                wq_marks[i] = (ld, cnt["T"])
                wt(nc.scalar, "T")
                if i == 0 and th2 == 0:
                    nc.scalar.wait_ge(semV, xn_all)
                inc("S", nc.scalar.copy(B1a[:, i * 1024 + th2 * 512:i * 1024 + (th2 + 1) * 512], psl))
                psum_used(kname, "S")
                pump(2)
        q_proj_S = cnt["S"]
        nc.vector.wait_ge(semS, q_proj_S)
        rope_featmaj(B1a, H, XA_G0, XA_G0 + 512)
        q_rope_V = cnt["V"]
        MAIN_MARKS["q_rope_V"] = q_rope_V

        nc.scalar.wait_ge(semV, cnt["V"])
        dg_ld = dinc("X", nc.scalar.dma_start(out=XAa[:, XA_K:XA_K + 2048], in_=diagT))
        qd_ld = dinc("X", nc.scalar.dma_start(out=XAa[:, XA_ST:XA_ST + 2048], in_=qdecT))
        diagT_s = XAa[:, XA_K:XA_K + 2048]
        qdecT_s = XAa[:, XA_ST:XA_ST + 2048]
        sc_tmp = xab(XA_G1, 1024)
        qt_tmp = xab(XA_G1 + 512, 1024)
        while TH_ST["g1_mm"] == 0 and not TH_ST["done"]:
            pump()
        nc.vector.wait_ge(semtT, TH_ST["g1_mm"])
        o_evac_S = None
        for c in range(NCH):
            for hg in range(2):
                nc.tensor.wait_ge(semS, k_rope_S)
                nc.tensor.wait_ge(semV, q_rope_V)
                psum_wait(nc.tensor, "A0", "A1")
                for j in range(8):
                    h = hg * 8 + j
                    kv = h // (H // HKV)
                    mmi = nc.tensor.matmul(
                        PAa[:, j * 128:(j + 1) * 128],
                        XAb[:, 2 * XA_KT + kv * 1024 + c * 128: 2 * XA_KT + kv * 1024 + (c + 1) * 128],
                        B1a[:, h * 1024 + c * 128:h * 1024 + (c + 1) * 128],
                        start=True, stop=True)
                    if j == 7:
                        inc("T", mmi)
                sc_T = cnt["T"]
                nc.vector.wait_ge(semT, sc_T)
                nc.vector.wait_ge(semX, qd_ld)
                inc("V", nc.vector.tensor_tensor(sc_tmp, PAa[:, 0:1024],
                                                 diagT_s[:, hg * 1024:(hg + 1) * 1024], ALU.mult))
                psum_used("A0", "V")
                psum_used("A1", "V")
                nc.vector.wait_ge(semX, qd_ld)
                qsl = B1a[:, hg * 8192 + c * 128:hg * 8192 + c * 128 + 128]
                qsl = dataclasses.replace(qsl, ap=[qsl.ap[0], [1024, 8], [1, 128]])
                dsl = qdecT_s[:, hg * 1024:hg * 1024 + 128]
                dsl = dataclasses.replace(dsl, ap=[dsl.ap[0], [128, 8], [1, 128]])
                inc("V", nc.vector.tensor_tensor(qt_tmp, qsl, dsl, ALU.mult))
                ev_V = cnt["V"]
                nc.tensor.wait_ge(semV, ev_V)
                psum_wait(nc.tensor, "B0", "B1")
                for j in range(8):
                    h = hg * 8 + j
                    kv = h // (H // HKV)
                    po = PBa[:, j * 128:(j + 1) * 128]
                    nc.tensor.matmul(
                        po,
                        XAb[:, 2 * XA_V + c * 512 + kv * 128: 2 * XA_V + c * 512 + (kv + 1) * 128],
                        sc_tmp[:, j * 128:(j + 1) * 128], start=True, stop=False)
                    mmi = nc.tensor.matmul(
                        po,
                        SOa[:, c * 2048 + h * 128:c * 2048 + (h + 1) * 128],
                        qt_tmp[:, j * 128:(j + 1) * 128], start=False, stop=True)
                    if j == 7:
                        inc("T", mmi)
                o_T = cnt["T"]
                nc.scalar.wait_ge(semT, o_T)
                if o_evac_S is None:
                    nc.scalar.wait_ge(semV, adj_V)
                o_evac_S = inc("S", nc.scalar.copy(
                    SOa[:, c * 2048 + hg * 1024:c * 2048 + (hg + 1) * 1024],
                    PBa[:, 0:1024]))
                psum_used("B0", "S")
                psum_used("B1", "S")
                pump(3)
        o_done_S = cnt["S"]
        pump_all()

        B_XRE = 0
        B_G = 1024
        B_XNT = 2048
        B_T1 = 3072
        B_OUT = 4096
        B_BC1 = 5120
        nc.vector.wait_ge(semS, o_done_S)
        nc.vector.wait_ge(semT, cnt["T"])
        bcast_row(row(0, 1), B1f[:, B_BC1:B_BC1 + 1024], "A0", "A1", A0, A1)
        bcast_row(row(64, 0), B1f[:, B_BC1 + 1024:B_BC1 + 2048], "A0", "A1", A0, A1)
        rstd1t = B1f[:, B_BC1:B_BC1 + 1024]
        nmr1t = B1f[:, B_BC1 + 1024:B_BC1 + 2048]
        tail_bc_V = cnt["V"]

        gate_marks = []
        pre_V = []
        ln2_mm = 0
        for i in range(NDT):
            slot = i % 2
            if i >= 2:
                nc.gpsimd.wait_ge(semT, gate_marks[i - 2])
            else:
                nc.gpsimd.wait_ge(semT, cnt["T"])
            wkey = "We" if slot == 0 else "Wo"
            gld = dinc(wkey, nc.gpsimd.dma_start(
                out=WSa[:, slot * 2048:(slot + 1) * 2048].rearrange("p (k n) -> p k n", k=16),
                in_=gw[:, i * 128:(i + 1) * 128].rearrange("(k p) n -> p k n", p=128)))
            for th2 in range(2):
                sslot = (2 * i + th2) % 2
                psl, kname = (A0, "A0") if sslot == 0 else (A1, "A1")
                if len(pre_V) >= 2:
                    nc.scalar.wait_ge(semV, pre_V[-2])
                bkey = "Re" if sslot == 0 else "Ro"
                xre = B1f[:, B_XRE + sslot * 512:B_XRE + (sslot + 1) * 512]
                xrm = dinc(bkey, nc.scalar.dma_start(
                    out=xre, in_=xsT[i * 128:(i + 1) * 128, th2 * 512:(th2 + 1) * 512]))
                nc.tensor.wait_ge(sems[wkey], gld)
                psum_wait(nc.tensor, kname)
                for k in range(16):
                    st, sp = (k == 0), (k == 15)
                    mmi = nc.tensor.matmul(
                        psl, WSa[:, slot * 2048 + k * 128:slot * 2048 + (k + 1) * 128],
                        XNa[:, k * 1024 + th2 * 512:k * 1024 + (th2 + 1) * 512],
                        start=st, stop=sp)
                    if sp:
                        inc("T", mmi)
                gate_marks_val = cnt["T"]
                wt(nc.scalar, "T")
                gt = B1f[:, B_G + sslot * 512:B_G + (sslot + 1) * 512]
                inc("S", nc.scalar.activation(gt, psl, AF.Sigmoid, bias=pvecs_s[:, i:i + 1]))
                psum_used(kname, "S")
                nc.vector.wait_ge(sems[bkey], xrm)
                nc.vector.wait_ge(semV, tail_bc_V)
                xnt = B1f[:, B_XNT + sslot * 512:B_XNT + (sslot + 1) * 512]
                inc("V", nc.vector.tensor_tensor(xnt, xre,
                                                 rstd1t[:, th2 * 512:(th2 + 1) * 512], ALU.mult))
                wt(nc.vector, "V")
                inc("V", nc.vector.tensor_tensor(xnt, xnt,
                                                 nmr1t[:, th2 * 512:(th2 + 1) * 512], ALU.add))
                wt(nc.vector, "S")
                wt(nc.vector, "V")
                if i == 0 and th2 == 0:
                    nc.vector.wait_ge(semtV, TH_ST["bc_tV"])
                if i == 10 and th2 == 0:
                    nc.vector.wait_ge(semtT, TH_ST["end_tT"])
                pre_sl = XAa[:, i * 1024 + th2 * 512:i * 1024 + (th2 + 1) * 512]
                pre_V.append(inc("V", nc.vector.scalar_tensor_tensor(
                    pre_sl, gt, bcP_s[:, i:i + 1], xnt, ALU.mult, ALU.add)))
                wt(nc.scalar, "V")
                pb16 = B1a[:, 14336 + sslot * 512:14336 + (sslot + 1) * 512]
                inc("S", nc.scalar.copy(pb16, pre_sl))
                wt(nc.scalar, "S")
                sq16 = B1a[:, 15360 + sslot * 512:15360 + (sslot + 1) * 512]
                inc("S", nc.scalar.activation(sq16, pb16, AF.Square))
                wt(nc.tensor, "S")
                st16 = (i == 0)
                sp16 = (i == NDT - 1)
                if st16 and th2 == 0:
                    psum_wait(nc.tensor, "B0", "B1", "B2", "B3")
                nc.tensor.matmul(PBa[0:1, th2 * 512:(th2 + 1) * 512], onesb_s[:], pb16,
                                 start=st16, stop=sp16)
                inc("T", nc.tensor.matmul(PBa[0:1, 1024 + th2 * 512:1024 + (th2 + 1) * 512],
                                          onesb_s[:], sq16, start=st16, stop=sp16))
            gate_marks.append(cnt["T"])
        ln2_T = cnt["T"]

        wt(nc.vector, "T")
        nc.vector.tensor_copy(row(0, 0)[:, 0:512], PBa[0:1, 0:512])
        nc.vector.tensor_copy(row(0, 0)[:, 512:1024], PBa[0:1, 512:1024])
        nc.vector.tensor_copy(row(32, 0)[:, 0:512], PBa[0:1, 1024:1536])
        inc("V", nc.vector.tensor_copy(row(32, 0)[:, 512:1024], PBa[0:1, 1536:2048]))
        for kk in ("B0", "B1", "B2", "B3"):
            psum_used(kk, "V")
        ln_rows(row(0, 0), row(32, 0), row(0, 1), row(64, 0), row(32, 0), EPS2)
        bcast_row(row(0, 1), B1f[:, B_BC1:B_BC1 + 1024], "A0", "A1", A0, A1)
        bcast_row(row(64, 0), B1f[:, B_BC1 + 1024:B_BC1 + 2048], "A0", "A1", A0, A1)
        rstd2t = B1f[:, B_BC1:B_BC1 + 1024]
        nmr2t = B1f[:, B_BC1 + 1024:B_BC1 + 2048]
        tail2_bc_V = cnt["V"]

        wo_marks = []
        out_V = []
        out_marks = [0, 0]
        for i in range(NDT):
            slot = i % 2
            if i >= 2:
                nc.gpsimd.wait_ge(semT, wo_marks[i - 2])
            else:
                nc.gpsimd.wait_ge(semT, ln2_T)
            wkey = "We" if slot == 0 else "Wo"
            wld = dinc(wkey, nc.gpsimd.dma_start(
                out=WSa[:, slot * 2048:(slot + 1) * 2048].rearrange("p (k n) -> p k n", k=16),
                in_=woT[:, i * 128:(i + 1) * 128].rearrange("(k p) n -> p k n", p=128)))
            for th2 in range(2):
                sslot = (2 * i + th2) % 2
                psl, kname = (A0, "A0") if sslot == 0 else (A1, "A1")
                if len(out_V) >= 2:
                    nc.scalar.wait_ge(semV, out_V[-2])
                else:
                    nc.scalar.wait_ge(semV, tail2_bc_V)
                bkey = "Re" if sslot == 0 else "Ro"
                xre = B1f[:, B_XRE + sslot * 512:B_XRE + (sslot + 1) * 512]
                xrm = dinc(bkey, nc.scalar.dma_start(
                    out=xre, in_=xsT[i * 128:(i + 1) * 128, th2 * 512:(th2 + 1) * 512]))
                nc.tensor.wait_ge(sems[wkey], wld)
                nc.tensor.wait_ge(semS, o_done_S)
                psum_wait(nc.tensor, kname)
                for k in range(16):
                    st, sp = (k == 0), (k == 15)
                    osl = SOa[:, th2 * 8192 + k * 128:th2 * 8192 + k * 128 + 128]
                    osl = dataclasses.replace(osl, ap=[osl.ap[0], [2048, 4], [1, 128]])
                    mmi = nc.tensor.matmul(
                        psl, WSa[:, slot * 2048 + k * 128:slot * 2048 + (k + 1) * 128],
                        osl, start=st, stop=sp)
                    if sp:
                        inc("T", mmi)
                wo_mark_val = cnt["T"]
                wt(nc.scalar, "T")
                att = B1f[:, 7168 + sslot * 512:7168 + (sslot + 1) * 512]
                att_S = inc("S", nc.scalar.copy(att, psl))
                psum_used(kname, "S")
                wt(nc.vector, "T")
                nc.vector.wait_ge(semV, tail2_bc_V)
                nc.vector.wait_ge(sems[bkey], xrm)
                t1 = B1f[:, B_T1 + sslot * 512:B_T1 + (sslot + 1) * 512]
                pre_sl = XAa[:, i * 1024 + th2 * 512:i * 1024 + (th2 + 1) * 512]
                inc("V", nc.vector.tensor_tensor(t1, pre_sl,
                                                 rstd2t[:, th2 * 512:(th2 + 1) * 512], ALU.mult))
                wt(nc.vector, "V")
                inc("V", nc.vector.tensor_tensor(t1, t1, nmr2t[:, th2 * 512:(th2 + 1) * 512], ALU.add))
                wt(nc.vector, "V")
                inc("V", nc.vector.tensor_scalar(t1, t1, pvecs_s[:, 16 + i:17 + i],
                                                 pvecs_s[:, 32 + i:33 + i], ALU.mult, ALU.add))
                wt(nc.vector, "V")
                inc("V", nc.vector.tensor_tensor(t1, t1, xre, ALU.add))
                wt(nc.vector, "V")
                okey = "Oe" if sslot == 0 else "Oo"
                if out_marks[sslot]:
                    nc.vector.wait_ge(sems[okey], out_marks[sslot])
                nc.vector.wait_ge(semS, att_S)
                osl2 = B1f[:, B_OUT + sslot * 512:B_OUT + (sslot + 1) * 512]
                out_V.append(inc("V", nc.vector.tensor_tensor(osl2, t1, att, ALU.add)))
                okey = "Oe" if sslot == 0 else "Oo"
                wt(nc.scalar, "V")
                out_marks[sslot] = dinc(okey, nc.scalar.dma_start(
                    out=outT[i * 128:(i + 1) * 128, th2 * 512:(th2 + 1) * 512], in_=osl2))
            wo_marks.append(cnt["T"])

        nc.scalar.wait_ge(semOe, cnt["Oe"])
        nc.scalar.wait_ge(semOo, cnt["Oo"])
        nc.sync.wait_ge(semOe, cnt["Oe"])
        nc.sync.wait_ge(semOo, cnt["Oo"])

    return nc


def _prep_inmaps(inputs):
    x = np.asarray(inputs["L0_attn_x"], dtype=np.float32)
    wq = np.asarray(inputs["L0_attn_wq"], dtype=np.float32)
    wk = np.asarray(inputs["L0_attn_wk"], dtype=np.float32)
    wv = np.asarray(inputs["L0_attn_wv"], dtype=np.float32)
    wo = np.asarray(inputs["L0_attn_wo"], dtype=np.float32)
    wqT = np.ascontiguousarray(wq.T)
    wkT = np.ascontiguousarray(wk.T)
    wvT = np.ascontiguousarray(wv.T)
    woT = np.ascontiguousarray(wo.T)
    gwv = np.ascontiguousarray(np.asarray(inputs["L0_ffn_gate_w"], dtype=np.float32))
    aggw_f = np.asarray(inputs["L0_ffn_agg_w"], dtype=np.float32)
    bcw_f = np.asarray(inputs["L0_ffn_bc_w"], dtype=np.float32)
    d1w = np.asarray(inputs["L0_ffn_d1_w"], dtype=np.float32)
    d2w = np.asarray(inputs["L0_ffn_d2_w"], dtype=np.float32)
    d1b = np.asarray(inputs["L0_ffn_d1_b"], dtype=np.float32)

    ident_b = np.eye(128, dtype=bfl)
    ident_f = np.eye(128, dtype=np.float32)[:, 0:4].copy()
    ones_b = np.ones((128, 1), dtype=bfl)
    ones_f = np.ones((128, 128), dtype=np.float32)

    def pcol(v):
        return np.ascontiguousarray(v.reshape(16, 128).T.astype(np.float32))

    pv = np.concatenate([
        pcol(np.asarray(inputs["L0_ffn_gate_b"], dtype=np.float32)),
        pcol(np.asarray(inputs["L0_ffn_out_gamma"], dtype=np.float32)),
        pcol(np.asarray(inputs["L0_ffn_out_beta"], dtype=np.float32)),
    ], axis=1)

    lam = np.exp2(-8.0 * np.arange(1, H + 1, dtype=np.float64) / H)
    r = np.arange(CH, dtype=np.float64)
    dm = r[None, None, :] - r[None, :, None]
    dmat = np.where(dm >= 0, np.exp(-lam[:, None, None] * np.maximum(dm, 0)), 0.0)
    diag_t = np.ascontiguousarray(
        dmat.transpose(1, 0, 2).reshape(CH, H * CH).astype(np.float32))
    qdec = np.exp(-lam[:, None] * (r[None, :] + 1.0))
    qdec_t = np.ascontiguousarray(
        np.broadcast_to(qdec.reshape(1, H * CH), (128, H * CH)).astype(np.float32))
    kdec = np.exp(-lam[None, :] * (CH - 1.0 - r[:, None]))
    kdec_c = np.ascontiguousarray(kdec.astype(np.float32))

    inv_freq = 1.0 / 10000.0 ** (np.arange(0, DH, 2, dtype=np.float64) / DH)

    rv = np.concatenate([
        np.asarray(inputs["L0_ffn_in_gamma"], dtype=np.float32),
        np.asarray(inputs["L0_ffn_in_beta"], dtype=np.float32),
        np.asarray(inputs["L0_ffn_pn_gamma"], dtype=np.float32),
        np.asarray(inputs["L0_ffn_pn_beta"], dtype=np.float32),
        np.asarray(inputs["L0_ffn_d2_b"], dtype=np.float32),
        np.asarray(inputs["L0_ffn_bc_b"], dtype=np.float32),
        np.asarray(inputs["L0_ffn_agg_b"], dtype=np.float32),
    ]).reshape(1, -1)

    xlast_np = np.ascontiguousarray(x[:, S - 1, :])
    nonce_np = (np.random.randint(1, 2 ** 27, size=(1, 8)) * 8 +
                np.arange(8).reshape(1, 8)).astype(np.int32)

    in_maps = []
    for c in range(8):
        b, half = c // 2, c % 2
        xsT_np = np.ascontiguousarray(x[b, half * TLOC:(half + 1) * TLOC, :].T)
        pos = np.arange(half * TLOC, (half + 1) * TLOC, dtype=np.float64)
        fr = pos[None, :] * inv_freq[:, None]
        cosf = np.cos(fr)
        sinf = np.sin(fr)
        cosF_np = np.ascontiguousarray(
            np.concatenate([cosf, cosf], axis=0).astype(np.float32))
        sinS_np = np.ascontiguousarray(
            np.concatenate([-sinf, sinf], axis=0).astype(np.float32))
        bsel = np.zeros((128, 4), dtype=np.float32)
        bsel[:, b] = 1.0
        sel = np.full((128, 1), 1.0 if half == 1 else 0.0, dtype=np.float32)
        hh = c % 2
        rvc = np.concatenate(
            [rv, d1b[hh * 4096:(hh + 1) * 4096].reshape(1, -1)], axis=1)
        xinfo_np = np.array([[hh * 136, hh, 1 - hh] + [0] * 21],
                            dtype=np.int32)
        in_maps.append({
            "xsT": xsT_np,
            "xlast": xlast_np,
            "wqT": wqT, "wkT": wkT, "wvT": wvT, "woT": woT, "gw": gwv,
            "aggw": np.ascontiguousarray(aggw_f[:, hh * 1024:(hh + 1) * 1024]),
            "bcw": np.ascontiguousarray(bcw_f[:, hh * 1024:(hh + 1) * 1024]),
            "d1h": np.ascontiguousarray(d1w[:, hh * 4096:(hh + 1) * 4096]),
            "d2h": np.ascontiguousarray(d2w[hh * 4096:(hh + 1) * 4096, :]),
            "xinfo": xinfo_np,
            "nonces": nonce_np,
            "rvec": np.ascontiguousarray(np.tile(rvc, (4, 1))),
            "pvecs": pv,
            "cosF": cosF_np, "sinS": sinS_np,
            "diagT": diag_t, "qdecT": qdec_t, "kdecC": kdec_c,
            "identb": ident_b, "identf": ident_f,
            "onesb": ones_b, "onesf": ones_f,
            "bselP": bsel, "selS": sel,
            "consts": np.tile(np.array([[EPS1, EPS2]], dtype=np.float32), (128, 1)),
        })
    return in_maps


def kernel(**inputs):
    in_maps = _prep_inmaps(inputs)
    nc = build_nc()
    res = run_bass_kernel_spmd(nc, in_maps, core_ids=list(range(8)))
    out = np.empty((B, S, D), dtype=np.float32)
    for c in range(8):
        b, half = c // 2, c % 2
        out[b, half * TLOC:(half + 1) * TLOC, :] = res.results[c]["outT"].T
    return out
